# version 03
# baseline (speedup 1.0000x reference)
import contextlib
import numpy as np
import ml_dtypes
import concourse.bass as bass
import concourse.mybir as mybir
from concourse.bass_utils import run_bass_kernel_spmd

F32 = mybir.dt.float32
BF16 = mybir.dt.bfloat16
I32 = mybir.dt.int32
U32 = mybir.dt.uint32
AF = mybir.ActivationFunctionType
ALU = mybir.AluOpType
AX = mybir.AxisListType
BF16NP = ml_dtypes.bfloat16

ENGINES = ("sync", "tensor", "vector", "scalar", "gpsimd")


def mk_cfg(full=True):
    if full:
        return dict(T=16384, D=1024, EH=1024, N=16, R=64, F=4096, NE=9, TC=512,
                    ST=1024, EPS=1e-5, POLL_OUT=4000, POLL_IN=64)
    return dict(T=512, D=256, EH=128, N=4, R=32, F=256, NE=3, TC=128,
                ST=128, EPS=1e-5, POLL_OUT=1000, POLL_IN=8)


def mk_cfg_mid():
    return dict(T=2048, D=256, EH=128, N=4, R=32, F=512, NE=3, TC=128,
                ST=256, EPS=1e-5, POLL_OUT=1000, POLL_IN=8)


class Builder:
    def __init__(self, cfg):
        self.cfg = cfg
        self.nc = bass.Bass("TRN2", target_bir_lowering=False)
        self.ops = {e: [] for e in ENGINES}
        self.cnt = {}
        self.sems = {}
        self.stack = contextlib.ExitStack()
        self._mgrs = {}
        self._regs = {}

    def sb(self, nm, shp, dt=F32):
        mgr = self.nc.sbuf_tensor(nm, shp, dt)
        t = mgr.__enter__()
        self._mgrs[nm] = mgr
        return t

    def psum(self, nm, shp, dt=F32):
        mgr = self.nc.psum_tensor(nm, shp, dt)
        t = mgr.__enter__()
        self._mgrs[nm] = mgr
        return t

    def free(self, t):
        self._mgrs.pop(t.name).__exit__(None, None, None)

    def sem(self, name):
        h = self.stack.enter_context(self.nc.semaphore(name))
        self.sems[name] = h
        self.cnt[name] = 0
        return h

    def emit(self, eng, fn):
        self.ops[eng].append(fn)

    def reg(self, E, name):
        rr = self._regs.setdefault(name, None)
        if rr is None:
            rr = E.alloc_register(name=name)
            self._regs[name] = rr
        return rr

    def inc(self, name, k=1):
        self.cnt[name] += k
        return self.cnt[name]

    def tok(self, name):
        return (name, self.cnt[name])

    def waitv(self, eng, tokv):
        if tokv is None:
            return
        nm, v = tokv
        if nm in ("dO", "dG", "dS"):
            v = self.cnt[nm]
        self.emit(eng, lambda E, s=self.sems[nm], vv=v: E.wait_ge(s, vv))

    def dma(self, eng, out, in_, sem="dO"):
        v = self.inc(sem, 16)
        self.emit(eng, lambda E, o=out, i=in_, s=self.sems[sem]:
                  E.dma_start(out=o, in_=i).then_inc(s, 16))
        self.emit(eng, lambda E, s=self.sems[sem], vv=v: E.wait_ge(s, vv))
        return (sem, v)

    def idma(self, fn, sem="dG"):
        v = self.inc(sem, 16)
        self.emit("gpsimd", lambda E, f=fn, s=self.sems[sem]: f(E, s))
        self.emit("gpsimd", lambda E, s=self.sems[sem], vv=v: E.wait_ge(s, vv))
        return (sem, v)

    def _c(self, eng, semname, fn):
        prev = self.cnt[semname]
        self.emit(eng, lambda E, s=self.sems[semname], vv=prev: E.wait_ge(s, vv))
        v = self.inc(semname)
        self.emit(eng, lambda E, f=fn, s=self.sems[semname]: f(E).then_inc(s, 1))
        return (semname, v)

    def pe(self, fn):
        return self._c("tensor", "sPE", fn)

    def act(self, fn):
        return self._c("scalar", "sACT", fn)

    def dve(self, fn):
        return self._c("vector", "sDVE", fn)

    def pool(self, fn):
        return self._c("gpsimd", "sPOOL", fn)

    def join(self):
        v = self.inc("sJOIN", 5)
        for e in ENGINES:
            self.emit(e, lambda E, s=self.sems["sJOIN"]: E.nop().then_inc(s, 1))
        for e in ENGINES:
            self.emit(e, lambda E, s=self.sems["sJOIN"], vv=v: E.wait_ge(s, vv))

    def barrier(self, data_toks):
        slot = self._bar_slot
        self._bar_slot += 1
        assert slot < self._nflag
        for t in data_toks:
            self.waitv("gpsimd", t)
        flags_h, flagval_t, flagrd, flgidx_t, flgsrc_t = self._bar_tiles
        dS = self.sems["dS"]
        PO, PI = self.cfg["POLL_OUT"], self.cfg["POLL_IN"]

        def _run(E):
            if self._ds_reg is None:
                self._ds_reg = E.alloc_register(name="ds_count")
                E.reg_mov(self._ds_reg, 0)
                self._bar_regs = tuple(E.alloc_register(name=f"barreg{j}") for j in range(5))
            tgt, done, r0, r1, it = self._bar_regs
            nc_ = E.bass
            engs = [E.engine]
            E.indirect_dma_start(
                out=flags_h.ap()[:, :],
                out_offset=bass.IndirectOffsetOnAxis(flgidx_t[0:2, slot:slot + 1], 0),
                in_=flgsrc_t[0:2, slot * 16:(slot + 1) * 16], in_offset=None).then_inc(dS, 16)
            E.reg_alu(self._ds_reg, self._ds_reg, 16, ALU.add)
            E.reg_load(tgt, flagval_t[0:1, slot:slot + 1])
            E.reg_mov(done, 0)
            E.reg_mov(it, 0)
            loop_s = f"barloop{slot}"
            loop_e = f"barend{slot}"
            nc_.br(loop_s, engines=engs)
            with nc_.body(loop_s, valid_engines=engs):
                E.reg_alu(self._ds_reg, self._ds_reg, 16, ALU.add)
                E.dma_start(out=flagrd[0:1, 0:2],
                            in_=bass.AP(flags_h, slot * 2 * 16, [[16, 2], [1, 1]]),
                            allow_slow_non_contiguous=True).then_inc(dS, 16)
                E.wait_ge(dS, self._ds_reg)
                E.reg_load(r0, flagrd[0:1, 0:1])
                E.reg_load(r1, flagrd[0:1, 1:2])
                with E.If_cmp(r0, tgt, "IS_GE"):
                    with E.If_cmp(r1, tgt, "IS_GE"):
                        E.reg_mov(done, 1)
                E.reg_alu(it, it, 1, ALU.add)
                with E.If_ne(done, 0):
                    E.reg_mov(it, PO * PI)
                nc_.br_lt(it, PO * PI, on_true=loop_s, on_false=loop_e, engines=engs)
            nc_.switch_bb(loop_e)
            return E.nop()

        v = self.inc("sBAR")
        self.emit("gpsimd", lambda E, f=_run, s=self.sems["sBAR"]: f(E).then_inc(s, 1))
        return ("sBAR", v)


def build(cfg):
    K = Builder(cfg)
    nc = K.nc
    T, D, EH, N, R, F, NE, TC, ST = (cfg[k] for k in
                                     ("T", "D", "EH", "N", "R", "F", "NE", "TC", "ST"))
    EPS = cfg["EPS"]
    SL = EH // 128
    DC = D // 128
    H = T // 2
    Q = T // 4
    E = 2 * EH
    EC = E // 128
    FC = F // 128
    NGA = T // 128
    NGO = Q // 128
    NST = Q // ST
    STG = ST // 128
    GB = min(16, NGA)
    NCHA = H // TC
    NCHB = T // TC
    NCH = NCHA + NCHB
    NFLAG = NCH + 8
    MT = min(ST, 512)
    NMT = ST // MT
    assert TC <= 512

    def din(nm, shp, dt=F32):
        h = nc.dram_tensor(nm, shp, dt, kind="ExternalInput")
        return h, h.ap()

    x_b_h, x_b = din("x_b", [T, D])
    x_o_h, x_o = din("x_o", [Q, D])
    winT_h, winT = din("winT", [D, 4 * EH])
    convd_h, convd = din("convd", [128, 2 * 4 * SL * 128])
    bconv_h, bconv_ap = din("bconv", [128, 2 * SL])
    bdt_h, bdt_ap = din("bdt", [128, 2 * SL])
    alog_h, alog_ap = din("alog", [128, 2 * SL * N])
    dvec_h, dvec_ap = din("dvec", [128, 2 * SL])
    wxT_h, wxT_ap = din("wxT", [2 * EH, 96])
    wdtT_h, wdtT_ap = din("wdtT", [64, 2 * EH])
    lnw_h, lnw_ap = din("lnw_b", [128, D])
    lnb_h, lnb_ap = din("lnb_b", [128, D])
    woutT_h, woutT_ap = din("woutT", [2 * E, D])
    fup_h, fup_ap = din("fup", [NE * D, F])
    fdn_h, fdn_ap = din("fdn", [NE * F, D])
    rw_h, rw_ap = din("rw", [D, 8])
    selm_h, selm_ap = din("selm", [16, N * 128])
    ident_h, ident_ap = din("ident", [128, 128])
    meta_h, meta_ap = din("meta", [1, 16], I32)
    flagval_h, flagval_ap = din("flagval", [1, NFLAG], I32)

    out_o_h = nc.dram_tensor("out_o", [Q, D], F32, kind="ExternalOutput")
    out_o = out_o_h.ap()
    DBGF = cfg.get("DBGF", False)
    if DBGF:
        dbgff_h = nc.dram_tensor("dbgff", [Q, D], F32, kind="ExternalOutput")
        dbgg_h = nc.dram_tensor("dbgg", [Q, 8], F32, kind="ExternalOutput")
        dbgh_h = nc.dram_tensor("dbgh", [F, ST], BF16, kind="ExternalOutput")

    xn2fm_h = nc.dram_tensor("xn2fm", [D, T], BF16)
    ffnout_h = nc.dram_tensor("ffnout", [Q, D], F32)
    xdsl_h = nc.dram_tensor("xdsl", [2 * NCH * 96 + 128, TC], F32, addr_space="Shared")
    ysl_h = nc.dram_tensor("ysl", [4 * EH + 128, H], BF16, addr_space="Shared")
    flags_h = nc.dram_tensor("flags", [NFLAG * 2 + 128, 16], I32, addr_space="Shared")
    xdidx_h, xdidx_ap = din("xdidx", [128, NCH * 6], U32)
    yslidx_h, yslidx_ap = din("yslidx", [128, 2 * SL], U32)
    yrdidx_h, yrdidx_ap = din("yrdidx", [128, 4 * SL * NGO], U32)
    flgidx_h, flgidx_ap = din("flgidx", [128, NFLAG], U32)
    flgsrc_h, flgsrc_ap = din("flgsrc", [2, NFLAG * 16], I32)

    for s in ("dS", "dO", "dG", "sPE", "sACT", "sDVE", "sPOOL", "sBAR", "sJOIN"):
        K.sem(s)

    meta_t = K.sb("meta_t", [1, 16], I32)
    flagval_t = K.sb("flagval_t", [1, NFLAG], I32)
    flagrd = K.sb("flagrd", [1, 2], I32)
    xdidx_t = K.sb("xdidx_t", [128, NCH * 6], U32)
    yslidx_t = K.sb("yslidx_t", [128, 2 * SL], U32)
    yrdidx_t = K.sb("yrdidx_t", [128, 4 * SL * NGO], U32)
    flgidx_t = K.sb("flgidx_t", [128, NFLAG], U32)
    flgsrc_t = K.sb("flgsrc_t", [2, NFLAG * 16], I32)
    K._bar_tiles = (flags_h, flagval_t, flagrd, flgidx_t, flgsrc_t)
    K._bar_slot = 0
    K._nflag = NFLAG
    K._ds_reg = None

    identb_t = K.sb("identb_t", [128, 128], BF16)
    identf_t = K.sb("identf_t", [128, 128])
    lnw_t = K.sb("lnw_t", [128, D])
    lnb_t = K.sb("lnb_t", [128, D])
    selm_t = K.sb("selm_t", [16, N * 128], BF16)
    bconv_t = K.sb("bconv_t", [128, 2 * SL])
    bdt_t = K.sb("bdt_t", [128, 2 * SL])
    apos_t = K.sb("apos_t", [128, 2 * SL * N])
    dvec_t = K.sb("dvec_t", [128, 2 * SL])
    wx_st = K.sb("wx_st", [128, 2 * SL * 96], BF16)
    wdt_st = K.sb("wdt_st", [64, 2 * EH])
    stat_c = K.sb("stat_c", [128, NGA])
    stat_ob = K.sb("stat_ob", [128, NGA])
    stato_r = K.sb("stato_r", [128, NGO])
    stato_b = K.sb("stato_b", [128, NGO])
    ones_t = K.sb("ones_t", [128, 1])
    eps_t = K.sb("eps_t", [128, 1])
    carr = K.sb("carr", [128, 2 * SL * N])

    t_meta = K.dma("sync", meta_t[:], meta_ap)
    K.dma("sync", flagval_t[:], flagval_ap)
    t_id = K.dma("gpsimd", identb_t[:], ident_ap, "dG")
    K.dma("sync", identf_t[:], ident_ap)
    K.dma("sync", lnw_t[:], lnw_ap)
    t_lnb = K.dma("sync", lnb_t[:], lnb_ap)
    K.dma("gpsimd", selm_t[:], selm_ap, "dG")
    K.dma("sync", bconv_t[:], bconv_ap)
    K.dma("sync", bdt_t[:], bdt_ap)
    t_al = K.dma("sync", apos_t[:], alog_ap)
    K.dma("sync", dvec_t[:], dvec_ap)
    t_wx = K.dma("gpsimd", wx_st[:].rearrange("p (s c) -> p s c", s=2 * SL),
                 bass.AP(wxT_h, 0, [[96, 128], [128 * 96, 2 * SL], [1, 96]]), "dG")
    t_wdt = K.dma("sync", wdt_st[:], wdtT_ap)
    K.dma("sync", xdidx_t[:], xdidx_ap)
    K.dma("sync", yslidx_t[:], yslidx_ap)
    K.dma("sync", yrdidx_t[:], yrdidx_ap)
    tfi = K.dma("sync", flgidx_t[:], flgidx_ap)
    tfs = K.dma("sync", flgsrc_t[:], flgsrc_ap)
    K.waitv("gpsimd", tfs)
    K.waitv("gpsimd", tfi)

    zt = K.sb("zt", [128, 2048])
    tz = K.dve(lambda E: E.memset(zt[:], 0.0))
    K.waitv("gpsimd", tz)
    ZCH = 128 * 2048
    xd_elems = (2 * NCH * 96 + 128) * TC
    for o in range(0, xd_elems, ZCH):
        n = min(ZCH, xd_elems - o)
        nr, ncol = (128, n // 128) if n % 128 == 0 and n >= 128 else (1, n)
        K.dma("gpsimd", bass.AP(xdsl_h, o, [[ncol, nr], [1, ncol]]),
              zt[0:nr, 0:ncol], "dG")
    ys_elems = (4 * EH + 128) * H
    ztb = zt[:].bitcast(BF16)
    for o in range(0, ys_elems, 2 * ZCH):
        n = min(2 * ZCH, ys_elems - o)
        nr, ncol = (128, n // 128) if n % 128 == 0 and n >= 128 else (1, n)
        K.dma("gpsimd", bass.AP(ysl_h, o, [[ncol, nr], [1, ncol]]),
              ztb[0:nr, 0:ncol], "dG")
    K.barrier([K.tok("dG")])
    K.join()
    K.free(zt)
    K.waitv("scalar", t_al)
    K.act(lambda E: E.activation(apos_t[:], apos_t[:], AF.Exp))
    K.waitv("vector", t_meta)
    K.dve(lambda E: E.memset(carr[:], 0.0))
    K.dve(lambda E: E.memset(ones_t[:], 1.0))
    tmemc = K.dve(lambda E: E.memset(eps_t[:], float(EPS)))
    K.waitv("scalar", tmemc)

    g_x = K.sb("g_x", [128, D])
    g_t1 = K.sb("g_t1", [128, D])
    g_t2 = K.sb("g_t2", [128, D], BF16)
    g_a1 = K.sb("g_a1", [128, NGA])
    g_a2 = K.sb("g_a2", [128, NGA])
    trbuf = K.sb("trbuf", [128, DC * GB * 128], BF16)
    PT32 = K.psum("PT32", [128, 128])
    PTRB = K.psum("PTRB", [128, 128], BF16)

    def ln_stats(x_ap, ngroups, acc1, acc2, prev=None):
        last = prev
        for g in range(ngroups):
            K.waitv("sync", last)
            t_x = K.dma("sync", g_x[:], x_ap[g * 128:(g + 1) * 128, :])
            K.waitv("scalar", t_x)
            K.act(lambda E, g=g: E.activation(g_t1[:], g_x[:], AF.Identity,
                                              accum_out=acc1[:, g:g + 1]))
            last = K.act(lambda E, g=g: E.activation(g_t1[:], g_x[:], AF.Square,
                                                     accum_out=acc2[:, g:g + 1]))
        return last

    def finalize_stats(acc1, acc2, out_scale, out_bias, ngroups, combined):
        a1 = acc1[:, 0:ngroups]
        a2 = acc2[:, 0:ngroups]
        sc = out_scale[:, 0:ngroups]
        ob = out_bias[:, 0:ngroups]
        K.dve(lambda E: E.tensor_scalar_mul(a1, a1, 1.0 / D))
        K.dve(lambda E: E.tensor_scalar_mul(a2, a2, 1.0 / D))
        K.dve(lambda E: E.tensor_tensor(sc, a1, a1, ALU.mult))
        tv = K.dve(lambda E: E.tensor_tensor(a2, a2, sc, ALU.subtract))
        K.waitv("scalar", tv)
        K.act(lambda E: E.activation(sc, a2, AF.Ln, bias=eps_t[:]))
        tr = K.act(lambda E: E.activation(sc, sc, AF.Exp, scale=-0.5))
        if combined:
            K.waitv("vector", tr)
            K.dve(lambda E: E.tensor_tensor(ob, sc, sc, ALU.mult))
            tv2 = K.dve(lambda E: E.tensor_tensor(ob, a2, ob, ALU.mult))
            K.waitv("scalar", tv2)
            K.act(lambda E: E.activation(ob, ob, AF.Ln, bias=eps_t[:]))
            tr2 = K.act(lambda E: E.activation(ob, ob, AF.Exp, scale=-0.5))
            K.waitv("vector", tr2)
            K.dve(lambda E: E.tensor_tensor(sc, sc, ob, ALU.mult))
        else:
            K.waitv("vector", tr)
        K.dve(lambda E: E.tensor_tensor(ob, a1, sc, ALU.mult))
        return K.dve(lambda E: E.tensor_scalar_mul(ob, ob, -1.0))

    tq = ln_stats(x_b, NGA, g_a1, g_a2)
    K.waitv("vector", tq)
    tst = finalize_stats(g_a1, g_a2, stat_c, stat_ob, NGA, True)
    K.waitv("scalar", tst)
    tq2 = ln_stats(x_o, NGO, g_a1, g_a2, prev=tq)
    K.waitv("vector", tq2)
    tst2 = finalize_stats(g_a1, g_a2, stato_r, stato_b, NGO, False)

    K.waitv("scalar", tst2)
    K.waitv("tensor", t_id)
    nblk = (NGA + GB - 1) // GB
    tr_toks = [None, None]
    wr_tok = None
    prev_x2 = tst2
    prev_ttr = None
    for blk in range(nblk):
        gs = list(range(blk * GB, min((blk + 1) * GB, NGA)))
        K.waitv("scalar", wr_tok)
        for gi, g in enumerate(gs):
            K.waitv("sync", prev_x2)
            t_x = K.dma("sync", g_x[:], x_b[g * 128:(g + 1) * 128, :])
            K.waitv("scalar", t_x)
            tc_ = K.act(lambda E, g=g: E.activation(g_t1[:], g_x[:], AF.Identity,
                                                    scale=stat_c[:, g:g + 1],
                                                    bias=stat_ob[:, g:g + 1]))
            K.waitv("vector", tc_)
            K.waitv("vector", prev_ttr)
            K.dve(lambda E: E.tensor_tensor(g_t1[:], g_t1[:], lnw_t[:], ALU.mult))
            tx2 = K.dve(lambda E: E.tensor_tensor(g_t2[:], g_t1[:], lnb_t[:], ALU.add))
            prev_x2 = tx2
            K.waitv("tensor", tx2)
            for dc in range(DC):
                K.waitv("tensor", tr_toks[1])
                ttr = K.pe(lambda E, dc=dc: E.transpose(
                    PTRB[:], g_t2[:, dc * 128:(dc + 1) * 128], identb_t[:]))
                prev_ttr = ttr
                K.waitv("scalar", ttr)
                tev = K.act(lambda E, dc=dc, gi=gi: E.copy(
                    trbuf[:, (dc * GB + gi) * 128:(dc * GB + gi) * 128 + 128], PTRB[:]))
                tr_toks = [tr_toks[1], ("sACT", tev[1])]
        t0 = blk * GB * 128
        ncols = len(gs) * 128
        K.waitv("sync", tr_toks[0])
        K.waitv("sync", tr_toks[1])
        wr_tok = K.dma("sync", bass.AP(xn2fm_h, t0, [[T, 128], [128 * T, DC], [1, ncols]]),
                       trbuf[:, 0:DC * ncols].rearrange("p (dc c) -> p dc c", dc=DC))
    K.join()
    for t in reversed((g_x, g_t1, g_t2, g_a1, g_a2, trbuf)):
        K.free(t)
    K.free(PTRB)

    f_xt = K.sb("f_xt", [128, D])
    f_xn = K.sb("f_xn", [128, D])
    f_xnb = K.sb("f_xnb", [128, DC * ST], BF16)
    f_xnf = K.sb("f_xnf", [128, DC * 128])
    f_rw = K.sb("f_rw", [128, DC * 8])
    f_lg = K.sb("f_lg", [128, 8])
    f_g1 = K.sb("f_g1", [128, 8])
    f_g2 = K.sb("f_g2", [128, 8])
    f_m = K.sb("f_m", [128, 1])
    f_gate = K.sb("f_gate", [128, STG * 8])
    f_wu = [K.sb(f"f_wu{i}", [128, DC * 128], BF16) for i in range(2)]
    f_wdh = K.sb("f_wdh", [128, (FC // 2) * D], BF16)
    f_h = K.sb("f_h", [128, FC * ST], BF16)
    f_acc = K.sb("f_acc", [128, STG * D])
    PU = [K.psum(f"PU{i}", [128, MT]) for i in range(2)]
    PDW = [K.psum(f"PDW{i}", [128, D]) for i in range(2)]
    PRT = K.psum("PRT", [128, 8])

    t_rw = K.dma("sync", f_rw[:].rearrange("p (dc c) -> p dc c", dc=DC),
                 bass.AP(rw_h, 0, [[8, 128], [128 * 8, DC], [1, 8]]))
    K.waitv("tensor", t_rw)

    FH = FC // 2
    tff = None
    for st in range(NST):
        K.waitv("vector", tff)
        tacc0 = K.dve(lambda E: E.memset(f_acc[:], 0.0))
        tr_toks = [None, None]
        tgate = None
        prev_xn = None
        for tg in range(STG):
            g = st * STG + tg
            K.waitv("sync", prev_xn)
            t_x = K.dma("sync", f_xt[:], x_o[g * 128:(g + 1) * 128, :])
            K.waitv("scalar", t_x)
            K.waitv("scalar", tr_toks[0])
            K.waitv("scalar", tr_toks[1])
            txn = K.act(lambda E, g=g: E.activation(f_xn[:], f_xt[:], AF.Identity,
                                                    scale=stato_r[:, g:g + 1],
                                                    bias=stato_b[:, g:g + 1]))
            prev_xn = txn
            K.waitv("tensor", txn)
            for dc in range(DC):
                K.waitv("tensor", tr_toks[1])
                ttr = K.pe(lambda E, dc=dc: E.transpose(
                    PT32[:], f_xn[:, dc * 128:(dc + 1) * 128], identf_t[:]))
                K.waitv("scalar", ttr)
                K.act(lambda E, dc=dc: E.copy(
                    f_xnf[:, dc * 128:(dc + 1) * 128], PT32[:]))
                tev = K.act(lambda E, dc=dc, tg=tg: E.copy(
                    f_xnb[:, dc * ST + tg * 128: dc * ST + tg * 128 + 128], PT32[:]))
                tr_toks = [tr_toks[1], ("sACT", tev[1])]
            K.waitv("tensor", tr_toks[1])
            for dc in range(DC):
                K.pe(lambda E, dc=dc: E.matmul(
                    PRT[:], f_xnf[:, dc * 128:(dc + 1) * 128],
                    f_rw[:, dc * 8:(dc + 1) * 8],
                    start=(dc == 0), stop=(dc == DC - 1)))
            trt = K.tok("sPE")
            K.waitv("vector", trt)
            K.dve(lambda E: E.tensor_copy(f_lg[:], PRT[:]))
            K.dve(lambda E: E.tensor_reduce(f_m[:], f_lg[:], AX.X, ALU.max))
            tnm = K.dve(lambda E: E.tensor_scalar_mul(f_m[:], f_m[:], -1.0))
            K.waitv("scalar", tnm)
            te = K.act(lambda E: E.activation(f_g1[:], f_lg[:], AF.Exp, bias=f_m[:]))
            K.waitv("vector", te)
            K.dve(lambda E: E.tensor_reduce(f_m[:], f_g1[:], AX.X, ALU.add))
            K.dve(lambda E: E.reciprocal(f_m[:], f_m[:]))
            K.dve(lambda E: E.tensor_scalar(f_g1[:], f_g1[:], f_m[:], None, ALU.mult))
            K.dve(lambda E: E.tensor_reduce(f_m[:], f_g1[:], AX.X, ALU.max))
            K.dve(lambda E: E.tensor_scalar(f_g2[:], f_g1[:], f_m[:], None, ALU.is_equal))
            K.dve(lambda E: E.scalar_tensor_tensor(f_g2[:], f_g2[:], -1e30, f_g1[:],
                                                   ALU.mult, ALU.add))
            K.dve(lambda E: E.tensor_reduce(f_m[:], f_g2[:], AX.X, ALU.max))
            K.dve(lambda E: E.tensor_scalar(f_g2[:], f_g1[:], f_m[:], None, ALU.is_ge))
            tgate = K.dve(lambda E, tg=tg: E.tensor_tensor(
                f_gate[:, tg * 8:(tg + 1) * 8], f_g1[:], f_g2[:], ALU.mult))

        if DBGF:
            K.waitv("sync", tgate)
            for tg in range(STG):
                K.dma("sync", bass.AP(dbgg_h, (st * ST + tg * 128) * 8, [[8, 128], [1, 8]]),
                      f_gate[:, tg * 8:(tg + 1) * 8])
        wu_free = [None, None]
        pu_free = [None, None]
        tdn_last = None
        fwd_guard = None
        for e in range(NE):
            K.waitv("scalar", tdn_last)
            K.waitv("scalar", tgate)
            for fc in range(FC):
                buf = fc % 2
                K.waitv("gpsimd", wu_free[buf])
                twu = K.dma("gpsimd", f_wu[buf][:].rearrange("p (dc c) -> p dc c", dc=DC),
                            bass.AP(fup_h, (e * D) * F + fc * 128,
                                    [[F, 128], [128 * F, DC], [1, 128]]), "dG")
                K.waitv("tensor", twu)
                for mt in range(NMT):
                    pu = PU[mt % 2]
                    K.waitv("tensor", pu_free[mt % 2])
                    for dc in range(DC):
                        K.pe(lambda E, dc=dc, mt=mt, buf=buf, pu=pu: E.matmul(
                            pu[:], f_wu[buf][:, dc * 128:(dc + 1) * 128],
                            f_xnb[:, dc * ST + mt * MT: dc * ST + mt * MT + MT],
                            start=(dc == 0), stop=(dc == DC - 1)))
                    tmm = K.tok("sPE")
                    K.waitv("scalar", tmm)
                    tgl = K.act(lambda E, fc=fc, mt=mt, pu=pu: E.activation(
                        f_h[:, fc * ST + mt * MT: fc * ST + mt * MT + MT],
                        pu[:], AF.Gelu))
                    pu_free[mt % 2] = ("sACT", tgl[1])
                wu_free[buf] = ("sPE", tmm[1])
            if DBGF and e == 0 and st == 0:
                K.waitv("sync", K.tok("sACT"))
                for fc in range(FC):
                    K.dma("sync", bass.AP(dbgh_h, fc * 128 * ST, [[ST, 128], [1, ST]]),
                          f_h[:, fc * ST:(fc + 1) * ST])
            tgl_last = K.tok("sACT")
            for fh in range(2):
                K.waitv("gpsimd", fwd_guard)
                twd = K.dma("gpsimd", f_wdh[:].rearrange("p (fc c) -> p fc c", fc=FH),
                            bass.AP(fdn_h, (e * F + fh * FH * 128) * D,
                                    [[D, 128], [128 * D, FH], [1, D]]), "dG")
                K.waitv("tensor", twd)
                K.waitv("tensor", tgl_last)
                if e > 0 or fh > 0:
                    K.waitv("tensor", tga_last)
                for tg in range(STG):
                    pd = PDW[tg % 2]
                    if tg >= 2:
                        K.waitv("tensor", tga_toks[tg - 2])
                    for fc in range(FH):
                        fcg = fh * FH + fc
                        for dm in range(D // 512 if D >= 512 else 1):
                            MD = min(D, 512)
                            K.pe(lambda E, fcg=fcg, fc=fc, tg=tg, dm=dm, pd=pd, MD=MD: E.matmul(
                                pd[:, dm * MD: dm * MD + MD],
                                f_h[:, fcg * ST + tg * 128: fcg * ST + tg * 128 + 128],
                                f_wdh[:, fc * D + dm * MD: fc * D + dm * MD + MD],
                                start=(fc == 0), stop=(fc == FH - 1)))
                    tdn = K.tok("sPE")
                    K.waitv("vector", tdn)
                    if e == 0:
                        tga = K.dve(lambda E, tg=tg, pd=pd: E.scalar_tensor_tensor(
                            f_acc[:, tg * D: tg * D + D], pd[:], ones_t[:],
                            f_acc[:, tg * D: tg * D + D], ALU.mult, ALU.add))
                    else:
                        tga = K.dve(lambda E, tg=tg, e=e, pd=pd: E.scalar_tensor_tensor(
                            f_acc[:, tg * D: tg * D + D], pd[:],
                            f_gate[:, tg * 8 + e - 1: tg * 8 + e],
                            f_acc[:, tg * D: tg * D + D], ALU.mult, ALU.add))
                    if tg == 0:
                        tga_toks = {}
                    tga_toks[tg] = tga
                tga_last = tga
                tdn_last = tdn
                fwd_guard = tdn
        K.waitv("sync", tga_last)
        tff = K.dma("sync", bass.AP(ffnout_h, st * ST * D, [[D, 128], [128 * D, STG], [1, D]]),
                    f_acc[:].rearrange("p (tg d) -> p tg d", tg=STG))
    K.join()
    for t in reversed((f_xt, f_xn, f_xnb, f_xnf, f_rw, f_lg, f_g1, f_g2, f_m, f_gate,
                       f_wu[0], f_wu[1], f_wdh, f_h, f_acc)):
        K.free(t)
    for t in reversed((PU[0], PU[1], PDW[0], PDW[1], PRT)):
        K.free(t)

    winT_t = K.sb("winT_t", [128, DC * 2 * EH], BF16)
    convd_t = K.sb("convd_t", [128, 4 * SL * 128], BF16)
    xn2c = K.sb("xn2c", [128, DC * TC], BF16)
    xp_t = K.sb("xp_t", [128, SL * (TC + 4)], BF16)
    zs_t = K.sb("zs_t", [128, SL * TC], BF16)
    xc_t = K.sb("xc_t", [128, SL * TC], BF16)
    ndt_t = K.sb("ndt_t", [128, SL * TC], BF16)
    w_t = K.sb("w_t", [128, SL * TC], BF16)
    sp1 = K.sb("sp1", [128, TC])
    sp2 = K.sb("sp2", [128, TC])
    dtrmy = K.sb("dtrmy", [64, TC])
    dtrpr = K.sb("dtrpr", [64, TC])
    dtrs = K.sb("dtrs", [64, TC])
    bmy = K.sb("bmy", [16, TC])
    bpr = K.sb("bpr", [16, TC])
    cmy = K.sb("cmy", [16, TC])
    cpr = K.sb("cpr", [16, TC])
    bsb = K.sb("bsb", [16, TC], BF16)
    csb = K.sb("csb", [16, TC], BF16)
    brep = [K.sb(f"brep{i}", [128, TC], BF16) for i in range(2)]
    crep = [K.sb(f"crep{i}", [128, TC], BF16) for i in range(2)]
    a_t = [K.sb(f"a_t{i}", [128, TC]) for i in range(2)]
    b_t = [K.sb(f"b_t{i}", [128, TC], BF16) for i in range(2)]
    h_t = K.sb("h_t", [128, TC], BF16)
    tmp_t = K.sb("tmp_t", [128, TC], BF16)
    yacc = K.sb("yacc", [128, SL * TC])
    ybf = [K.sb(f"ybf{i}", [128, TC], BF16) for i in range(2)]
    PZ = [K.psum(f"PZ{i}", [128, TC]) for i in range(2)]
    PCV = K.psum("PCV", [128, TC])
    PXD = K.psum("PXD", [128, TC])
    PXB = K.psum("PXB", [128, TC])
    PREP = K.psum("PREP", [128, TC])

    a_free = [None, None]
    b_free = [None, None]
    ybf_free = [None, None]
    carry_tok = {}
    ysl_toks = []

    chunks = [("B", k) for k in range(NCHB - 1, -1, -1)] + [("A", k) for k in range(NCHA)]
    cur_dir = None

    for ci, (dr, k) in enumerate(chunks):
        d = 1 if dr == "B" else 0
        t0 = k * TC
        owned = (dr == "A") or (t0 < H)
        first = (dr == "B" and k == NCHB - 1) or (dr == "A" and k == 0)
        slot = ci
        K.join()

        if cur_dir != d:
            twl = K.dma("gpsimd", winT_t[:].rearrange("p (dc c) -> p dc c", dc=DC),
                        bass.AP(winT_h, d * 2 * EH,
                                [[4 * EH, 128], [128 * 4 * EH, DC], [1, 2 * EH]]), "dG")
            tcl = K.dma("gpsimd", convd_t[:],
                        convd[:, d * 4 * SL * 128:(d + 1) * 4 * SL * 128], "dG")
            cur_dir = d
            K.waitv("tensor", twl)
            K.waitv("tensor", tcl)

        txn = K.dma("sync", xn2c[:].rearrange("p (dc c) -> p dc c", dc=DC),
                    bass.AP(xn2fm_h, t0, [[T, 128], [128 * T, DC], [1, TC]]))
        if first:
            tst_ = K.dve(lambda E: E.memset(xp_t[:], 0.0))
        else:
            for i in range(SL):
                if dr == "A":
                    tst_ = K.dve(lambda E, i=i: E.tensor_copy(
                        xp_t[:, i * (TC + 4): i * (TC + 4) + 3],
                        xp_t[:, i * (TC + 4) + TC: i * (TC + 4) + TC + 3]))
                else:
                    tst_ = K.dve(lambda E, i=i: E.tensor_copy(
                        xp_t[:, i * (TC + 4) + TC: i * (TC + 4) + TC + 3],
                        xp_t[:, i * (TC + 4): i * (TC + 4) + 3]))
        K.waitv("scalar", tst_)
        K.waitv("tensor", txn)

        off0 = 3 if dr == "A" else 0
        pz_ev = [None, None]
        last_ev = None
        nxz = 2 * SL if owned else SL
        for j in range(nxz):
            i = j % SL
            isz = j >= SL
            pz = PZ[j % 2]
            K.waitv("tensor", pz_ev[j % 2])
            for dc in range(DC):
                col = dc * 2 * EH + (EH if isz else 0) + i * 128
                K.pe(lambda E, col=col, dc=dc, pz=pz: E.matmul(
                    pz[:], winT_t[:, col: col + 128],
                    xn2c[:, dc * TC:(dc + 1) * TC],
                    start=(dc == 0), stop=(dc == DC - 1)))
            tm = K.tok("sPE")
            K.waitv("scalar", tm)
            if not isz:
                last_ev = K.act(lambda E, i=i, pz=pz, off0=off0: E.copy(
                    xp_t[:, i * (TC + 4) + off0: i * (TC + 4) + off0 + TC], pz[:]))
            else:
                last_ev = K.act(lambda E, i=i, pz=pz: E.activation(
                    zs_t[:, i * TC:(i + 1) * TC], pz[:], AF.Silu))
            pz_ev[j % 2] = last_ev

        K.waitv("tensor", last_ev)
        txc = None
        for i in range(SL):
            if txc is not None:
                K.waitv("tensor", txc)
            for tap in range(4):
                off = tap if dr == "A" else (3 - tap)
                K.pe(lambda E, i=i, tap=tap, off=off: E.matmul(
                    PCV[:], convd_t[:, (tap * SL + i) * 128:(tap * SL + i) * 128 + 128],
                    xp_t[:, i * (TC + 4) + off: i * (TC + 4) + off + TC],
                    start=(tap == 0), stop=(tap == 3)))
            tcv = K.tok("sPE")
            K.waitv("scalar", tcv)
            txc = K.act(lambda E, i=i, d=d: E.activation(
                xc_t[:, i * TC:(i + 1) * TC], PCV[:], AF.Silu,
                bias=bconv_t[:, d * SL + i: d * SL + i + 1]))

        K.waitv("tensor", txc)
        for grp, (pj, c0, cw) in enumerate(((PXD, 0, 64), (PXB, 64, 16), (PCV, 80, 16))):
            for i in range(SL):
                K.pe(lambda E, i=i, pj=pj, c0=c0, cw=cw, d=d: E.matmul(
                    pj[0:cw, :], wx_st[:, (d * SL + i) * 96 + c0:(d * SL + i) * 96 + c0 + cw],
                    xc_t[:, i * TC:(i + 1) * TC],
                    start=(i == 0), stop=(i == SL - 1)))
        txd = K.tok("sPE")
        K.waitv("scalar", txd)
        K.act(lambda E: E.copy(dtrmy[:], PXD[0:64, :]))
        K.act(lambda E: E.copy(bmy[:], PXB[0:16, :]))
        tevc = K.act(lambda E: E.copy(cmy[:], PCV[0:16, :]))
        K.waitv("gpsimd", tevc)
        xbufs = (dtrmy, bmy, cmy)
        rbufs = (dtrpr, bpr, cpr)
        v3 = None
        for kk in range(3):
            nr = (64, 16, 16)[kk]
            v3 = K.idma(lambda E, s, kk=kk, slot=slot, nr=nr: E.indirect_dma_start(
                out=xdsl_h.ap()[:, :],
                out_offset=bass.IndirectOffsetOnAxis(
                    xdidx_t[0:nr, slot * 6 + kk: slot * 6 + kk + 1], 0),
                in_=xbufs[kk][:], in_offset=None).then_inc(s, 16))
        tbar = K.barrier([v3])
        v6 = None
        for kk in range(3):
            nr = (64, 16, 16)[kk]
            v6 = K.idma(lambda E, s, kk=kk, slot=slot, nr=nr: E.indirect_dma_start(
                out=rbufs[kk][:], out_offset=None,
                in_=xdsl_h.ap()[:, :],
                in_offset=bass.IndirectOffsetOnAxis(
                    xdidx_t[0:nr, slot * 6 + 3 + kk: slot * 6 + 3 + kk + 1], 0)).then_inc(s, 16))
        K.waitv("vector", v6)
        K.dve(lambda E: E.tensor_tensor(dtrs[:], dtrmy[:], dtrpr[:], ALU.add))
        tbs = K.dve(lambda E: E.tensor_tensor(bsb[:], bmy[:], bpr[:], ALU.add))
        tcs = K.dve(lambda E: E.tensor_tensor(csb[:], cmy[:], cpr[:], ALU.add))

        K.waitv("tensor", tcs)
        prev_sp = None
        prev_nd = None
        for i in range(SL):
            K.waitv("tensor", prev_sp)
            K.pe(lambda E, i=i, d=d: E.matmul(
                PXD[:], wdt_st[:, d * EH + i * 128: d * EH + i * 128 + 128],
                dtrs[:], start=True, stop=True))
            tdt = K.tok("sPE")
            K.waitv("scalar", tdt)
            K.waitv("scalar", prev_nd)
            K.act(lambda E, i=i, d=d: E.activation(sp1[:], PXD[:], AF.Abs,
                                                   bias=bdt_t[:, d * SL + i: d * SL + i + 1]))
            K.act(lambda E: E.activation(sp1[:], sp1[:], AF.Exp, scale=-1.0))
            K.act(lambda E: E.activation(sp1[:], sp1[:], AF.Ln, bias=1.0))
            tsp = K.act(lambda E, i=i, d=d: E.activation(sp2[:], PXD[:], AF.Relu,
                                                         bias=bdt_t[:, d * SL + i: d * SL + i + 1]))
            prev_sp = tsp
            K.waitv("vector", tsp)
            tnd = K.dve(lambda E, i=i: E.scalar_tensor_tensor(
                ndt_t[:, i * TC:(i + 1) * TC], sp1[:], -1.0, sp2[:], ALU.mult, ALU.subtract))
            tw = K.dve(lambda E, i=i: E.tensor_tensor(
                w_t[:, i * TC:(i + 1) * TC], ndt_t[:, i * TC:(i + 1) * TC],
                xc_t[:, i * TC:(i + 1) * TC], ALU.mult))
            prev_nd = tw
        tndt = K.tok("sDVE")

        K.waitv("scalar", tndt)
        K.waitv("gpsimd", tndt)
        if owned:
            tyz = K.dve(lambda E: E.memset(yacc[:], 0.0))
        rep_prev = [None, None]
        prep_guard = None
        for n in range(N):
            p = n % 2
            K.waitv("tensor", rep_prev[p])
            K.waitv("tensor", prep_guard)
            trb = K.pe(lambda E, n=n: E.matmul(
                PREP[:], selm_t[:, n * 128:(n + 1) * 128], bsb[:], start=True, stop=True))
            K.waitv("scalar", trb)
            tbe = K.act(lambda E, p=p: E.copy(brep[p][:], PREP[:]))
            if owned:
                K.waitv("tensor", tbe)
                trc = K.pe(lambda E, n=n: E.matmul(
                    PREP[:], selm_t[:, n * 128:(n + 1) * 128], csb[:], start=True, stop=True))
                K.waitv("scalar", trc)
                tce = K.act(lambda E, p=p: E.copy(crep[p][:], PREP[:]))
                prep_guard = tce
            else:
                prep_guard = tbe
            b_last = None
            y_last = None
            for i in range(SL):
                ap_ = i % 2
                idx = d * SL * N + i * N + n
                K.waitv("scalar", a_free[ap_])
                ta = K.act(lambda E, i=i, n=n, d=d, ap_=ap_: E.activation(
                    a_t[ap_][:], ndt_t[:, i * TC:(i + 1) * TC], AF.Exp,
                    scale=apos_t[:, d * SL * N + i * N + n: d * SL * N + i * N + n + 1]))
                a_free[ap_] = None
                K.waitv("gpsimd", b_free[ap_])
                K.waitv("gpsimd", tbe)
                tb = K.pool(lambda E, i=i, ap_=ap_, p=p: E.tensor_tensor(
                    b_t[ap_][:], w_t[:, i * TC:(i + 1) * TC], brep[p][:], ALU.mult))
                b_free[ap_] = None
                b_last = tb
                K.waitv("vector", ta)
                K.waitv("vector", tb)
                K.waitv("vector", carry_tok.get(idx))
                if dr == "A":
                    tts = K.dve(lambda E, ap_=ap_, idx=idx: E.tensor_tensor_scan(
                        h_t[:], a_t[ap_][:], b_t[ap_][:], carr[:, idx:idx + 1],
                        ALU.mult, ALU.add))
                    lastcol = TC - 1
                else:
                    tts = K.dve(lambda E, ap_=ap_, idx=idx: E.tensor_tensor_scan(
                        h_t[:, ::-1], a_t[ap_][:, ::-1], b_t[ap_][:, ::-1],
                        carr[:, idx:idx + 1], ALU.mult, ALU.add))
                    lastcol = 0
                a_free[ap_] = ("sDVE", tts[1])
                b_free[ap_] = ("sDVE", tts[1])
                tcar = K.dve(lambda E, idx=idx, lastcol=lastcol: E.tensor_copy(
                    carr[:, idx:idx + 1], h_t[:, lastcol:lastcol + 1]))
                carry_tok[idx] = tcar
                if owned:
                    K.waitv("vector", tce)
                    K.dve(lambda E, p=p: E.tensor_tensor(tmp_t[:], h_t[:], crep[p][:], ALU.mult))
                    if n == 0:
                        ty = K.dve(lambda E, i=i: E.tensor_copy(
                            yacc[:, i * TC:(i + 1) * TC], tmp_t[:]))
                    else:
                        ty = K.dve(lambda E, i=i: E.tensor_tensor(
                            yacc[:, i * TC:(i + 1) * TC], yacc[:, i * TC:(i + 1) * TC],
                            tmp_t[:], ALU.add))
                    y_last = ty
            rep_prev[p] = ("sPOOL", b_last[1]) if not owned else ("sDVE", y_last[1])

        if owned:
            for i in range(SL):
                yb = ybf[i % 2]
                K.waitv("vector", ybf_free[i % 2])
                K.dve(lambda E, i=i, d=d: E.scalar_tensor_tensor(
                    sp1[:], xc_t[:, i * TC:(i + 1) * TC], dvec_t[:, d * SL + i: d * SL + i + 1],
                    yacc[:, i * TC:(i + 1) * TC], ALU.mult, ALU.subtract))
                tyf = K.dve(lambda E, i=i, yb=yb: E.tensor_tensor(
                    yb[:], sp1[:], zs_t[:, i * TC:(i + 1) * TC], ALU.mult))
                K.waitv("gpsimd", tyf)
                vy = K.idma(lambda E, s, yb=yb, i=i, d=d, t0=t0:
                            E.indirect_dma_start(
                                out=ysl_h.ap()[:, :],
                                out_offset=bass.IndirectOffsetOnAxis(
                                    yslidx_t[:, d * SL + i: d * SL + i + 1], 0),
                                in_=yb[:], in_offset=None,
                                element_offset=t0).then_inc(s, 16))
                ysl_toks.append(vy)
                ybf_free[i % 2] = vy

    tyb = K.barrier(ysl_toks[-1:])
    K.join()
    for t in reversed((winT_t, convd_t, xn2c, xp_t, zs_t, xc_t, ndt_t, w_t, sp1, sp2,
                       dtrmy, dtrpr, dtrs, bmy, bpr, cmy, cpr, bsb, csb,
                       brep[0], brep[1], crep[0], crep[1], a_t[0], a_t[1],
                       b_t[0], b_t[1], h_t, tmp_t, yacc, ybf[0], ybf[1])):
        K.free(t)
    for t in reversed((PZ[0], PZ[1], PCV, PXD, PXB, PREP)):
        K.free(t)

    p3_y = K.sb("p3_y", [128, 2 * EC * 128], BF16)
    p3_w = K.sb("p3_w", [128, 2 * EC * D], BF16)
    p3_x = K.sb("p3_x", [128, D])
    p3_n = K.sb("p3_n", [128, D])
    p3_f = K.sb("p3_f", [128, D])
    p3_o = K.sb("p3_o", [128, D])
    PP3 = [K.psum(f"PP3{i}", [128, D]) for i in range(2)]

    if DBGF:
        for qrow in range(0, Q, 128):
            K.dma("sync", bass.AP(dbgff_h, qrow * D, [[D, 128], [1, D]]),
                  bass.AP(ffnout_h, qrow * D, [[D, 128], [1, D]]))
    twout = K.dma("gpsimd", p3_w[:].rearrange("p (dc c) -> p dc c", dc=2 * EC),
                  bass.AP(woutT_h, 0, [[D, 128], [128 * D, 2 * EC], [1, D]]), "dG")
    K.waitv("tensor", twout)

    out_toks = []
    pp_free = [None, None]
    prev_fin = None
    prev_out = None
    for tg in range(NGO):
        ysl_flat = bass.AP(ysl_h, 0, [[128, (4 * EH + 128) * H // 128], [1, 128]])
        tydma = None
        for ehs in range(2):
            for dd in range(2):
                for i in range(SL):
                    col = ((dd * 2 + ehs) * SL + i) * NGO + tg
                    dst = (dd * EC + ehs * SL + i) * 128
                    tydma = K.idma(lambda E, s, col=col, dst=dst:
                                   E.indirect_dma_start(
                                       out=p3_y[:, dst:dst + 128], out_offset=None,
                                       in_=ysl_flat,
                                       in_offset=bass.IndirectOffsetOnAxis(
                                           yrdidx_t[:, col:col + 1], 0)).then_inc(s, 16))
        pp = PP3[tg % 2]
        K.waitv("tensor", pp_free[tg % 2])
        K.waitv("tensor", tydma)
        nec = 2 * EC
        for ecg in range(nec):
            for dm in range(max(1, D // 512)):
                MD = min(D, 512)
                K.pe(lambda E, ecg=ecg, dm=dm, pp=pp, MD=MD: E.matmul(
                    pp[:, dm * MD: dm * MD + MD],
                    p3_y[:, ecg * 128:(ecg + 1) * 128],
                    p3_w[:, ecg * D + dm * MD: ecg * D + dm * MD + MD],
                    start=(ecg == 0), stop=(ecg == nec - 1)))
        tpw = K.tok("sPE")
        K.waitv("sync", prev_fin)
        t_x = K.dma("sync", p3_x[:], x_o[tg * 128:(tg + 1) * 128, :])
        t_f = K.dma("sync", p3_f[:], bass.AP(ffnout_h, tg * 128 * D, [[D, 128], [1, D]]))
        K.waitv("scalar", t_x)
        txn = K.act(lambda E, tg=tg: E.activation(p3_n[:], p3_x[:], AF.Identity,
                                                  scale=stato_r[:, tg:tg + 1],
                                                  bias=stato_b[:, tg:tg + 1]))
        K.waitv("vector", txn)
        K.waitv("vector", t_f)
        K.waitv("vector", tpw)
        K.waitv("vector", prev_out)
        K.dve(lambda E: E.scalar_tensor_tensor(p3_o[:], p3_n[:], 2.0, p3_x[:],
                                               ALU.mult, ALU.add))
        K.dve(lambda E: E.tensor_tensor(p3_o[:], p3_o[:], p3_f[:], ALU.add))
        tfin = K.dve(lambda E, pp=pp: E.scalar_tensor_tensor(
            p3_o[:], pp[:], 0.5, p3_o[:], ALU.mult, ALU.add))
        pp_free[tg % 2] = ("sDVE", tfin[1])
        prev_fin = tfin
        K.waitv("sync", tfin)
        tout = K.dma("sync", out_o[tg * 128:(tg + 1) * 128, :], p3_o[:])
        prev_out = tout
        out_toks.append(tout)

    K.waitv("sync", out_toks[-1])
    K.join()

    with K.stack:
        with nc.Block() as block:
            @block.sync
            def _(E):
                for f in K.ops["sync"]:
                    f(E)

            @block.tensor
            def _(E):
                for f in K.ops["tensor"]:
                    f(E)

            @block.vector
            def _(E):
                for f in K.ops["vector"]:
                    f(E)

            @block.scalar
            def _(E):
                for f in K.ops["scalar"]:
                    f(E)

            @block.gpsimd
            def _(E):
                for f in K.ops["gpsimd"]:
                    f(E)
    return nc



INPUT_NAMES = (
    "L0_attn_x", "L0_attn_ln_w", "L0_attn_ln_b",
    "L0_attn_f_W_in", "L0_attn_f_W_conv", "L0_attn_f_b_conv", "L0_attn_f_W_x",
    "L0_attn_f_W_dt", "L0_attn_f_b_dt", "L0_attn_f_A_log", "L0_attn_f_D", "L0_attn_f_W_out",
    "L0_attn_b_W_in", "L0_attn_b_W_conv", "L0_attn_b_b_conv", "L0_attn_b_W_x",
    "L0_attn_b_W_dt", "L0_attn_b_b_dt", "L0_attn_b_A_log", "L0_attn_b_D", "L0_attn_b_W_out",
    "L0_ffn_shared_up", "L0_ffn_shared_down", "L0_ffn_routed_up", "L0_ffn_routed_down",
    "L0_ffn_router_w")


def _prep_core_inputs(cfg, c, I, nonce, solo=False):
    T, D, EH, N, R, F, NE, TC = (cfg[k] for k in ("T", "D", "EH", "N", "R", "F", "NE", "TC"))
    SL = EH // 128
    H, Q = T // 2, T // 4
    E = 2 * EH
    NCH = H // TC + T // TC
    NFLAG = NCH + 8
    b = c // 4
    hi = (c % 4) >= 2
    eh = c % 2
    qq = eh
    if solo:
        b, hi, qq = 0, False, 0

    x = I["L0_attn_x"][b]
    xp_ = x[::-1] if hi else x
    xp_ = np.ascontiguousarray(xp_, np.float32)
    x_o = np.ascontiguousarray(xp_[qq * Q:(qq + 1) * Q])

    pfx = ("f_", "b_") if not hi else ("b_", "f_")
    g = lambda d, nm: I["L0_attn_" + pfx[d] + nm]

    Etrue = EH if solo else E
    ehw = 0 if solo else eh
    winT = np.zeros((D, 4 * EH), np.float32)
    for d in range(2):
        Win = g(d, "W_in")
        xp_rows = Win[ehw * EH:(ehw + 1) * EH]
        z_rows = Win[Etrue + ehw * EH: Etrue + (ehw + 1) * EH]
        winT[:, d * 2 * EH: d * 2 * EH + EH] = xp_rows.T
        winT[:, d * 2 * EH + EH: d * 2 * EH + 2 * EH] = z_rows.T

    convdm = np.zeros((2 * 4 * SL * 128, 128), np.float32)
    for d in range(2):
        Wc = g(d, "W_conv")[ehw * EH:(ehw + 1) * EH]
        for tap in range(4):
            for i in range(SL):
                r0 = ((d * 4 + tap) * SL + i) * 128
                convdm[r0:r0 + 128, :] = np.diag(Wc[i * 128:(i + 1) * 128, tap])
    convd = np.ascontiguousarray(
        convdm.reshape(2 * 4 * SL, 128, 128).transpose(1, 0, 2).reshape(128, -1))

    def col2(nm):
        out = np.zeros((128, 2 * SL), np.float32)
        for d in range(2):
            v = g(d, nm)[ehw * EH:(ehw + 1) * EH]
            out[:, d * SL:(d + 1) * SL] = v.reshape(SL, 128).T
        return out

    bconv = col2("b_conv")
    bdt = col2("b_dt")
    dvec = col2("D")
    alog = np.zeros((128, 2 * SL * N), np.float32)
    for d in range(2):
        A = g(d, "A_log")[ehw * EH:(ehw + 1) * EH]
        alog[:, d * SL * N:(d + 1) * SL * N] = \
            A.reshape(SL, 128, N).transpose(1, 0, 2).reshape(128, SL * N)

    wxT = np.zeros((2 * EH, 96), np.float32)
    for d in range(2):
        Wx = g(d, "W_x")
        half = slice(ehw * EH, (ehw + 1) * EH)
        wxT[d * EH:(d + 1) * EH, 0:R] = Wx[0:R, half].T
        wxT[d * EH:(d + 1) * EH, 64:64 + N] = Wx[R:R + N, half].T
        wxT[d * EH:(d + 1) * EH, 80:80 + N] = Wx[R + N:R + 2 * N, half].T
    if solo:
        wxT *= 0.5

    wdtT = np.zeros((64, 2 * EH), np.float32)
    for d in range(2):
        Wdt = g(d, "W_dt")[ehw * EH:(ehw + 1) * EH]
        wdtT[0:R, d * EH:(d + 1) * EH] = Wdt.T

    woutT = np.zeros((2 * E, D), np.float32)
    for d in range(2):
        if solo:
            woutT[d * E: d * E + EH] = g(d, "W_out").T[:EH]
        else:
            woutT[d * E:(d + 1) * E] = g(d, "W_out").T

    lnw_b = np.ascontiguousarray(np.broadcast_to(I["L0_attn_ln_w"], (128, D)), np.float32)
    lnb_b = np.ascontiguousarray(np.broadcast_to(I["L0_attn_ln_b"], (128, D)), np.float32)

    fup = np.concatenate([I["L0_ffn_shared_up"], I["L0_ffn_routed_up"]], 0)[:NE]
    fdn = np.concatenate([I["L0_ffn_shared_down"], I["L0_ffn_routed_down"]], 0)[:NE]
    fup = np.ascontiguousarray(fup.reshape(NE * D, F), np.float32)
    fdn = np.ascontiguousarray(fdn.reshape(NE * F, D), np.float32)
    rw = np.ascontiguousarray(I["L0_ffn_router_w"], np.float32)

    selm = np.zeros((16, N * 128), np.float32)
    for n in range(N):
        selm[n, n * 128:(n + 1) * 128] = 1.0
    ident = np.eye(128, dtype=np.float32)
    meta = np.zeros((1, 16), np.int32)
    meta[0, 0] = eh
    meta[0, 1] = 1 - eh
    meta[0, 2] = nonce
    meta[0, 3] = qq
    flagval = (nonce + 1 + np.arange(NFLAG)).astype(np.int32).reshape(1, NFLAG)

    p = np.arange(128, dtype=np.uint32)
    xdump = 2 * NCH * 96 + p
    xdidx = np.zeros((128, NCH * 6), np.uint32)
    offs = (0, 64, 80)
    nrs = (64, 16, 16)
    for s_ in range(NCH):
        for k in range(3):
            reh = eh if solo else (1 - eh)
            w = np.where(p < nrs[k], eh * NCH * 96 + s_ * 96 + offs[k] + p, xdump)
            r_ = np.where(p < nrs[k], reh * NCH * 96 + s_ * 96 + offs[k] + p, xdump)
            xdidx[:, s_ * 6 + k] = w
            xdidx[:, s_ * 6 + 3 + k] = r_
    EHl = EH
    yslidx = np.zeros((128, 2 * SL), np.uint32)
    for d in range(2):
        for i in range(SL):
            yslidx[:, d * SL + i] = eh * 2 * EHl + d * EHl + i * 128 + p
    NGO = Q // 128
    yrdidx = np.zeros((128, 4 * SL * NGO), np.uint32)
    for dd_ in range(2):
        for ehs in range(2):
            for i in range(SL):
                for tg in range(NGO):
                    col = ((dd_ * 2 + ehs) * SL + i) * NGO + tg
                    es = eh if solo else ehs
                    yrdidx[:, col] = ((es * 2 * EHl + dd_ * EHl + i * 128 + p) * (H // 128)
                                     + qq * (Q // 128) + tg)
    fdump = NFLAG * 2 + p
    flgidx = np.zeros((128, NFLAG), np.uint32)
    for s_ in range(NFLAG):
        flgidx[:, s_] = fdump
        flgidx[0, s_] = s_ * 2 + eh
        flgidx[1, s_] = (s_ * 2 + 1 - eh) if solo else (NFLAG * 2 + 64 + (s_ % 32))
    flgsrc = np.zeros((2, NFLAG * 16), np.int32)
    for s_ in range(NFLAG):
        flgsrc[0, s_ * 16:(s_ + 1) * 16] = nonce + 1 + s_
        if solo:
            flgsrc[1, s_ * 16:(s_ + 1) * 16] = nonce + 1 + s_

    return dict(x_b=xp_, x_o=x_o, winT=winT, convd=convd, bconv=bconv, bdt=bdt,
                alog=alog, dvec=dvec, wxT=wxT, wdtT=wdtT, lnw_b=lnw_b, lnb_b=lnb_b,
                woutT=woutT, fup=fup, fdn=fdn, rw=rw, selm=selm, ident=ident,
                meta=meta, flagval=flagval, xdidx=xdidx, yslidx=yslidx,
                yrdidx=yrdidx, flgidx=flgidx, flgsrc=flgsrc)


def assemble_output(cfg, results):
    T, D = cfg["T"], cfg["D"]
    H, Q = T // 2, T // 4
    out = np.zeros((2, T, D), np.float32)
    for c in range(8):
        b = c // 4
        hi = (c % 4) >= 2
        qq = c % 2
        y = results[c]["out_o"]
        if not hi:
            out[b, qq * Q:(qq + 1) * Q] = y
        else:
            out[b, T - (qq + 1) * Q: T - qq * Q] = y[::-1]
    return out


_BUILD_CACHE = {}


def kernel(**inputs):
    assert all(n in inputs for n in INPUT_NAMES), sorted(inputs)
    cfg = mk_cfg(full=True)
    key = "full"
    if key not in _BUILD_CACHE:
        _BUILD_CACHE[key] = build(cfg)
    nc = _BUILD_CACHE[key]
    nonce = int(np.random.randint(1, 2**30))
    in_maps = [_prep_core_inputs(cfg, c, inputs, nonce) for c in range(8)]
    res = run_bass_kernel_spmd(nc, in_maps, core_ids=list(range(8)))
    out = assemble_output(cfg, [res.results[c] for c in range(8)])
    return out.astype(np.float32)


# version 04
# speedup vs baseline: 1.7405x; 1.7405x over previous
import contextlib
import numpy as np
import ml_dtypes
import concourse.bass as bass
import concourse.mybir as mybir
from concourse.bass_utils import run_bass_kernel_spmd

F32 = mybir.dt.float32
BF16 = mybir.dt.bfloat16
I32 = mybir.dt.int32
U32 = mybir.dt.uint32
AF = mybir.ActivationFunctionType
ALU = mybir.AluOpType
AX = mybir.AxisListType
BF16NP = ml_dtypes.bfloat16

ENGINES = ("sync", "tensor", "vector", "scalar", "gpsimd")


def mk_cfg(full=True):
    if full:
        return dict(T=16384, D=1024, EH=1024, N=16, R=64, F=4096, NE=9, TC=512,
                    ST=1024, EPS=1e-5, POLL_OUT=4000, POLL_IN=64)
    return dict(T=512, D=256, EH=128, N=4, R=32, F=256, NE=3, TC=128,
                ST=128, EPS=1e-5, POLL_OUT=1000, POLL_IN=8)


def mk_cfg_mid():
    return dict(T=2048, D=256, EH=128, N=4, R=32, F=512, NE=3, TC=128,
                ST=256, EPS=1e-5, POLL_OUT=1000, POLL_IN=8)


class Builder:
    def __init__(self, cfg):
        self.cfg = cfg
        self.nc = bass.Bass("TRN2", target_bir_lowering=False)
        self.ops = {e: [] for e in ENGINES}
        self.cnt = {}
        self.sems = {}
        self.stack = contextlib.ExitStack()
        self._mgrs = {}
        self._regs = {}
        self.selfsync = bool(cfg.get("SELFSYNC", False))
        self.pipedma = (not self.selfsync) and bool(cfg.get("PIPEDMA", True))
        self.blanket = set(cfg.get("BLANKET", ("scalar", "vector", "gpsimd")))
        if self.selfsync:
            self.blanket = set(ENGINES)

    def sb(self, nm, shp, dt=F32):
        mgr = self.nc.sbuf_tensor(nm, shp, dt)
        t = mgr.__enter__()
        self._mgrs[nm] = mgr
        return t

    def psum(self, nm, shp, dt=F32):
        mgr = self.nc.psum_tensor(nm, shp, dt)
        t = mgr.__enter__()
        self._mgrs[nm] = mgr
        return t

    def free(self, t):
        self._mgrs.pop(t.name).__exit__(None, None, None)

    def sem(self, name):
        h = self.stack.enter_context(self.nc.semaphore(name))
        self.sems[name] = h
        self.cnt[name] = 0
        return h

    def emit(self, eng, fn):
        self.ops[eng].append(fn)

    def reg(self, E, name):
        rr = self._regs.setdefault(name, None)
        if rr is None:
            rr = E.alloc_register(name=name)
            self._regs[name] = rr
        return rr

    def inc(self, name, k=1):
        self.cnt[name] += k
        return self.cnt[name]

    def tok(self, name):
        return (name, self.cnt[name])

    def waitv(self, eng, tokv):
        if tokv is None:
            return
        nm, v = tokv
        if nm in ("dO", "dG", "dS"):
            v = self.cnt[nm]
        self.emit(eng, lambda E, s=self.sems[nm], vv=v: E.wait_ge(s, vv))

    def dma(self, eng, out, in_, sem="dO"):
        v = self.inc(sem, 16)
        self.emit(eng, lambda E, o=out, i=in_, s=self.sems[sem]:
                  E.dma_start(out=o, in_=i).then_inc(s, 16))
        pipe = self.pipedma and self.cfg.get("PIPE_DO" if sem == "dO" else "PIPE_DG", True)
        if not pipe:
            self.emit(eng, lambda E, s=self.sems[sem], vv=v: E.wait_ge(s, vv))
        return (sem, v)

    def idma(self, fn, sem="dG"):
        v = self.inc(sem, 16)
        self.emit("gpsimd", lambda E, f=fn, s=self.sems[sem]: f(E, s))
        pipe = self.pipedma and self.cfg.get("PIPE_IDMA", True)
        if not pipe:
            self.emit("gpsimd", lambda E, s=self.sems[sem], vv=v: E.wait_ge(s, vv))
        return (sem, v)

    def _c(self, eng, semname, fn):
        if eng in self.blanket:
            prev = self.cnt[semname]
            self.emit(eng, lambda E, s=self.sems[semname], vv=prev: E.wait_ge(s, vv))
        v = self.inc(semname)
        self.emit(eng, lambda E, f=fn, s=self.sems[semname]: f(E).then_inc(s, 1))
        return (semname, v)

    def pe(self, fn):
        return self._c("tensor", "sPE", fn)

    def act(self, fn):
        return self._c("scalar", "sACT", fn)

    def dve(self, fn):
        return self._c("vector", "sDVE", fn)

    def pool(self, fn):
        return self._c("gpsimd", "sPOOL", fn)

    def join(self):
        self.emit("sync", lambda E, s=self.sems["dO"], vv=self.cnt["dO"]: E.wait_ge(s, vv))
        self.emit("gpsimd", lambda E, s=self.sems["dG"], vv=self.cnt["dG"]: E.wait_ge(s, vv))
        v = self.inc("sJOIN", 5)
        for e in ENGINES:
            self.emit(e, lambda E, s=self.sems["sJOIN"]: E.nop().then_inc(s, 1))
        for e in ENGINES:
            self.emit(e, lambda E, s=self.sems["sJOIN"], vv=v: E.wait_ge(s, vv))

    def barrier(self, data_toks):
        slot = self._bar_slot
        self._bar_slot += 1
        assert slot < self._nflag
        for t in data_toks:
            self.waitv("gpsimd", t)
        flags_h, flagval_t, flagrd, flgidx_t, flgsrc_t = self._bar_tiles
        dS = self.sems["dS"]
        PO, PI = self.cfg["POLL_OUT"], self.cfg["POLL_IN"]

        def _run(E):
            if self._ds_reg is None:
                self._ds_reg = E.alloc_register(name="ds_count")
                E.reg_mov(self._ds_reg, 0)
                self._bar_regs = tuple(E.alloc_register(name=f"barreg{j}") for j in range(5))
            tgt, done, r0, r1, it = self._bar_regs
            nc_ = E.bass
            engs = [E.engine]
            E.indirect_dma_start(
                out=flags_h.ap()[:, :],
                out_offset=bass.IndirectOffsetOnAxis(flgidx_t[0:2, slot:slot + 1], 0),
                in_=flgsrc_t[0:2, slot * 16:(slot + 1) * 16], in_offset=None).then_inc(dS, 16)
            E.reg_alu(self._ds_reg, self._ds_reg, 16, ALU.add)
            E.reg_load(tgt, flagval_t[0:1, slot:slot + 1])
            E.reg_mov(done, 0)
            E.reg_mov(it, 0)
            loop_s = f"barloop{slot}"
            loop_e = f"barend{slot}"
            nc_.br(loop_s, engines=engs)
            with nc_.body(loop_s, valid_engines=engs):
                E.reg_alu(self._ds_reg, self._ds_reg, 16, ALU.add)
                E.dma_start(out=flagrd[0:1, 0:2],
                            in_=bass.AP(flags_h, slot * 2 * 16, [[16, 2], [1, 1]]),
                            allow_slow_non_contiguous=True).then_inc(dS, 16)
                E.wait_ge(dS, self._ds_reg)
                E.reg_load(r0, flagrd[0:1, 0:1])
                E.reg_load(r1, flagrd[0:1, 1:2])
                with E.If_cmp(r0, tgt, "IS_GE"):
                    with E.If_cmp(r1, tgt, "IS_GE"):
                        E.reg_mov(done, 1)
                E.reg_alu(it, it, 1, ALU.add)
                with E.If_ne(done, 0):
                    E.reg_mov(it, PO * PI)
                nc_.br_lt(it, PO * PI, on_true=loop_s, on_false=loop_e, engines=engs)
            nc_.switch_bb(loop_e)
            return E.nop()

        v = self.inc("sBAR")
        self.emit("gpsimd", lambda E, f=_run, s=self.sems["sBAR"]: f(E).then_inc(s, 1))
        return ("sBAR", v)


def build(cfg):
    K = Builder(cfg)
    nc = K.nc
    T, D, EH, N, R, F, NE, TC, ST = (cfg[k] for k in
                                     ("T", "D", "EH", "N", "R", "F", "NE", "TC", "ST"))
    EPS = cfg["EPS"]
    SL = EH // 128
    DC = D // 128
    H = T // 2
    Q = T // 4
    E = 2 * EH
    EC = E // 128
    FC = F // 128
    NGA = T // 128
    NGO = Q // 128
    NST = Q // ST
    STG = ST // 128
    GB = min(16, NGA)
    NCHA = H // TC
    NCHB = T // TC
    NCH = NCHA + NCHB
    NFLAG = NCH + 8
    MT = min(ST, 512)
    NMT = ST // MT
    assert TC <= 512

    def din(nm, shp, dt=F32):
        h = nc.dram_tensor(nm, shp, dt, kind="ExternalInput")
        return h, h.ap()

    x_b_h, x_b = din("x_b", [T, D])
    x_o_h, x_o = din("x_o", [Q, D])
    winT_h, winT = din("winT", [D, 4 * EH])
    convd_h, convd = din("convd", [128, 2 * 4 * SL * 128])
    bconv_h, bconv_ap = din("bconv", [128, 2 * SL])
    bdt_h, bdt_ap = din("bdt", [128, 2 * SL])
    alog_h, alog_ap = din("alog", [128, 2 * SL * N])
    dvec_h, dvec_ap = din("dvec", [128, 2 * SL])
    wxT_h, wxT_ap = din("wxT", [2 * EH, 96])
    wdtT_h, wdtT_ap = din("wdtT", [64, 2 * EH])
    lnw_h, lnw_ap = din("lnw_b", [128, D])
    lnb_h, lnb_ap = din("lnb_b", [128, D])
    woutT_h, woutT_ap = din("woutT", [2 * E, D])
    fup_h, fup_ap = din("fup", [NE * D, F])
    fdn_h, fdn_ap = din("fdn", [NE * F, D])
    rw_h, rw_ap = din("rw", [D, 8])
    selm_h, selm_ap = din("selm", [16, N * 128])
    ident_h, ident_ap = din("ident", [128, 128])
    meta_h, meta_ap = din("meta", [1, 16], I32)
    flagval_h, flagval_ap = din("flagval", [1, NFLAG], I32)

    out_o_h = nc.dram_tensor("out_o", [Q, D], F32, kind="ExternalOutput")
    out_o = out_o_h.ap()
    DBGF = cfg.get("DBGF", False)
    if DBGF:
        dbgff_h = nc.dram_tensor("dbgff", [Q, D], F32, kind="ExternalOutput")
        dbgg_h = nc.dram_tensor("dbgg", [Q, 8], F32, kind="ExternalOutput")
        dbgh_h = nc.dram_tensor("dbgh", [F, ST], BF16, kind="ExternalOutput")

    xn2fm_h = nc.dram_tensor("xn2fm", [D, T], BF16)
    ffnout_h = nc.dram_tensor("ffnout", [Q, D], F32)
    xdsl_h = nc.dram_tensor("xdsl", [2 * NCH * 96 + 128, TC], F32, addr_space="Shared")
    ysl_h = nc.dram_tensor("ysl", [4 * EH + 128, H], BF16, addr_space="Shared")
    flags_h = nc.dram_tensor("flags", [NFLAG * 2 + 128, 16], I32, addr_space="Shared")
    xdidx_h, xdidx_ap = din("xdidx", [128, NCH * 6], U32)
    yslidx_h, yslidx_ap = din("yslidx", [128, 2 * SL], U32)
    yrdidx_h, yrdidx_ap = din("yrdidx", [128, 4 * SL * NGO], U32)
    flgidx_h, flgidx_ap = din("flgidx", [128, NFLAG], U32)
    flgsrc_h, flgsrc_ap = din("flgsrc", [2, NFLAG * 16], I32)

    for s in ("dS", "dO", "dG", "sPE", "sACT", "sDVE", "sPOOL", "sBAR", "sJOIN"):
        K.sem(s)

    meta_t = K.sb("meta_t", [1, 16], I32)
    flagval_t = K.sb("flagval_t", [1, NFLAG], I32)
    flagrd = K.sb("flagrd", [1, 2], I32)
    xdidx_t = K.sb("xdidx_t", [128, NCH * 6], U32)
    yslidx_t = K.sb("yslidx_t", [128, 2 * SL], U32)
    yrdidx_t = K.sb("yrdidx_t", [128, 4 * SL * NGO], U32)
    flgidx_t = K.sb("flgidx_t", [128, NFLAG], U32)
    flgsrc_t = K.sb("flgsrc_t", [2, NFLAG * 16], I32)
    K._bar_tiles = (flags_h, flagval_t, flagrd, flgidx_t, flgsrc_t)
    K._bar_slot = 0
    K._nflag = NFLAG
    K._ds_reg = None

    identb_t = K.sb("identb_t", [128, 128], BF16)
    identf_t = K.sb("identf_t", [128, 128])
    lnw_t = K.sb("lnw_t", [128, D])
    lnb_t = K.sb("lnb_t", [128, D])
    selm_t = K.sb("selm_t", [16, N * 128], BF16)
    bconv_t = K.sb("bconv_t", [128, 2 * SL])
    bdt_t = K.sb("bdt_t", [128, 2 * SL])
    apos_t = K.sb("apos_t", [128, 2 * SL * N])
    dvec_t = K.sb("dvec_t", [128, 2 * SL])
    wx_st = K.sb("wx_st", [128, 2 * SL * 96], BF16)
    wdt_st = K.sb("wdt_st", [64, 2 * EH])
    stat_c = K.sb("stat_c", [128, NGA])
    stat_ob = K.sb("stat_ob", [128, NGA])
    stato_r = K.sb("stato_r", [128, NGO])
    stato_b = K.sb("stato_b", [128, NGO])
    ones_t = K.sb("ones_t", [128, 1])
    eps_t = K.sb("eps_t", [128, 1])
    carr = K.sb("carr", [128, 2 * SL * N])

    t_meta = K.dma("sync", meta_t[:], meta_ap)
    K.dma("sync", flagval_t[:], flagval_ap)
    t_id = K.dma("gpsimd", identb_t[:], ident_ap, "dG")
    K.dma("sync", identf_t[:], ident_ap)
    K.dma("sync", lnw_t[:], lnw_ap)
    t_lnb = K.dma("sync", lnb_t[:], lnb_ap)
    K.dma("gpsimd", selm_t[:], selm_ap, "dG")
    K.dma("sync", bconv_t[:], bconv_ap)
    K.dma("sync", bdt_t[:], bdt_ap)
    t_al = K.dma("sync", apos_t[:], alog_ap)
    K.dma("sync", dvec_t[:], dvec_ap)
    t_wx = K.dma("gpsimd", wx_st[:].rearrange("p (s c) -> p s c", s=2 * SL),
                 bass.AP(wxT_h, 0, [[96, 128], [128 * 96, 2 * SL], [1, 96]]), "dG")
    t_wdt = K.dma("sync", wdt_st[:], wdtT_ap)
    K.dma("sync", xdidx_t[:], xdidx_ap)
    K.dma("sync", yslidx_t[:], yslidx_ap)
    K.dma("sync", yrdidx_t[:], yrdidx_ap)
    tfi = K.dma("sync", flgidx_t[:], flgidx_ap)
    tfs = K.dma("sync", flgsrc_t[:], flgsrc_ap)
    K.waitv("gpsimd", tfs)
    K.waitv("gpsimd", tfi)

    zt = K.sb("zt", [128, 2048])
    tz = K.dve(lambda E: E.memset(zt[:], 0.0))
    K.waitv("gpsimd", tz)
    ZCH = 128 * 2048
    xd_elems = (2 * NCH * 96 + 128) * TC
    for o in range(0, xd_elems, ZCH):
        n = min(ZCH, xd_elems - o)
        nr, ncol = (128, n // 128) if n % 128 == 0 and n >= 128 else (1, n)
        K.dma("gpsimd", bass.AP(xdsl_h, o, [[ncol, nr], [1, ncol]]),
              zt[0:nr, 0:ncol], "dG")
    ys_elems = (4 * EH + 128) * H
    ztb = zt[:].bitcast(BF16)
    for o in range(0, ys_elems, 2 * ZCH):
        n = min(2 * ZCH, ys_elems - o)
        nr, ncol = (128, n // 128) if n % 128 == 0 and n >= 128 else (1, n)
        K.dma("gpsimd", bass.AP(ysl_h, o, [[ncol, nr], [1, ncol]]),
              ztb[0:nr, 0:ncol], "dG")
    K.barrier([K.tok("dG")])
    K.join()
    K.free(zt)
    K.waitv("scalar", t_al)
    K.act(lambda E: E.activation(apos_t[:], apos_t[:], AF.Exp))
    K.waitv("vector", t_meta)
    K.dve(lambda E: E.memset(carr[:], 0.0))
    K.dve(lambda E: E.memset(ones_t[:], 1.0))
    tmemc = K.dve(lambda E: E.memset(eps_t[:], float(EPS)))
    K.waitv("scalar", tmemc)

    g_x = K.sb("g_x", [128, D])
    g_t1 = K.sb("g_t1", [128, D])
    g_t2 = K.sb("g_t2", [128, D], BF16)
    g_a1 = K.sb("g_a1", [128, NGA])
    g_a2 = K.sb("g_a2", [128, NGA])
    trbuf = K.sb("trbuf", [128, DC * GB * 128], BF16)
    PT32 = K.psum("PT32", [128, 128])
    PTRB = K.psum("PTRB", [128, 128], BF16)

    def ln_stats(x_ap, ngroups, acc1, acc2, prev=None):
        last = prev
        for g in range(ngroups):
            K.waitv("sync", last)
            t_x = K.dma("sync", g_x[:], x_ap[g * 128:(g + 1) * 128, :])
            K.waitv("scalar", t_x)
            K.act(lambda E, g=g: E.activation(g_t1[:], g_x[:], AF.Identity,
                                              accum_out=acc1[:, g:g + 1]))
            last = K.act(lambda E, g=g: E.activation(g_t1[:], g_x[:], AF.Square,
                                                     accum_out=acc2[:, g:g + 1]))
        return last

    def finalize_stats(acc1, acc2, out_scale, out_bias, ngroups, combined):
        a1 = acc1[:, 0:ngroups]
        a2 = acc2[:, 0:ngroups]
        sc = out_scale[:, 0:ngroups]
        ob = out_bias[:, 0:ngroups]
        K.dve(lambda E: E.tensor_scalar_mul(a1, a1, 1.0 / D))
        K.dve(lambda E: E.tensor_scalar_mul(a2, a2, 1.0 / D))
        K.dve(lambda E: E.tensor_tensor(sc, a1, a1, ALU.mult))
        tv = K.dve(lambda E: E.tensor_tensor(a2, a2, sc, ALU.subtract))
        K.waitv("scalar", tv)
        K.act(lambda E: E.activation(sc, a2, AF.Ln, bias=eps_t[:]))
        tr = K.act(lambda E: E.activation(sc, sc, AF.Exp, scale=-0.5))
        if combined:
            K.waitv("vector", tr)
            K.dve(lambda E: E.tensor_tensor(ob, sc, sc, ALU.mult))
            tv2 = K.dve(lambda E: E.tensor_tensor(ob, a2, ob, ALU.mult))
            K.waitv("scalar", tv2)
            K.act(lambda E: E.activation(ob, ob, AF.Ln, bias=eps_t[:]))
            tr2 = K.act(lambda E: E.activation(ob, ob, AF.Exp, scale=-0.5))
            K.waitv("vector", tr2)
            K.dve(lambda E: E.tensor_tensor(sc, sc, ob, ALU.mult))
        else:
            K.waitv("vector", tr)
        K.dve(lambda E: E.tensor_tensor(ob, a1, sc, ALU.mult))
        return K.dve(lambda E: E.tensor_scalar_mul(ob, ob, -1.0))

    tq = ln_stats(x_b, NGA, g_a1, g_a2)
    K.waitv("vector", tq)
    tst = finalize_stats(g_a1, g_a2, stat_c, stat_ob, NGA, True)
    K.waitv("scalar", tst)
    tq2 = ln_stats(x_o, NGO, g_a1, g_a2, prev=tq)
    K.waitv("vector", tq2)
    tst2 = finalize_stats(g_a1, g_a2, stato_r, stato_b, NGO, False)

    K.waitv("scalar", tst2)
    K.waitv("tensor", t_id)
    nblk = (NGA + GB - 1) // GB
    tr_toks = [None, None]
    wr_tok = None
    prev_x2 = tst2
    prev_ttr = None
    for blk in range(nblk):
        gs = list(range(blk * GB, min((blk + 1) * GB, NGA)))
        K.waitv("scalar", wr_tok)
        for gi, g in enumerate(gs):
            K.waitv("sync", prev_x2)
            t_x = K.dma("sync", g_x[:], x_b[g * 128:(g + 1) * 128, :])
            K.waitv("scalar", t_x)
            tc_ = K.act(lambda E, g=g: E.activation(g_t1[:], g_x[:], AF.Identity,
                                                    scale=stat_c[:, g:g + 1],
                                                    bias=stat_ob[:, g:g + 1]))
            K.waitv("vector", tc_)
            K.waitv("vector", prev_ttr)
            K.dve(lambda E: E.tensor_tensor(g_t1[:], g_t1[:], lnw_t[:], ALU.mult))
            tx2 = K.dve(lambda E: E.tensor_tensor(g_t2[:], g_t1[:], lnb_t[:], ALU.add))
            prev_x2 = tx2
            K.waitv("tensor", tx2)
            for dc in range(DC):
                K.waitv("tensor", tr_toks[1])
                ttr = K.pe(lambda E, dc=dc: E.transpose(
                    PTRB[:], g_t2[:, dc * 128:(dc + 1) * 128], identb_t[:]))
                prev_ttr = ttr
                K.waitv("scalar", ttr)
                tev = K.act(lambda E, dc=dc, gi=gi: E.copy(
                    trbuf[:, (dc * GB + gi) * 128:(dc * GB + gi) * 128 + 128], PTRB[:]))
                tr_toks = [tr_toks[1], ("sACT", tev[1])]
        t0 = blk * GB * 128
        ncols = len(gs) * 128
        K.waitv("sync", tr_toks[0])
        K.waitv("sync", tr_toks[1])
        wr_tok = K.dma("sync", bass.AP(xn2fm_h, t0, [[T, 128], [128 * T, DC], [1, ncols]]),
                       trbuf[:, 0:DC * ncols].rearrange("p (dc c) -> p dc c", dc=DC))
    K.join()
    for t in reversed((g_x, g_t1, g_t2, g_a1, g_a2, trbuf)):
        K.free(t)
    K.free(PTRB)

    f_xt = K.sb("f_xt", [128, D])
    f_xn = K.sb("f_xn", [128, D])
    f_xnb = K.sb("f_xnb", [128, DC * ST], BF16)
    f_xnf = K.sb("f_xnf", [128, DC * 128])
    f_rw = K.sb("f_rw", [128, DC * 8])
    f_lg = K.sb("f_lg", [128, 8])
    f_g1 = K.sb("f_g1", [128, 8])
    f_g2 = K.sb("f_g2", [128, 8])
    f_m = K.sb("f_m", [128, 1])
    f_gate = K.sb("f_gate", [128, STG * 8])
    f_wu = [K.sb(f"f_wu{i}", [128, DC * 128], BF16) for i in range(2)]
    f_wdh = K.sb("f_wdh", [128, (FC // 2) * D], BF16)
    f_h = K.sb("f_h", [128, FC * ST], BF16)
    f_acc = K.sb("f_acc", [128, STG * D])
    PU = [K.psum(f"PU{i}", [128, MT]) for i in range(2)]
    PDW = [K.psum(f"PDW{i}", [128, D]) for i in range(2)]
    PRT = K.psum("PRT", [128, 8])

    t_rw = K.dma("sync", f_rw[:].rearrange("p (dc c) -> p dc c", dc=DC),
                 bass.AP(rw_h, 0, [[8, 128], [128 * 8, DC], [1, 8]]))
    K.waitv("tensor", t_rw)

    FH = FC // 2
    tff = None
    for st in range(NST):
        K.waitv("vector", tff)
        tacc0 = K.dve(lambda E: E.memset(f_acc[:], 0.0))
        tr_toks = [None, None]
        tgate = None
        prev_xn = None
        for tg in range(STG):
            g = st * STG + tg
            K.waitv("sync", prev_xn)
            t_x = K.dma("sync", f_xt[:], x_o[g * 128:(g + 1) * 128, :])
            K.waitv("scalar", t_x)
            K.waitv("scalar", tr_toks[0])
            K.waitv("scalar", tr_toks[1])
            txn = K.act(lambda E, g=g: E.activation(f_xn[:], f_xt[:], AF.Identity,
                                                    scale=stato_r[:, g:g + 1],
                                                    bias=stato_b[:, g:g + 1]))
            prev_xn = txn
            K.waitv("tensor", txn)
            for dc in range(DC):
                K.waitv("tensor", tr_toks[1])
                ttr = K.pe(lambda E, dc=dc: E.transpose(
                    PT32[:], f_xn[:, dc * 128:(dc + 1) * 128], identf_t[:]))
                K.waitv("scalar", ttr)
                K.act(lambda E, dc=dc: E.copy(
                    f_xnf[:, dc * 128:(dc + 1) * 128], PT32[:]))
                tev = K.act(lambda E, dc=dc, tg=tg: E.copy(
                    f_xnb[:, dc * ST + tg * 128: dc * ST + tg * 128 + 128], PT32[:]))
                tr_toks = [tr_toks[1], ("sACT", tev[1])]
            K.waitv("tensor", tr_toks[1])
            for dc in range(DC):
                K.pe(lambda E, dc=dc: E.matmul(
                    PRT[:], f_xnf[:, dc * 128:(dc + 1) * 128],
                    f_rw[:, dc * 8:(dc + 1) * 8],
                    start=(dc == 0), stop=(dc == DC - 1)))
            trt = K.tok("sPE")
            K.waitv("vector", trt)
            K.dve(lambda E: E.tensor_copy(f_lg[:], PRT[:]))
            K.dve(lambda E: E.tensor_reduce(f_m[:], f_lg[:], AX.X, ALU.max))
            tnm = K.dve(lambda E: E.tensor_scalar_mul(f_m[:], f_m[:], -1.0))
            K.waitv("scalar", tnm)
            te = K.act(lambda E: E.activation(f_g1[:], f_lg[:], AF.Exp, bias=f_m[:]))
            K.waitv("vector", te)
            K.dve(lambda E: E.tensor_reduce(f_m[:], f_g1[:], AX.X, ALU.add))
            K.dve(lambda E: E.reciprocal(f_m[:], f_m[:]))
            K.dve(lambda E: E.tensor_scalar(f_g1[:], f_g1[:], f_m[:], None, ALU.mult))
            K.dve(lambda E: E.tensor_reduce(f_m[:], f_g1[:], AX.X, ALU.max))
            K.dve(lambda E: E.tensor_scalar(f_g2[:], f_g1[:], f_m[:], None, ALU.is_equal))
            K.dve(lambda E: E.scalar_tensor_tensor(f_g2[:], f_g2[:], -1e30, f_g1[:],
                                                   ALU.mult, ALU.add))
            K.dve(lambda E: E.tensor_reduce(f_m[:], f_g2[:], AX.X, ALU.max))
            K.dve(lambda E: E.tensor_scalar(f_g2[:], f_g1[:], f_m[:], None, ALU.is_ge))
            tgate = K.dve(lambda E, tg=tg: E.tensor_tensor(
                f_gate[:, tg * 8:(tg + 1) * 8], f_g1[:], f_g2[:], ALU.mult))

        if DBGF:
            K.waitv("sync", tgate)
            for tg in range(STG):
                K.dma("sync", bass.AP(dbgg_h, (st * ST + tg * 128) * 8, [[8, 128], [1, 8]]),
                      f_gate[:, tg * 8:(tg + 1) * 8])
        wu_free = [None, None]
        pu_free = [None, None]
        tdn_last = None
        fwd_guard = None
        for e in range(NE):
            K.waitv("scalar", tdn_last)
            K.waitv("scalar", tgate)
            for fc in range(FC):
                buf = fc % 2
                K.waitv("gpsimd", wu_free[buf])
                twu = K.dma("gpsimd", f_wu[buf][:].rearrange("p (dc c) -> p dc c", dc=DC),
                            bass.AP(fup_h, (e * D) * F + fc * 128,
                                    [[F, 128], [128 * F, DC], [1, 128]]), "dG")
                K.waitv("tensor", twu)
                for mt in range(NMT):
                    pu = PU[mt % 2]
                    K.waitv("tensor", pu_free[mt % 2])
                    for dc in range(DC):
                        K.pe(lambda E, dc=dc, mt=mt, buf=buf, pu=pu: E.matmul(
                            pu[:], f_wu[buf][:, dc * 128:(dc + 1) * 128],
                            f_xnb[:, dc * ST + mt * MT: dc * ST + mt * MT + MT],
                            start=(dc == 0), stop=(dc == DC - 1)))
                    tmm = K.tok("sPE")
                    K.waitv("scalar", tmm)
                    tgl = K.act(lambda E, fc=fc, mt=mt, pu=pu: E.activation(
                        f_h[:, fc * ST + mt * MT: fc * ST + mt * MT + MT],
                        pu[:], AF.Gelu))
                    pu_free[mt % 2] = ("sACT", tgl[1])
                wu_free[buf] = ("sPE", tmm[1])
            if DBGF and e == 0 and st == 0:
                K.waitv("sync", K.tok("sACT"))
                for fc in range(FC):
                    K.dma("sync", bass.AP(dbgh_h, fc * 128 * ST, [[ST, 128], [1, ST]]),
                          f_h[:, fc * ST:(fc + 1) * ST])
            tgl_last = K.tok("sACT")
            for fh in range(2):
                K.waitv("gpsimd", fwd_guard)
                twd = K.dma("gpsimd", f_wdh[:].rearrange("p (fc c) -> p fc c", fc=FH),
                            bass.AP(fdn_h, (e * F + fh * FH * 128) * D,
                                    [[D, 128], [128 * D, FH], [1, D]]), "dG")
                K.waitv("tensor", twd)
                K.waitv("tensor", tgl_last)
                if e > 0 or fh > 0:
                    K.waitv("tensor", tga_last)
                for tg in range(STG):
                    pd = PDW[tg % 2]
                    if tg >= 2:
                        K.waitv("tensor", tga_toks[tg - 2])
                    for fc in range(FH):
                        fcg = fh * FH + fc
                        for dm in range(D // 512 if D >= 512 else 1):
                            MD = min(D, 512)
                            K.pe(lambda E, fcg=fcg, fc=fc, tg=tg, dm=dm, pd=pd, MD=MD: E.matmul(
                                pd[:, dm * MD: dm * MD + MD],
                                f_h[:, fcg * ST + tg * 128: fcg * ST + tg * 128 + 128],
                                f_wdh[:, fc * D + dm * MD: fc * D + dm * MD + MD],
                                start=(fc == 0), stop=(fc == FH - 1)))
                    tdn = K.tok("sPE")
                    K.waitv("vector", tdn)
                    if e == 0:
                        tga = K.dve(lambda E, tg=tg, pd=pd: E.scalar_tensor_tensor(
                            f_acc[:, tg * D: tg * D + D], pd[:], ones_t[:],
                            f_acc[:, tg * D: tg * D + D], ALU.mult, ALU.add))
                    else:
                        tga = K.dve(lambda E, tg=tg, e=e, pd=pd: E.scalar_tensor_tensor(
                            f_acc[:, tg * D: tg * D + D], pd[:],
                            f_gate[:, tg * 8 + e - 1: tg * 8 + e],
                            f_acc[:, tg * D: tg * D + D], ALU.mult, ALU.add))
                    if tg == 0:
                        tga_toks = {}
                    tga_toks[tg] = tga
                tga_last = tga
                tdn_last = tdn
                fwd_guard = tdn
        K.waitv("sync", tga_last)
        tff = K.dma("sync", bass.AP(ffnout_h, st * ST * D, [[D, 128], [128 * D, STG], [1, D]]),
                    f_acc[:].rearrange("p (tg d) -> p tg d", tg=STG))
    K.join()
    for t in reversed((f_xt, f_xn, f_xnb, f_xnf, f_rw, f_lg, f_g1, f_g2, f_m, f_gate,
                       f_wu[0], f_wu[1], f_wdh, f_h, f_acc)):
        K.free(t)
    for t in reversed((PU[0], PU[1], PDW[0], PDW[1], PRT)):
        K.free(t)

    winT_t = K.sb("winT_t", [128, DC * 2 * EH], BF16)
    convd_t = K.sb("convd_t", [128, 4 * SL * 128], BF16)
    xn2c = K.sb("xn2c", [128, DC * TC], BF16)
    xp_t = K.sb("xp_t", [128, SL * (TC + 4)], BF16)
    zs_t = K.sb("zs_t", [128, SL * TC], BF16)
    xc_t = K.sb("xc_t", [128, SL * TC], BF16)
    ndt_t = K.sb("ndt_t", [128, SL * TC], BF16)
    w_t = K.sb("w_t", [128, SL * TC], BF16)
    sp1 = K.sb("sp1", [128, TC])
    sp2 = K.sb("sp2", [128, TC])
    dtrmy = K.sb("dtrmy", [64, TC])
    dtrpr = K.sb("dtrpr", [64, TC])
    dtrs = K.sb("dtrs", [64, TC])
    bmy = K.sb("bmy", [16, TC])
    bpr = K.sb("bpr", [16, TC])
    cmy = K.sb("cmy", [16, TC])
    cpr = K.sb("cpr", [16, TC])
    bsb = K.sb("bsb", [16, TC], BF16)
    csb = K.sb("csb", [16, TC], BF16)
    brep = [K.sb(f"brep{i}", [128, TC], BF16) for i in range(2)]
    crep = [K.sb(f"crep{i}", [128, TC], BF16) for i in range(2)]
    a_t = [K.sb(f"a_t{i}", [128, TC]) for i in range(2)]
    b_t = [K.sb(f"b_t{i}", [128, TC], BF16) for i in range(2)]
    h_t = K.sb("h_t", [128, TC], BF16)
    tmp_t = K.sb("tmp_t", [128, TC], BF16)
    yacc = K.sb("yacc", [128, SL * TC])
    ybf = [K.sb(f"ybf{i}", [128, TC], BF16) for i in range(2)]
    PZ = [K.psum(f"PZ{i}", [128, TC]) for i in range(2)]
    PCV = K.psum("PCV", [128, TC])
    PXD = K.psum("PXD", [128, TC])
    PXB = K.psum("PXB", [128, TC])
    PREP = K.psum("PREP", [128, TC])

    a_free = [None, None]
    b_free = [None, None]
    ybf_free = [None, None]
    carry_tok = {}
    ysl_toks = []

    chunks = [("B", k) for k in range(NCHB - 1, -1, -1)] + [("A", k) for k in range(NCHA)]
    cur_dir = None

    for ci, (dr, k) in enumerate(chunks):
        d = 1 if dr == "B" else 0
        t0 = k * TC
        owned = (dr == "A") or (t0 < H)
        first = (dr == "B" and k == NCHB - 1) or (dr == "A" and k == 0)
        slot = ci
        K.join()

        if cur_dir != d:
            twl = K.dma("gpsimd", winT_t[:].rearrange("p (dc c) -> p dc c", dc=DC),
                        bass.AP(winT_h, d * 2 * EH,
                                [[4 * EH, 128], [128 * 4 * EH, DC], [1, 2 * EH]]), "dG")
            tcl = K.dma("gpsimd", convd_t[:],
                        convd[:, d * 4 * SL * 128:(d + 1) * 4 * SL * 128], "dG")
            cur_dir = d
            K.waitv("tensor", twl)
            K.waitv("tensor", tcl)

        txn = K.dma("sync", xn2c[:].rearrange("p (dc c) -> p dc c", dc=DC),
                    bass.AP(xn2fm_h, t0, [[T, 128], [128 * T, DC], [1, TC]]))
        if first:
            tst_ = K.dve(lambda E: E.memset(xp_t[:], 0.0))
        else:
            for i in range(SL):
                if dr == "A":
                    tst_ = K.dve(lambda E, i=i: E.tensor_copy(
                        xp_t[:, i * (TC + 4): i * (TC + 4) + 3],
                        xp_t[:, i * (TC + 4) + TC: i * (TC + 4) + TC + 3]))
                else:
                    tst_ = K.dve(lambda E, i=i: E.tensor_copy(
                        xp_t[:, i * (TC + 4) + TC: i * (TC + 4) + TC + 3],
                        xp_t[:, i * (TC + 4): i * (TC + 4) + 3]))
        K.waitv("scalar", tst_)
        K.waitv("tensor", txn)

        off0 = 3 if dr == "A" else 0
        pz_ev = [None, None]
        last_ev = None
        nxz = 2 * SL if owned else SL
        for j in range(nxz):
            i = j % SL
            isz = j >= SL
            pz = PZ[j % 2]
            K.waitv("tensor", pz_ev[j % 2])
            for dc in range(DC):
                col = dc * 2 * EH + (EH if isz else 0) + i * 128
                K.pe(lambda E, col=col, dc=dc, pz=pz: E.matmul(
                    pz[:], winT_t[:, col: col + 128],
                    xn2c[:, dc * TC:(dc + 1) * TC],
                    start=(dc == 0), stop=(dc == DC - 1)))
            tm = K.tok("sPE")
            K.waitv("scalar", tm)
            if not isz:
                last_ev = K.act(lambda E, i=i, pz=pz, off0=off0: E.copy(
                    xp_t[:, i * (TC + 4) + off0: i * (TC + 4) + off0 + TC], pz[:]))
            else:
                last_ev = K.act(lambda E, i=i, pz=pz: E.activation(
                    zs_t[:, i * TC:(i + 1) * TC], pz[:], AF.Silu))
            pz_ev[j % 2] = last_ev

        K.waitv("tensor", last_ev)
        txc = None
        for i in range(SL):
            if txc is not None:
                K.waitv("tensor", txc)
            for tap in range(4):
                off = tap if dr == "A" else (3 - tap)
                K.pe(lambda E, i=i, tap=tap, off=off: E.matmul(
                    PCV[:], convd_t[:, (tap * SL + i) * 128:(tap * SL + i) * 128 + 128],
                    xp_t[:, i * (TC + 4) + off: i * (TC + 4) + off + TC],
                    start=(tap == 0), stop=(tap == 3)))
            tcv = K.tok("sPE")
            K.waitv("scalar", tcv)
            txc = K.act(lambda E, i=i, d=d: E.activation(
                xc_t[:, i * TC:(i + 1) * TC], PCV[:], AF.Silu,
                bias=bconv_t[:, d * SL + i: d * SL + i + 1]))

        K.waitv("tensor", txc)
        for grp, (pj, c0, cw) in enumerate(((PXD, 0, 64), (PXB, 64, 16), (PCV, 80, 16))):
            for i in range(SL):
                K.pe(lambda E, i=i, pj=pj, c0=c0, cw=cw, d=d: E.matmul(
                    pj[0:cw, :], wx_st[:, (d * SL + i) * 96 + c0:(d * SL + i) * 96 + c0 + cw],
                    xc_t[:, i * TC:(i + 1) * TC],
                    start=(i == 0), stop=(i == SL - 1)))
        txd = K.tok("sPE")
        K.waitv("scalar", txd)
        K.act(lambda E: E.copy(dtrmy[:], PXD[0:64, :]))
        K.act(lambda E: E.copy(bmy[:], PXB[0:16, :]))
        tevc = K.act(lambda E: E.copy(cmy[:], PCV[0:16, :]))
        K.waitv("gpsimd", tevc)
        xbufs = (dtrmy, bmy, cmy)
        rbufs = (dtrpr, bpr, cpr)
        v3 = None
        for kk in range(3):
            nr = (64, 16, 16)[kk]
            v3 = K.idma(lambda E, s, kk=kk, slot=slot, nr=nr: E.indirect_dma_start(
                out=xdsl_h.ap()[:, :],
                out_offset=bass.IndirectOffsetOnAxis(
                    xdidx_t[0:nr, slot * 6 + kk: slot * 6 + kk + 1], 0),
                in_=xbufs[kk][:], in_offset=None).then_inc(s, 16))
        tbar = K.barrier([v3])
        v6 = None
        for kk in range(3):
            nr = (64, 16, 16)[kk]
            v6 = K.idma(lambda E, s, kk=kk, slot=slot, nr=nr: E.indirect_dma_start(
                out=rbufs[kk][:], out_offset=None,
                in_=xdsl_h.ap()[:, :],
                in_offset=bass.IndirectOffsetOnAxis(
                    xdidx_t[0:nr, slot * 6 + 3 + kk: slot * 6 + 3 + kk + 1], 0)).then_inc(s, 16))
        K.waitv("vector", v6)
        K.dve(lambda E: E.tensor_tensor(dtrs[:], dtrmy[:], dtrpr[:], ALU.add))
        tbs = K.dve(lambda E: E.tensor_tensor(bsb[:], bmy[:], bpr[:], ALU.add))
        tcs = K.dve(lambda E: E.tensor_tensor(csb[:], cmy[:], cpr[:], ALU.add))

        K.waitv("tensor", tcs)
        prev_sp = None
        prev_nd = None
        for i in range(SL):
            K.waitv("tensor", prev_sp)
            K.pe(lambda E, i=i, d=d: E.matmul(
                PXD[:], wdt_st[:, d * EH + i * 128: d * EH + i * 128 + 128],
                dtrs[:], start=True, stop=True))
            tdt = K.tok("sPE")
            K.waitv("scalar", tdt)
            K.waitv("scalar", prev_nd)
            K.act(lambda E, i=i, d=d: E.activation(sp1[:], PXD[:], AF.Abs,
                                                   bias=bdt_t[:, d * SL + i: d * SL + i + 1]))
            K.act(lambda E: E.activation(sp1[:], sp1[:], AF.Exp, scale=-1.0))
            K.act(lambda E: E.activation(sp1[:], sp1[:], AF.Ln, bias=1.0))
            tsp = K.act(lambda E, i=i, d=d: E.activation(sp2[:], PXD[:], AF.Relu,
                                                         bias=bdt_t[:, d * SL + i: d * SL + i + 1]))
            prev_sp = tsp
            K.waitv("vector", tsp)
            tnd = K.dve(lambda E, i=i: E.scalar_tensor_tensor(
                ndt_t[:, i * TC:(i + 1) * TC], sp1[:], -1.0, sp2[:], ALU.mult, ALU.subtract))
            tw = K.dve(lambda E, i=i: E.tensor_tensor(
                w_t[:, i * TC:(i + 1) * TC], ndt_t[:, i * TC:(i + 1) * TC],
                xc_t[:, i * TC:(i + 1) * TC], ALU.mult))
            prev_nd = tw
        tndt = K.tok("sDVE")

        K.waitv("scalar", tndt)
        K.waitv("gpsimd", tndt)
        if owned:
            tyz = K.dve(lambda E: E.memset(yacc[:], 0.0))
        rep_prev = [None, None]
        prep_guard = None
        for n in range(N):
            p = n % 2
            K.waitv("tensor", rep_prev[p])
            K.waitv("tensor", prep_guard)
            trb = K.pe(lambda E, n=n: E.matmul(
                PREP[:], selm_t[:, n * 128:(n + 1) * 128], bsb[:], start=True, stop=True))
            K.waitv("scalar", trb)
            tbe = K.act(lambda E, p=p: E.copy(brep[p][:], PREP[:]))
            if owned:
                K.waitv("tensor", tbe)
                trc = K.pe(lambda E, n=n: E.matmul(
                    PREP[:], selm_t[:, n * 128:(n + 1) * 128], csb[:], start=True, stop=True))
                K.waitv("scalar", trc)
                tce = K.act(lambda E, p=p: E.copy(crep[p][:], PREP[:]))
                prep_guard = tce
            else:
                prep_guard = tbe
            b_last = None
            y_last = None
            for i in range(SL):
                ap_ = i % 2
                idx = d * SL * N + i * N + n
                K.waitv("scalar", a_free[ap_])
                ta = K.act(lambda E, i=i, n=n, d=d, ap_=ap_: E.activation(
                    a_t[ap_][:], ndt_t[:, i * TC:(i + 1) * TC], AF.Exp,
                    scale=apos_t[:, d * SL * N + i * N + n: d * SL * N + i * N + n + 1]))
                a_free[ap_] = None
                K.waitv("gpsimd", b_free[ap_])
                K.waitv("gpsimd", tbe)
                tb = K.pool(lambda E, i=i, ap_=ap_, p=p: E.tensor_tensor(
                    b_t[ap_][:], w_t[:, i * TC:(i + 1) * TC], brep[p][:], ALU.mult))
                b_free[ap_] = None
                b_last = tb
                K.waitv("vector", ta)
                K.waitv("vector", tb)
                K.waitv("vector", carry_tok.get(idx))
                if dr == "A":
                    tts = K.dve(lambda E, ap_=ap_, idx=idx: E.tensor_tensor_scan(
                        h_t[:], a_t[ap_][:], b_t[ap_][:], carr[:, idx:idx + 1],
                        ALU.mult, ALU.add))
                    lastcol = TC - 1
                else:
                    tts = K.dve(lambda E, ap_=ap_, idx=idx: E.tensor_tensor_scan(
                        h_t[:, ::-1], a_t[ap_][:, ::-1], b_t[ap_][:, ::-1],
                        carr[:, idx:idx + 1], ALU.mult, ALU.add))
                    lastcol = 0
                a_free[ap_] = ("sDVE", tts[1])
                b_free[ap_] = ("sDVE", tts[1])
                tcar = K.dve(lambda E, idx=idx, lastcol=lastcol: E.tensor_copy(
                    carr[:, idx:idx + 1], h_t[:, lastcol:lastcol + 1]))
                carry_tok[idx] = tcar
                if owned:
                    K.waitv("vector", tce)
                    K.dve(lambda E, p=p: E.tensor_tensor(tmp_t[:], h_t[:], crep[p][:], ALU.mult))
                    if n == 0:
                        ty = K.dve(lambda E, i=i: E.tensor_copy(
                            yacc[:, i * TC:(i + 1) * TC], tmp_t[:]))
                    else:
                        ty = K.dve(lambda E, i=i: E.tensor_tensor(
                            yacc[:, i * TC:(i + 1) * TC], yacc[:, i * TC:(i + 1) * TC],
                            tmp_t[:], ALU.add))
                    y_last = ty
            rep_prev[p] = ("sPOOL", b_last[1]) if not owned else ("sDVE", y_last[1])

        if owned:
            for i in range(SL):
                yb = ybf[i % 2]
                K.waitv("vector", ybf_free[i % 2])
                K.dve(lambda E, i=i, d=d: E.scalar_tensor_tensor(
                    sp1[:], xc_t[:, i * TC:(i + 1) * TC], dvec_t[:, d * SL + i: d * SL + i + 1],
                    yacc[:, i * TC:(i + 1) * TC], ALU.mult, ALU.subtract))
                tyf = K.dve(lambda E, i=i, yb=yb: E.tensor_tensor(
                    yb[:], sp1[:], zs_t[:, i * TC:(i + 1) * TC], ALU.mult))
                K.waitv("gpsimd", tyf)
                vy = K.idma(lambda E, s, yb=yb, i=i, d=d, t0=t0:
                            E.indirect_dma_start(
                                out=ysl_h.ap()[:, :],
                                out_offset=bass.IndirectOffsetOnAxis(
                                    yslidx_t[:, d * SL + i: d * SL + i + 1], 0),
                                in_=yb[:], in_offset=None,
                                element_offset=t0).then_inc(s, 16))
                ysl_toks.append(vy)
                ybf_free[i % 2] = vy

    tyb = K.barrier(ysl_toks[-1:])
    K.join()
    for t in reversed((winT_t, convd_t, xn2c, xp_t, zs_t, xc_t, ndt_t, w_t, sp1, sp2,
                       dtrmy, dtrpr, dtrs, bmy, bpr, cmy, cpr, bsb, csb,
                       brep[0], brep[1], crep[0], crep[1], a_t[0], a_t[1],
                       b_t[0], b_t[1], h_t, tmp_t, yacc, ybf[0], ybf[1])):
        K.free(t)
    for t in reversed((PZ[0], PZ[1], PCV, PXD, PXB, PREP)):
        K.free(t)

    p3_y = K.sb("p3_y", [128, 2 * EC * 128], BF16)
    p3_w = K.sb("p3_w", [128, 2 * EC * D], BF16)
    p3_x = K.sb("p3_x", [128, D])
    p3_n = K.sb("p3_n", [128, D])
    p3_f = K.sb("p3_f", [128, D])
    p3_o = K.sb("p3_o", [128, D])
    PP3 = [K.psum(f"PP3{i}", [128, D]) for i in range(2)]

    if DBGF:
        for qrow in range(0, Q, 128):
            K.dma("sync", bass.AP(dbgff_h, qrow * D, [[D, 128], [1, D]]),
                  bass.AP(ffnout_h, qrow * D, [[D, 128], [1, D]]))
    twout = K.dma("gpsimd", p3_w[:].rearrange("p (dc c) -> p dc c", dc=2 * EC),
                  bass.AP(woutT_h, 0, [[D, 128], [128 * D, 2 * EC], [1, D]]), "dG")
    K.waitv("tensor", twout)

    out_toks = []
    pp_free = [None, None]
    prev_fin = None
    prev_out = None
    prev_tpw = None
    for tg in range(NGO):
        ysl_flat = bass.AP(ysl_h, 0, [[128, (4 * EH + 128) * H // 128], [1, 128]])
        K.waitv("gpsimd", prev_tpw)
        tydma = None
        for ehs in range(2):
            for dd in range(2):
                for i in range(SL):
                    col = ((dd * 2 + ehs) * SL + i) * NGO + tg
                    dst = (dd * EC + ehs * SL + i) * 128
                    tydma = K.idma(lambda E, s, col=col, dst=dst:
                                   E.indirect_dma_start(
                                       out=p3_y[:, dst:dst + 128], out_offset=None,
                                       in_=ysl_flat,
                                       in_offset=bass.IndirectOffsetOnAxis(
                                           yrdidx_t[:, col:col + 1], 0)).then_inc(s, 16))
        pp = PP3[tg % 2]
        K.waitv("tensor", pp_free[tg % 2])
        K.waitv("tensor", tydma)
        nec = 2 * EC
        for ecg in range(nec):
            for dm in range(max(1, D // 512)):
                MD = min(D, 512)
                K.pe(lambda E, ecg=ecg, dm=dm, pp=pp, MD=MD: E.matmul(
                    pp[:, dm * MD: dm * MD + MD],
                    p3_y[:, ecg * 128:(ecg + 1) * 128],
                    p3_w[:, ecg * D + dm * MD: ecg * D + dm * MD + MD],
                    start=(ecg == 0), stop=(ecg == nec - 1)))
        tpw = K.tok("sPE")
        prev_tpw = tpw
        K.waitv("sync", prev_fin)
        t_x = K.dma("sync", p3_x[:], x_o[tg * 128:(tg + 1) * 128, :])
        t_f = K.dma("sync", p3_f[:], bass.AP(ffnout_h, tg * 128 * D, [[D, 128], [1, D]]))
        K.waitv("scalar", t_x)
        txn = K.act(lambda E, tg=tg: E.activation(p3_n[:], p3_x[:], AF.Identity,
                                                  scale=stato_r[:, tg:tg + 1],
                                                  bias=stato_b[:, tg:tg + 1]))
        K.waitv("vector", txn)
        K.waitv("vector", t_f)
        K.waitv("vector", tpw)
        K.waitv("vector", prev_out)
        K.dve(lambda E: E.scalar_tensor_tensor(p3_o[:], p3_n[:], 2.0, p3_x[:],
                                               ALU.mult, ALU.add))
        K.dve(lambda E: E.tensor_tensor(p3_o[:], p3_o[:], p3_f[:], ALU.add))
        tfin = K.dve(lambda E, pp=pp: E.scalar_tensor_tensor(
            p3_o[:], pp[:], 0.5, p3_o[:], ALU.mult, ALU.add))
        pp_free[tg % 2] = ("sDVE", tfin[1])
        prev_fin = tfin
        K.waitv("sync", tfin)
        tout = K.dma("sync", out_o[tg * 128:(tg + 1) * 128, :], p3_o[:])
        prev_out = tout
        out_toks.append(tout)

    K.waitv("sync", out_toks[-1])
    K.join()

    with K.stack:
        with nc.Block() as block:
            @block.sync
            def _(E):
                for f in K.ops["sync"]:
                    f(E)

            @block.tensor
            def _(E):
                for f in K.ops["tensor"]:
                    f(E)

            @block.vector
            def _(E):
                for f in K.ops["vector"]:
                    f(E)

            @block.scalar
            def _(E):
                for f in K.ops["scalar"]:
                    f(E)

            @block.gpsimd
            def _(E):
                for f in K.ops["gpsimd"]:
                    f(E)
    return nc



INPUT_NAMES = (
    "L0_attn_x", "L0_attn_ln_w", "L0_attn_ln_b",
    "L0_attn_f_W_in", "L0_attn_f_W_conv", "L0_attn_f_b_conv", "L0_attn_f_W_x",
    "L0_attn_f_W_dt", "L0_attn_f_b_dt", "L0_attn_f_A_log", "L0_attn_f_D", "L0_attn_f_W_out",
    "L0_attn_b_W_in", "L0_attn_b_W_conv", "L0_attn_b_b_conv", "L0_attn_b_W_x",
    "L0_attn_b_W_dt", "L0_attn_b_b_dt", "L0_attn_b_A_log", "L0_attn_b_D", "L0_attn_b_W_out",
    "L0_ffn_shared_up", "L0_ffn_shared_down", "L0_ffn_routed_up", "L0_ffn_routed_down",
    "L0_ffn_router_w")


def _prep_core_inputs(cfg, c, I, nonce, solo=False):
    T, D, EH, N, R, F, NE, TC = (cfg[k] for k in ("T", "D", "EH", "N", "R", "F", "NE", "TC"))
    SL = EH // 128
    H, Q = T // 2, T // 4
    E = 2 * EH
    NCH = H // TC + T // TC
    NFLAG = NCH + 8
    b = c // 4
    hi = (c % 4) >= 2
    eh = c % 2
    qq = eh
    if solo:
        b, hi, qq = 0, False, 0

    x = I["L0_attn_x"][b]
    xp_ = x[::-1] if hi else x
    xp_ = np.ascontiguousarray(xp_, np.float32)
    x_o = np.ascontiguousarray(xp_[qq * Q:(qq + 1) * Q])

    pfx = ("f_", "b_") if not hi else ("b_", "f_")
    g = lambda d, nm: I["L0_attn_" + pfx[d] + nm]

    Etrue = EH if solo else E
    ehw = 0 if solo else eh
    winT = np.zeros((D, 4 * EH), np.float32)
    for d in range(2):
        Win = g(d, "W_in")
        xp_rows = Win[ehw * EH:(ehw + 1) * EH]
        z_rows = Win[Etrue + ehw * EH: Etrue + (ehw + 1) * EH]
        winT[:, d * 2 * EH: d * 2 * EH + EH] = xp_rows.T
        winT[:, d * 2 * EH + EH: d * 2 * EH + 2 * EH] = z_rows.T

    convdm = np.zeros((2 * 4 * SL * 128, 128), np.float32)
    for d in range(2):
        Wc = g(d, "W_conv")[ehw * EH:(ehw + 1) * EH]
        for tap in range(4):
            for i in range(SL):
                r0 = ((d * 4 + tap) * SL + i) * 128
                convdm[r0:r0 + 128, :] = np.diag(Wc[i * 128:(i + 1) * 128, tap])
    convd = np.ascontiguousarray(
        convdm.reshape(2 * 4 * SL, 128, 128).transpose(1, 0, 2).reshape(128, -1))

    def col2(nm):
        out = np.zeros((128, 2 * SL), np.float32)
        for d in range(2):
            v = g(d, nm)[ehw * EH:(ehw + 1) * EH]
            out[:, d * SL:(d + 1) * SL] = v.reshape(SL, 128).T
        return out

    bconv = col2("b_conv")
    bdt = col2("b_dt")
    dvec = col2("D")
    alog = np.zeros((128, 2 * SL * N), np.float32)
    for d in range(2):
        A = g(d, "A_log")[ehw * EH:(ehw + 1) * EH]
        alog[:, d * SL * N:(d + 1) * SL * N] = \
            A.reshape(SL, 128, N).transpose(1, 0, 2).reshape(128, SL * N)

    wxT = np.zeros((2 * EH, 96), np.float32)
    for d in range(2):
        Wx = g(d, "W_x")
        half = slice(ehw * EH, (ehw + 1) * EH)
        wxT[d * EH:(d + 1) * EH, 0:R] = Wx[0:R, half].T
        wxT[d * EH:(d + 1) * EH, 64:64 + N] = Wx[R:R + N, half].T
        wxT[d * EH:(d + 1) * EH, 80:80 + N] = Wx[R + N:R + 2 * N, half].T
    if solo:
        wxT *= 0.5

    wdtT = np.zeros((64, 2 * EH), np.float32)
    for d in range(2):
        Wdt = g(d, "W_dt")[ehw * EH:(ehw + 1) * EH]
        wdtT[0:R, d * EH:(d + 1) * EH] = Wdt.T

    woutT = np.zeros((2 * E, D), np.float32)
    for d in range(2):
        if solo:
            woutT[d * E: d * E + EH] = g(d, "W_out").T[:EH]
        else:
            woutT[d * E:(d + 1) * E] = g(d, "W_out").T

    lnw_b = np.ascontiguousarray(np.broadcast_to(I["L0_attn_ln_w"], (128, D)), np.float32)
    lnb_b = np.ascontiguousarray(np.broadcast_to(I["L0_attn_ln_b"], (128, D)), np.float32)

    fup = np.concatenate([I["L0_ffn_shared_up"], I["L0_ffn_routed_up"]], 0)[:NE]
    fdn = np.concatenate([I["L0_ffn_shared_down"], I["L0_ffn_routed_down"]], 0)[:NE]
    fup = np.ascontiguousarray(fup.reshape(NE * D, F), np.float32)
    fdn = np.ascontiguousarray(fdn.reshape(NE * F, D), np.float32)
    rw = np.ascontiguousarray(I["L0_ffn_router_w"], np.float32)

    selm = np.zeros((16, N * 128), np.float32)
    for n in range(N):
        selm[n, n * 128:(n + 1) * 128] = 1.0
    ident = np.eye(128, dtype=np.float32)
    meta = np.zeros((1, 16), np.int32)
    meta[0, 0] = eh
    meta[0, 1] = 1 - eh
    meta[0, 2] = nonce
    meta[0, 3] = qq
    flagval = (nonce + 1 + np.arange(NFLAG)).astype(np.int32).reshape(1, NFLAG)

    p = np.arange(128, dtype=np.uint32)
    xdump = 2 * NCH * 96 + p
    xdidx = np.zeros((128, NCH * 6), np.uint32)
    offs = (0, 64, 80)
    nrs = (64, 16, 16)
    for s_ in range(NCH):
        for k in range(3):
            reh = eh if solo else (1 - eh)
            w = np.where(p < nrs[k], eh * NCH * 96 + s_ * 96 + offs[k] + p, xdump)
            r_ = np.where(p < nrs[k], reh * NCH * 96 + s_ * 96 + offs[k] + p, xdump)
            xdidx[:, s_ * 6 + k] = w
            xdidx[:, s_ * 6 + 3 + k] = r_
    EHl = EH
    yslidx = np.zeros((128, 2 * SL), np.uint32)
    for d in range(2):
        for i in range(SL):
            yslidx[:, d * SL + i] = eh * 2 * EHl + d * EHl + i * 128 + p
    NGO = Q // 128
    yrdidx = np.zeros((128, 4 * SL * NGO), np.uint32)
    for dd_ in range(2):
        for ehs in range(2):
            for i in range(SL):
                for tg in range(NGO):
                    col = ((dd_ * 2 + ehs) * SL + i) * NGO + tg
                    es = eh if solo else ehs
                    yrdidx[:, col] = ((es * 2 * EHl + dd_ * EHl + i * 128 + p) * (H // 128)
                                     + qq * (Q // 128) + tg)
    fdump = NFLAG * 2 + p
    flgidx = np.zeros((128, NFLAG), np.uint32)
    for s_ in range(NFLAG):
        flgidx[:, s_] = fdump
        flgidx[0, s_] = s_ * 2 + eh
        flgidx[1, s_] = (s_ * 2 + 1 - eh) if solo else (NFLAG * 2 + 64 + (s_ % 32))
    flgsrc = np.zeros((2, NFLAG * 16), np.int32)
    for s_ in range(NFLAG):
        flgsrc[0, s_ * 16:(s_ + 1) * 16] = nonce + 1 + s_
        if solo:
            flgsrc[1, s_ * 16:(s_ + 1) * 16] = nonce + 1 + s_

    return dict(x_b=xp_, x_o=x_o, winT=winT, convd=convd, bconv=bconv, bdt=bdt,
                alog=alog, dvec=dvec, wxT=wxT, wdtT=wdtT, lnw_b=lnw_b, lnb_b=lnb_b,
                woutT=woutT, fup=fup, fdn=fdn, rw=rw, selm=selm, ident=ident,
                meta=meta, flagval=flagval, xdidx=xdidx, yslidx=yslidx,
                yrdidx=yrdidx, flgidx=flgidx, flgsrc=flgsrc)


def assemble_output(cfg, results):
    T, D = cfg["T"], cfg["D"]
    H, Q = T // 2, T // 4
    out = np.zeros((2, T, D), np.float32)
    for c in range(8):
        b = c // 4
        hi = (c % 4) >= 2
        qq = c % 2
        y = results[c]["out_o"]
        if not hi:
            out[b, qq * Q:(qq + 1) * Q] = y
        else:
            out[b, T - (qq + 1) * Q: T - qq * Q] = y[::-1]
    return out


_BUILD_CACHE = {}


def kernel(**inputs):
    assert all(n in inputs for n in INPUT_NAMES), sorted(inputs)
    cfg = mk_cfg(full=True)
    key = "full"
    if key not in _BUILD_CACHE:
        _BUILD_CACHE[key] = build(cfg)
    nc = _BUILD_CACHE[key]
    nonce = int(np.random.randint(1, 2**30))
    in_maps = [_prep_core_inputs(cfg, c, inputs, nonce) for c in range(8)]
    res = run_bass_kernel_spmd(nc, in_maps, core_ids=list(range(8)))
    out = assemble_output(cfg, [res.results[c] for c in range(8)])
    return out.astype(np.float32)


# version 05
# speedup vs baseline: 1.8116x; 1.0409x over previous
import contextlib
import numpy as np
import ml_dtypes
import concourse.bass as bass
import concourse.mybir as mybir
from concourse.bass_utils import run_bass_kernel_spmd

F32 = mybir.dt.float32
BF16 = mybir.dt.bfloat16
I32 = mybir.dt.int32
U32 = mybir.dt.uint32
AF = mybir.ActivationFunctionType
ALU = mybir.AluOpType
AX = mybir.AxisListType
BF16NP = ml_dtypes.bfloat16

ENGINES = ("sync", "tensor", "vector", "scalar", "gpsimd")


def mk_cfg(full=True):
    if full:
        return dict(T=16384, D=1024, EH=1024, N=16, R=64, F=4096, NE=9, TC=512,
                    ST=1024, EPS=1e-5, POLL_OUT=4000, POLL_IN=64)
    return dict(T=512, D=256, EH=128, N=4, R=32, F=256, NE=3, TC=128,
                ST=128, EPS=1e-5, POLL_OUT=1000, POLL_IN=8)


def mk_cfg_mid():
    return dict(T=2048, D=256, EH=128, N=4, R=32, F=512, NE=3, TC=128,
                ST=256, EPS=1e-5, POLL_OUT=1000, POLL_IN=8)


class Builder:
    def __init__(self, cfg):
        self.cfg = cfg
        self.nc = bass.Bass("TRN2", target_bir_lowering=False)
        self.ops = {e: [] for e in ENGINES}
        self.cnt = {}
        self.sems = {}
        self.stack = contextlib.ExitStack()
        self._mgrs = {}
        self._regs = {}
        self.selfsync = bool(cfg.get("SELFSYNC", False))
        self.pipedma = (not self.selfsync) and bool(cfg.get("PIPEDMA", True))
        self.blanket = set(cfg.get("BLANKET", ()))
        if self.selfsync:
            self.blanket = set(ENGINES)

    def sb(self, nm, shp, dt=F32):
        mgr = self.nc.sbuf_tensor(nm, shp, dt)
        t = mgr.__enter__()
        self._mgrs[nm] = mgr
        return t

    def psum(self, nm, shp, dt=F32):
        mgr = self.nc.psum_tensor(nm, shp, dt)
        t = mgr.__enter__()
        self._mgrs[nm] = mgr
        return t

    def free(self, t):
        self._mgrs.pop(t.name).__exit__(None, None, None)

    def sem(self, name):
        h = self.stack.enter_context(self.nc.semaphore(name))
        self.sems[name] = h
        self.cnt[name] = 0
        return h

    def emit(self, eng, fn):
        self.ops[eng].append(fn)

    def reg(self, E, name):
        rr = self._regs.setdefault(name, None)
        if rr is None:
            rr = E.alloc_register(name=name)
            self._regs[name] = rr
        return rr

    def inc(self, name, k=1):
        self.cnt[name] += k
        return self.cnt[name]

    def tok(self, name):
        return (name, self.cnt[name])

    def waitv(self, eng, tokv):
        if tokv is None:
            return
        nm, v = tokv
        if nm in ("dO", "dG", "dS"):
            v = self.cnt[nm]
        self.emit(eng, lambda E, s=self.sems[nm], vv=v: E.wait_ge(s, vv))

    def dma(self, eng, out, in_, sem="dO"):
        v = self.inc(sem, 16)
        self.emit(eng, lambda E, o=out, i=in_, s=self.sems[sem]:
                  E.dma_start(out=o, in_=i).then_inc(s, 16))
        pipe = self.pipedma and self.cfg.get("PIPE_DO" if sem == "dO" else "PIPE_DG", True)
        if not pipe:
            self.emit(eng, lambda E, s=self.sems[sem], vv=v: E.wait_ge(s, vv))
        return (sem, v)

    def idma(self, fn, sem="dG"):
        v = self.inc(sem, 16)
        self.emit("gpsimd", lambda E, f=fn, s=self.sems[sem]: f(E, s))
        pipe = self.pipedma and self.cfg.get("PIPE_IDMA", True)
        if not pipe:
            self.emit("gpsimd", lambda E, s=self.sems[sem], vv=v: E.wait_ge(s, vv))
        return (sem, v)

    def selfe(self, eng):
        m = {"tensor": "sPE", "scalar": "sACT", "vector": "sDVE", "gpsimd": "sPOOL"}[eng]
        self.emit(eng, lambda E, s=self.sems[m], vv=self.cnt[m]: E.wait_ge(s, vv))

    def _c(self, eng, semname, fn):
        if eng in self.blanket:
            prev = self.cnt[semname]
            self.emit(eng, lambda E, s=self.sems[semname], vv=prev: E.wait_ge(s, vv))
        v = self.inc(semname)
        self.emit(eng, lambda E, f=fn, s=self.sems[semname]: f(E).then_inc(s, 1))
        return (semname, v)

    def pe(self, fn):
        return self._c("tensor", "sPE", fn)

    def act(self, fn):
        return self._c("scalar", "sACT", fn)

    def dve(self, fn):
        return self._c("vector", "sDVE", fn)

    def pool(self, fn):
        return self._c("gpsimd", "sPOOL", fn)

    def join(self):
        self.emit("sync", lambda E, s=self.sems["dO"], vv=self.cnt["dO"]: E.wait_ge(s, vv))
        self.emit("gpsimd", lambda E, s=self.sems["dG"], vv=self.cnt["dG"]: E.wait_ge(s, vv))
        v = self.inc("sJOIN", 5)
        for e in ENGINES:
            self.emit(e, lambda E, s=self.sems["sJOIN"]: E.nop().then_inc(s, 1))
        for e in ENGINES:
            self.emit(e, lambda E, s=self.sems["sJOIN"], vv=v: E.wait_ge(s, vv))

    def barrier(self, data_toks):
        slot = self._bar_slot
        self._bar_slot += 1
        assert slot < self._nflag
        for t in data_toks:
            self.waitv("gpsimd", t)
        flags_h, flagval_t, flagrd, flgidx_t, flgsrc_t = self._bar_tiles
        dS = self.sems["dS"]
        PO, PI = self.cfg["POLL_OUT"], self.cfg["POLL_IN"]

        def _run(E):
            if self._ds_reg is None:
                self._ds_reg = E.alloc_register(name="ds_count")
                E.reg_mov(self._ds_reg, 0)
                self._bar_regs = tuple(E.alloc_register(name=f"barreg{j}") for j in range(5))
            tgt, done, r0, r1, it = self._bar_regs
            nc_ = E.bass
            engs = [E.engine]
            E.indirect_dma_start(
                out=flags_h.ap()[:, :],
                out_offset=bass.IndirectOffsetOnAxis(flgidx_t[0:2, slot:slot + 1], 0),
                in_=flgsrc_t[0:2, slot * 16:(slot + 1) * 16], in_offset=None).then_inc(dS, 16)
            E.reg_alu(self._ds_reg, self._ds_reg, 16, ALU.add)
            E.reg_load(tgt, flagval_t[0:1, slot:slot + 1])
            E.reg_mov(done, 0)
            E.reg_mov(it, 0)
            loop_s = f"barloop{slot}"
            loop_e = f"barend{slot}"
            nc_.br(loop_s, engines=engs)
            with nc_.body(loop_s, valid_engines=engs):
                E.reg_alu(self._ds_reg, self._ds_reg, 16, ALU.add)
                E.dma_start(out=flagrd[0:1, 0:2],
                            in_=bass.AP(flags_h, slot * 2 * 16, [[16, 2], [1, 1]]),
                            allow_slow_non_contiguous=True).then_inc(dS, 16)
                E.wait_ge(dS, self._ds_reg)
                E.reg_load(r0, flagrd[0:1, 0:1])
                E.reg_load(r1, flagrd[0:1, 1:2])
                with E.If_cmp(r0, tgt, "IS_GE"):
                    with E.If_cmp(r1, tgt, "IS_GE"):
                        E.reg_mov(done, 1)
                E.reg_alu(it, it, 1, ALU.add)
                with E.If_ne(done, 0):
                    E.reg_mov(it, PO * PI)
                nc_.br_lt(it, PO * PI, on_true=loop_s, on_false=loop_e, engines=engs)
            nc_.switch_bb(loop_e)
            return E.nop()

        v = self.inc("sBAR")
        self.emit("gpsimd", lambda E, f=_run, s=self.sems["sBAR"]: f(E).then_inc(s, 1))
        return ("sBAR", v)


def build(cfg):
    K = Builder(cfg)
    nc = K.nc
    T, D, EH, N, R, F, NE, TC, ST = (cfg[k] for k in
                                     ("T", "D", "EH", "N", "R", "F", "NE", "TC", "ST"))
    EPS = cfg["EPS"]
    SL = EH // 128
    DC = D // 128
    H = T // 2
    Q = T // 4
    E = 2 * EH
    EC = E // 128
    FC = F // 128
    NGA = T // 128
    NGO = Q // 128
    NST = Q // ST
    STG = ST // 128
    GB = min(16, NGA)
    NCHA = H // TC
    NCHB = T // TC
    NCH = NCHA + NCHB
    NFLAG = NCH + 8
    MT = min(ST, 512)
    NMT = ST // MT
    assert TC <= 512

    def din(nm, shp, dt=F32):
        h = nc.dram_tensor(nm, shp, dt, kind="ExternalInput")
        return h, h.ap()

    x_b_h, x_b = din("x_b", [T, D])
    x_o_h, x_o = din("x_o", [Q, D])
    winT_h, winT = din("winT", [D, 4 * EH])
    convd_h, convd = din("convd", [128, 2 * 4 * SL * 128])
    bconv_h, bconv_ap = din("bconv", [128, 2 * SL])
    bdt_h, bdt_ap = din("bdt", [128, 2 * SL])
    alog_h, alog_ap = din("alog", [128, 2 * SL * N])
    dvec_h, dvec_ap = din("dvec", [128, 2 * SL])
    wxT_h, wxT_ap = din("wxT", [2 * EH, 96])
    wdtT_h, wdtT_ap = din("wdtT", [64, 2 * EH])
    lnw_h, lnw_ap = din("lnw_b", [128, D])
    lnb_h, lnb_ap = din("lnb_b", [128, D])
    woutT_h, woutT_ap = din("woutT", [2 * E, D])
    fup_h, fup_ap = din("fup", [NE * D, F])
    fdn_h, fdn_ap = din("fdn", [NE * F, D])
    rw_h, rw_ap = din("rw", [D, 8])
    selm_h, selm_ap = din("selm", [16, N * 128])
    ident_h, ident_ap = din("ident", [128, 128])
    meta_h, meta_ap = din("meta", [1, 16], I32)
    flagval_h, flagval_ap = din("flagval", [1, NFLAG], I32)

    out_o_h = nc.dram_tensor("out_o", [Q, D], F32, kind="ExternalOutput")
    out_o = out_o_h.ap()
    DBGF = cfg.get("DBGF", False)
    if DBGF:
        dbgff_h = nc.dram_tensor("dbgff", [Q, D], F32, kind="ExternalOutput")
        dbgg_h = nc.dram_tensor("dbgg", [Q, 8], F32, kind="ExternalOutput")
        dbgh_h = nc.dram_tensor("dbgh", [F, ST], BF16, kind="ExternalOutput")

    xn2fm_h = nc.dram_tensor("xn2fm", [D, T], BF16)
    ffnout_h = nc.dram_tensor("ffnout", [Q, D], F32)
    xdsl_h = nc.dram_tensor("xdsl", [2 * NCH * 96 + 128, TC], F32, addr_space="Shared")
    ysl_h = nc.dram_tensor("ysl", [4 * EH + 128, H], BF16, addr_space="Shared")
    flags_h = nc.dram_tensor("flags", [NFLAG * 2 + 128, 16], I32, addr_space="Shared")
    xdidx_h, xdidx_ap = din("xdidx", [128, NCH * 6], U32)
    yslidx_h, yslidx_ap = din("yslidx", [128, 2 * SL], U32)
    yrdidx_h, yrdidx_ap = din("yrdidx", [128, 4 * SL * NGO], U32)
    flgidx_h, flgidx_ap = din("flgidx", [128, NFLAG], U32)
    flgsrc_h, flgsrc_ap = din("flgsrc", [2, NFLAG * 16], I32)

    for s in ("dS", "dO", "dG", "sPE", "sACT", "sDVE", "sPOOL", "sBAR", "sJOIN"):
        K.sem(s)

    meta_t = K.sb("meta_t", [1, 16], I32)
    flagval_t = K.sb("flagval_t", [1, NFLAG], I32)
    flagrd = K.sb("flagrd", [1, 2], I32)
    xdidx_t = K.sb("xdidx_t", [128, NCH * 6], U32)
    yslidx_t = K.sb("yslidx_t", [128, 2 * SL], U32)
    yrdidx_t = K.sb("yrdidx_t", [128, 4 * SL * NGO], U32)
    flgidx_t = K.sb("flgidx_t", [128, NFLAG], U32)
    flgsrc_t = K.sb("flgsrc_t", [2, NFLAG * 16], I32)
    K._bar_tiles = (flags_h, flagval_t, flagrd, flgidx_t, flgsrc_t)
    K._bar_slot = 0
    K._nflag = NFLAG
    K._ds_reg = None

    identb_t = K.sb("identb_t", [128, 128], BF16)
    identf_t = K.sb("identf_t", [128, 128])
    lnw_t = K.sb("lnw_t", [128, D])
    lnb_t = K.sb("lnb_t", [128, D])
    selm_t = K.sb("selm_t", [16, N * 128], BF16)
    bconv_t = K.sb("bconv_t", [128, 2 * SL])
    bdt_t = K.sb("bdt_t", [128, 2 * SL])
    apos_t = K.sb("apos_t", [128, 2 * SL * N])
    dvec_t = K.sb("dvec_t", [128, 2 * SL])
    wx_st = K.sb("wx_st", [128, 2 * SL * 96], BF16)
    wdt_st = K.sb("wdt_st", [64, 2 * EH])
    stat_c = K.sb("stat_c", [128, NGA])
    stat_ob = K.sb("stat_ob", [128, NGA])
    stato_r = K.sb("stato_r", [128, NGO])
    stato_b = K.sb("stato_b", [128, NGO])
    ones_t = K.sb("ones_t", [128, 1])
    eps_t = K.sb("eps_t", [128, 1])
    carr = K.sb("carr", [128, 2 * SL * N])

    t_meta = K.dma("sync", meta_t[:], meta_ap)
    K.dma("sync", flagval_t[:], flagval_ap)
    t_id = K.dma("gpsimd", identb_t[:], ident_ap, "dG")
    K.dma("sync", identf_t[:], ident_ap)
    K.dma("sync", lnw_t[:], lnw_ap)
    t_lnb = K.dma("sync", lnb_t[:], lnb_ap)
    K.dma("gpsimd", selm_t[:], selm_ap, "dG")
    K.dma("sync", bconv_t[:], bconv_ap)
    K.dma("sync", bdt_t[:], bdt_ap)
    t_al = K.dma("sync", apos_t[:], alog_ap)
    K.dma("sync", dvec_t[:], dvec_ap)
    t_wx = K.dma("gpsimd", wx_st[:].rearrange("p (s c) -> p s c", s=2 * SL),
                 bass.AP(wxT_h, 0, [[96, 128], [128 * 96, 2 * SL], [1, 96]]), "dG")
    t_wdt = K.dma("sync", wdt_st[:], wdtT_ap)
    K.dma("sync", xdidx_t[:], xdidx_ap)
    K.dma("sync", yslidx_t[:], yslidx_ap)
    K.dma("sync", yrdidx_t[:], yrdidx_ap)
    tfi = K.dma("sync", flgidx_t[:], flgidx_ap)
    tfs = K.dma("sync", flgsrc_t[:], flgsrc_ap)
    K.waitv("gpsimd", tfs)
    K.waitv("gpsimd", tfi)

    zt = K.sb("zt", [128, 2048])
    tz = K.dve(lambda E: E.memset(zt[:], 0.0))
    K.waitv("gpsimd", tz)
    ZCH = 128 * 2048
    xd_elems = (2 * NCH * 96 + 128) * TC
    for o in range(0, xd_elems, ZCH):
        n = min(ZCH, xd_elems - o)
        nr, ncol = (128, n // 128) if n % 128 == 0 and n >= 128 else (1, n)
        K.dma("gpsimd", bass.AP(xdsl_h, o, [[ncol, nr], [1, ncol]]),
              zt[0:nr, 0:ncol], "dG")
    ys_elems = (4 * EH + 128) * H
    ztb = zt[:].bitcast(BF16)
    for o in range(0, ys_elems, 2 * ZCH):
        n = min(2 * ZCH, ys_elems - o)
        nr, ncol = (128, n // 128) if n % 128 == 0 and n >= 128 else (1, n)
        K.dma("gpsimd", bass.AP(ysl_h, o, [[ncol, nr], [1, ncol]]),
              ztb[0:nr, 0:ncol], "dG")
    K.barrier([K.tok("dG")])
    K.join()
    K.free(zt)
    K.waitv("scalar", t_al)
    K.act(lambda E: E.activation(apos_t[:], apos_t[:], AF.Exp))
    K.waitv("vector", t_meta)
    K.dve(lambda E: E.memset(carr[:], 0.0))
    K.dve(lambda E: E.memset(ones_t[:], 1.0))
    tmemc = K.dve(lambda E: E.memset(eps_t[:], float(EPS)))
    K.waitv("scalar", tmemc)

    g_x = K.sb("g_x", [128, D])
    g_t1 = K.sb("g_t1", [128, D])
    g_t2 = K.sb("g_t2", [128, D], BF16)
    g_a1 = K.sb("g_a1", [128, NGA])
    g_a2 = K.sb("g_a2", [128, NGA])
    trbuf = K.sb("trbuf", [128, DC * GB * 128], BF16)
    PT32 = K.psum("PT32", [128, 128])
    PTRB = K.psum("PTRB", [128, 128], BF16)

    def ln_stats(x_ap, ngroups, acc1, acc2, prev=None):
        last = prev
        for g in range(ngroups):
            K.waitv("sync", last)
            t_x = K.dma("sync", g_x[:], x_ap[g * 128:(g + 1) * 128, :])
            K.waitv("scalar", t_x)
            K.act(lambda E, g=g: E.activation(g_t1[:], g_x[:], AF.Identity,
                                              accum_out=acc1[:, g:g + 1]))
            last = K.act(lambda E, g=g: E.activation(g_t1[:], g_x[:], AF.Square,
                                                     accum_out=acc2[:, g:g + 1]))
        return last

    def finalize_stats(acc1, acc2, out_scale, out_bias, ngroups, combined):
        a1 = acc1[:, 0:ngroups]
        a2 = acc2[:, 0:ngroups]
        sc = out_scale[:, 0:ngroups]
        ob = out_bias[:, 0:ngroups]
        K.dve(lambda E: E.tensor_scalar_mul(a1, a1, 1.0 / D))
        K.dve(lambda E: E.tensor_scalar_mul(a2, a2, 1.0 / D))
        K.selfe("vector")
        K.dve(lambda E: E.tensor_tensor(sc, a1, a1, ALU.mult))
        K.selfe("vector")
        tv = K.dve(lambda E: E.tensor_tensor(a2, a2, sc, ALU.subtract))
        K.waitv("scalar", tv)
        K.act(lambda E: E.activation(sc, a2, AF.Ln, bias=eps_t[:]))
        K.selfe("scalar")
        tr = K.act(lambda E: E.activation(sc, sc, AF.Exp, scale=-0.5))
        if combined:
            K.waitv("vector", tr)
            K.dve(lambda E: E.tensor_tensor(ob, sc, sc, ALU.mult))
            K.selfe("vector")
            tv2 = K.dve(lambda E: E.tensor_tensor(ob, a2, ob, ALU.mult))
            K.waitv("scalar", tv2)
            K.act(lambda E: E.activation(ob, ob, AF.Ln, bias=eps_t[:]))
            K.selfe("scalar")
            tr2 = K.act(lambda E: E.activation(ob, ob, AF.Exp, scale=-0.5))
            K.waitv("vector", tr2)
            K.dve(lambda E: E.tensor_tensor(sc, sc, ob, ALU.mult))
        else:
            K.waitv("vector", tr)
        K.selfe("vector")
        K.dve(lambda E: E.tensor_tensor(ob, a1, sc, ALU.mult))
        K.selfe("vector")
        return K.dve(lambda E: E.tensor_scalar_mul(ob, ob, -1.0))

    tq = ln_stats(x_b, NGA, g_a1, g_a2)
    K.waitv("vector", tq)
    tst = finalize_stats(g_a1, g_a2, stat_c, stat_ob, NGA, True)
    K.waitv("scalar", tst)
    tq2 = ln_stats(x_o, NGO, g_a1, g_a2, prev=tq)
    K.waitv("vector", tq2)
    tst2 = finalize_stats(g_a1, g_a2, stato_r, stato_b, NGO, False)

    K.waitv("scalar", tst2)
    K.waitv("tensor", t_id)
    nblk = (NGA + GB - 1) // GB
    tr_toks = [None, None]
    wr_tok = None
    prev_x2 = tst2
    prev_ttr = None
    for blk in range(nblk):
        gs = list(range(blk * GB, min((blk + 1) * GB, NGA)))
        K.waitv("scalar", wr_tok)
        for gi, g in enumerate(gs):
            K.waitv("sync", prev_x2)
            t_x = K.dma("sync", g_x[:], x_b[g * 128:(g + 1) * 128, :])
            K.waitv("scalar", t_x)
            tc_ = K.act(lambda E, g=g: E.activation(g_t1[:], g_x[:], AF.Identity,
                                                    scale=stat_c[:, g:g + 1],
                                                    bias=stat_ob[:, g:g + 1]))
            K.waitv("vector", tc_)
            K.waitv("vector", prev_ttr)
            K.dve(lambda E: E.tensor_tensor(g_t1[:], g_t1[:], lnw_t[:], ALU.mult))
            K.selfe("vector")
            tx2 = K.dve(lambda E: E.tensor_tensor(g_t2[:], g_t1[:], lnb_t[:], ALU.add))
            prev_x2 = tx2
            K.waitv("tensor", tx2)
            for dc in range(DC):
                K.waitv("tensor", tr_toks[1])
                ttr = K.pe(lambda E, dc=dc: E.transpose(
                    PTRB[:], g_t2[:, dc * 128:(dc + 1) * 128], identb_t[:]))
                prev_ttr = ttr
                K.waitv("scalar", ttr)
                tev = K.act(lambda E, dc=dc, gi=gi: E.copy(
                    trbuf[:, (dc * GB + gi) * 128:(dc * GB + gi) * 128 + 128], PTRB[:]))
                tr_toks = [tr_toks[1], ("sACT", tev[1])]
        t0 = blk * GB * 128
        ncols = len(gs) * 128
        K.waitv("sync", tr_toks[0])
        K.waitv("sync", tr_toks[1])
        wr_tok = K.dma("sync", bass.AP(xn2fm_h, t0, [[T, 128], [128 * T, DC], [1, ncols]]),
                       trbuf[:, 0:DC * ncols].rearrange("p (dc c) -> p dc c", dc=DC))
    K.join()
    for t in reversed((g_x, g_t1, g_t2, g_a1, g_a2, trbuf)):
        K.free(t)
    K.free(PTRB)

    f_xt = K.sb("f_xt", [128, D])
    f_xn = K.sb("f_xn", [128, D])
    f_xnb = K.sb("f_xnb", [128, DC * ST], BF16)
    f_xnf = K.sb("f_xnf", [128, DC * 128])
    f_rw = K.sb("f_rw", [128, DC * 8])
    f_lg = K.sb("f_lg", [128, 8])
    f_g1 = K.sb("f_g1", [128, 8])
    f_g2 = K.sb("f_g2", [128, 8])
    f_m = K.sb("f_m", [128, 1])
    f_gate = K.sb("f_gate", [128, STG * 8])
    f_wu = [K.sb(f"f_wu{i}", [128, DC * 128], BF16) for i in range(2)]
    f_wdh = K.sb("f_wdh", [128, (FC // 2) * D], BF16)
    f_h = K.sb("f_h", [128, FC * ST], BF16)
    f_acc = K.sb("f_acc", [128, STG * D])
    PU = [K.psum(f"PU{i}", [128, MT]) for i in range(2)]
    PDW = [K.psum(f"PDW{i}", [128, D]) for i in range(2)]
    PRT = K.psum("PRT", [128, 8])

    t_rw = K.dma("sync", f_rw[:].rearrange("p (dc c) -> p dc c", dc=DC),
                 bass.AP(rw_h, 0, [[8, 128], [128 * 8, DC], [1, 8]]))
    K.waitv("tensor", t_rw)

    FH = FC // 2
    tff = None
    for st in range(NST):
        K.waitv("vector", tff)
        tacc0 = K.dve(lambda E: E.memset(f_acc[:], 0.0))
        tr_toks = [None, None]
        tgate = None
        prev_xn = None
        for tg in range(STG):
            g = st * STG + tg
            K.waitv("sync", prev_xn)
            t_x = K.dma("sync", f_xt[:], x_o[g * 128:(g + 1) * 128, :])
            K.waitv("scalar", t_x)
            K.waitv("scalar", tr_toks[0])
            K.waitv("scalar", tr_toks[1])
            txn = K.act(lambda E, g=g: E.activation(f_xn[:], f_xt[:], AF.Identity,
                                                    scale=stato_r[:, g:g + 1],
                                                    bias=stato_b[:, g:g + 1]))
            prev_xn = txn
            K.waitv("tensor", txn)
            for dc in range(DC):
                K.waitv("tensor", tr_toks[1])
                ttr = K.pe(lambda E, dc=dc: E.transpose(
                    PT32[:], f_xn[:, dc * 128:(dc + 1) * 128], identf_t[:]))
                K.waitv("scalar", ttr)
                K.act(lambda E, dc=dc: E.copy(
                    f_xnf[:, dc * 128:(dc + 1) * 128], PT32[:]))
                tev = K.act(lambda E, dc=dc, tg=tg: E.copy(
                    f_xnb[:, dc * ST + tg * 128: dc * ST + tg * 128 + 128], PT32[:]))
                tr_toks = [tr_toks[1], ("sACT", tev[1])]
            K.waitv("tensor", tr_toks[1])
            for dc in range(DC):
                K.pe(lambda E, dc=dc: E.matmul(
                    PRT[:], f_xnf[:, dc * 128:(dc + 1) * 128],
                    f_rw[:, dc * 8:(dc + 1) * 8],
                    start=(dc == 0), stop=(dc == DC - 1)))
            trt = K.tok("sPE")
            K.waitv("vector", trt)
            K.dve(lambda E: E.tensor_copy(f_lg[:], PRT[:]))
            K.selfe("vector")
            K.dve(lambda E: E.tensor_reduce(f_m[:], f_lg[:], AX.X, ALU.max))
            K.selfe("vector")
            tnm = K.dve(lambda E: E.tensor_scalar_mul(f_m[:], f_m[:], -1.0))
            K.waitv("scalar", tnm)
            te = K.act(lambda E: E.activation(f_g1[:], f_lg[:], AF.Exp, bias=f_m[:]))
            K.waitv("vector", te)
            K.dve(lambda E: E.tensor_reduce(f_m[:], f_g1[:], AX.X, ALU.add))
            K.selfe("vector")
            K.dve(lambda E: E.reciprocal(f_m[:], f_m[:]))
            K.selfe("vector")
            K.dve(lambda E: E.tensor_scalar(f_g1[:], f_g1[:], f_m[:], None, ALU.mult))
            K.selfe("vector")
            K.dve(lambda E: E.tensor_reduce(f_m[:], f_g1[:], AX.X, ALU.max))
            K.selfe("vector")
            K.dve(lambda E: E.tensor_scalar(f_g2[:], f_g1[:], f_m[:], None, ALU.is_equal))
            K.selfe("vector")
            K.dve(lambda E: E.scalar_tensor_tensor(f_g2[:], f_g2[:], -1e30, f_g1[:],
                                                   ALU.mult, ALU.add))
            K.selfe("vector")
            K.dve(lambda E: E.tensor_reduce(f_m[:], f_g2[:], AX.X, ALU.max))
            K.selfe("vector")
            K.dve(lambda E: E.tensor_scalar(f_g2[:], f_g1[:], f_m[:], None, ALU.is_ge))
            K.selfe("vector")
            tgate = K.dve(lambda E, tg=tg: E.tensor_tensor(
                f_gate[:, tg * 8:(tg + 1) * 8], f_g1[:], f_g2[:], ALU.mult))

        if DBGF:
            K.waitv("sync", tgate)
            for tg in range(STG):
                K.dma("sync", bass.AP(dbgg_h, (st * ST + tg * 128) * 8, [[8, 128], [1, 8]]),
                      f_gate[:, tg * 8:(tg + 1) * 8])
        wu_free = [None, None]
        pu_free = [None, None]
        tdn_last = None
        fwd_guard = None
        for e in range(NE):
            K.waitv("scalar", tdn_last)
            K.waitv("scalar", tgate)
            for fc in range(FC):
                buf = fc % 2
                K.waitv("gpsimd", wu_free[buf])
                twu = K.dma("gpsimd", f_wu[buf][:].rearrange("p (dc c) -> p dc c", dc=DC),
                            bass.AP(fup_h, (e * D) * F + fc * 128,
                                    [[F, 128], [128 * F, DC], [1, 128]]), "dG")
                K.waitv("tensor", twu)
                for mt in range(NMT):
                    pu = PU[mt % 2]
                    K.waitv("tensor", pu_free[mt % 2])
                    for dc in range(DC):
                        K.pe(lambda E, dc=dc, mt=mt, buf=buf, pu=pu: E.matmul(
                            pu[:], f_wu[buf][:, dc * 128:(dc + 1) * 128],
                            f_xnb[:, dc * ST + mt * MT: dc * ST + mt * MT + MT],
                            start=(dc == 0), stop=(dc == DC - 1)))
                    tmm = K.tok("sPE")
                    K.waitv("scalar", tmm)
                    tgl = K.act(lambda E, fc=fc, mt=mt, pu=pu: E.activation(
                        f_h[:, fc * ST + mt * MT: fc * ST + mt * MT + MT],
                        pu[:], AF.Gelu))
                    pu_free[mt % 2] = ("sACT", tgl[1])
                wu_free[buf] = ("sPE", tmm[1])
            if DBGF and e == 0 and st == 0:
                K.waitv("sync", K.tok("sACT"))
                for fc in range(FC):
                    K.dma("sync", bass.AP(dbgh_h, fc * 128 * ST, [[ST, 128], [1, ST]]),
                          f_h[:, fc * ST:(fc + 1) * ST])
            tgl_last = K.tok("sACT")
            for fh in range(2):
                K.waitv("gpsimd", fwd_guard)
                twd = K.dma("gpsimd", f_wdh[:].rearrange("p (fc c) -> p fc c", fc=FH),
                            bass.AP(fdn_h, (e * F + fh * FH * 128) * D,
                                    [[D, 128], [128 * D, FH], [1, D]]), "dG")
                K.waitv("tensor", twd)
                K.waitv("tensor", tgl_last)
                if e > 0 or fh > 0:
                    K.waitv("tensor", tga_last)
                for tg in range(STG):
                    pd = PDW[tg % 2]
                    if tg >= 2:
                        K.waitv("tensor", tga_toks[tg - 2])
                    for fc in range(FH):
                        fcg = fh * FH + fc
                        for dm in range(D // 512 if D >= 512 else 1):
                            MD = min(D, 512)
                            K.pe(lambda E, fcg=fcg, fc=fc, tg=tg, dm=dm, pd=pd, MD=MD: E.matmul(
                                pd[:, dm * MD: dm * MD + MD],
                                f_h[:, fcg * ST + tg * 128: fcg * ST + tg * 128 + 128],
                                f_wdh[:, fc * D + dm * MD: fc * D + dm * MD + MD],
                                start=(fc == 0), stop=(fc == FH - 1)))
                    tdn = K.tok("sPE")
                    K.waitv("vector", tdn)
                    if e == 0:
                        tga = K.dve(lambda E, tg=tg, pd=pd: E.scalar_tensor_tensor(
                            f_acc[:, tg * D: tg * D + D], pd[:], ones_t[:],
                            f_acc[:, tg * D: tg * D + D], ALU.mult, ALU.add))
                    else:
                        tga = K.dve(lambda E, tg=tg, e=e, pd=pd: E.scalar_tensor_tensor(
                            f_acc[:, tg * D: tg * D + D], pd[:],
                            f_gate[:, tg * 8 + e - 1: tg * 8 + e],
                            f_acc[:, tg * D: tg * D + D], ALU.mult, ALU.add))
                    if tg == 0:
                        tga_toks = {}
                    tga_toks[tg] = tga
                tga_last = tga
                tdn_last = tdn
                fwd_guard = tdn
        K.waitv("sync", tga_last)
        tff = K.dma("sync", bass.AP(ffnout_h, st * ST * D, [[D, 128], [128 * D, STG], [1, D]]),
                    f_acc[:].rearrange("p (tg d) -> p tg d", tg=STG))
    K.join()
    for t in reversed((f_xt, f_xn, f_xnb, f_xnf, f_rw, f_lg, f_g1, f_g2, f_m, f_gate,
                       f_wu[0], f_wu[1], f_wdh, f_h, f_acc)):
        K.free(t)
    for t in reversed((PU[0], PU[1], PDW[0], PDW[1], PRT)):
        K.free(t)

    winT_t = K.sb("winT_t", [128, DC * 2 * EH], BF16)
    convd_t = K.sb("convd_t", [128, 4 * SL * 128], BF16)
    xn2c = K.sb("xn2c", [128, DC * TC], BF16)
    xp_t = K.sb("xp_t", [128, SL * (TC + 4)], BF16)
    zs_t = K.sb("zs_t", [128, SL * TC], BF16)
    xc_t = K.sb("xc_t", [128, SL * TC], BF16)
    ndt_t = K.sb("ndt_t", [128, SL * TC], BF16)
    w_t = K.sb("w_t", [128, SL * TC], BF16)
    sp1 = K.sb("sp1", [128, TC])
    sp2 = K.sb("sp2", [128, TC])
    dtrmy = K.sb("dtrmy", [64, TC])
    dtrpr = K.sb("dtrpr", [64, TC])
    dtrs = K.sb("dtrs", [64, TC])
    bmy = K.sb("bmy", [16, TC])
    bpr = K.sb("bpr", [16, TC])
    cmy = K.sb("cmy", [16, TC])
    cpr = K.sb("cpr", [16, TC])
    bsb = K.sb("bsb", [16, TC], BF16)
    csb = K.sb("csb", [16, TC], BF16)
    brep = [K.sb(f"brep{i}", [128, TC], BF16) for i in range(2)]
    crep = [K.sb(f"crep{i}", [128, TC], BF16) for i in range(2)]
    a_t = [K.sb(f"a_t{i}", [128, TC]) for i in range(2)]
    b_t = [K.sb(f"b_t{i}", [128, TC], BF16) for i in range(2)]
    h_t = K.sb("h_t", [128, TC], BF16)
    tmp_t = K.sb("tmp_t", [128, TC], BF16)
    yacc = K.sb("yacc", [128, SL * TC])
    ybf = [K.sb(f"ybf{i}", [128, TC], BF16) for i in range(2)]
    PZ = [K.psum(f"PZ{i}", [128, TC]) for i in range(2)]
    PCV = K.psum("PCV", [128, TC])
    PXD = K.psum("PXD", [128, TC])
    PXB = K.psum("PXB", [128, TC])
    PREP = K.psum("PREP", [128, TC])

    a_free = [None, None]
    b_free = [None, None]
    ybf_free = [None, None]
    carry_tok = {}
    ysl_toks = []

    chunks = [("B", k) for k in range(NCHB - 1, -1, -1)] + [("A", k) for k in range(NCHA)]
    cur_dir = None

    for ci, (dr, k) in enumerate(chunks):
        d = 1 if dr == "B" else 0
        t0 = k * TC
        owned = (dr == "A") or (t0 < H)
        first = (dr == "B" and k == NCHB - 1) or (dr == "A" and k == 0)
        slot = ci
        K.join()

        if cur_dir != d:
            twl = K.dma("gpsimd", winT_t[:].rearrange("p (dc c) -> p dc c", dc=DC),
                        bass.AP(winT_h, d * 2 * EH,
                                [[4 * EH, 128], [128 * 4 * EH, DC], [1, 2 * EH]]), "dG")
            tcl = K.dma("gpsimd", convd_t[:],
                        convd[:, d * 4 * SL * 128:(d + 1) * 4 * SL * 128], "dG")
            cur_dir = d
            K.waitv("tensor", twl)
            K.waitv("tensor", tcl)

        txn = K.dma("sync", xn2c[:].rearrange("p (dc c) -> p dc c", dc=DC),
                    bass.AP(xn2fm_h, t0, [[T, 128], [128 * T, DC], [1, TC]]))
        if first:
            tst_ = K.dve(lambda E: E.memset(xp_t[:], 0.0))
        else:
            for i in range(SL):
                if dr == "A":
                    tst_ = K.dve(lambda E, i=i: E.tensor_copy(
                        xp_t[:, i * (TC + 4): i * (TC + 4) + 3],
                        xp_t[:, i * (TC + 4) + TC: i * (TC + 4) + TC + 3]))
                else:
                    tst_ = K.dve(lambda E, i=i: E.tensor_copy(
                        xp_t[:, i * (TC + 4) + TC: i * (TC + 4) + TC + 3],
                        xp_t[:, i * (TC + 4): i * (TC + 4) + 3]))
        K.waitv("scalar", tst_)
        K.waitv("tensor", txn)

        off0 = 3 if dr == "A" else 0
        pz_ev = [None, None]
        last_ev = None
        nxz = 2 * SL if owned else SL
        for j in range(nxz):
            i = j % SL
            isz = j >= SL
            pz = PZ[j % 2]
            K.waitv("tensor", pz_ev[j % 2])
            for dc in range(DC):
                col = dc * 2 * EH + (EH if isz else 0) + i * 128
                K.pe(lambda E, col=col, dc=dc, pz=pz: E.matmul(
                    pz[:], winT_t[:, col: col + 128],
                    xn2c[:, dc * TC:(dc + 1) * TC],
                    start=(dc == 0), stop=(dc == DC - 1)))
            tm = K.tok("sPE")
            K.waitv("scalar", tm)
            if not isz:
                last_ev = K.act(lambda E, i=i, pz=pz, off0=off0: E.copy(
                    xp_t[:, i * (TC + 4) + off0: i * (TC + 4) + off0 + TC], pz[:]))
            else:
                last_ev = K.act(lambda E, i=i, pz=pz: E.activation(
                    zs_t[:, i * TC:(i + 1) * TC], pz[:], AF.Silu))
            pz_ev[j % 2] = last_ev

        K.waitv("tensor", last_ev)
        txc = None
        for i in range(SL):
            if txc is not None:
                K.waitv("tensor", txc)
            for tap in range(4):
                off = tap if dr == "A" else (3 - tap)
                K.pe(lambda E, i=i, tap=tap, off=off: E.matmul(
                    PCV[:], convd_t[:, (tap * SL + i) * 128:(tap * SL + i) * 128 + 128],
                    xp_t[:, i * (TC + 4) + off: i * (TC + 4) + off + TC],
                    start=(tap == 0), stop=(tap == 3)))
            tcv = K.tok("sPE")
            K.waitv("scalar", tcv)
            txc = K.act(lambda E, i=i, d=d: E.activation(
                xc_t[:, i * TC:(i + 1) * TC], PCV[:], AF.Silu,
                bias=bconv_t[:, d * SL + i: d * SL + i + 1]))

        K.waitv("tensor", txc)
        for grp, (pj, c0, cw) in enumerate(((PXD, 0, 64), (PXB, 64, 16), (PCV, 80, 16))):
            for i in range(SL):
                K.pe(lambda E, i=i, pj=pj, c0=c0, cw=cw, d=d: E.matmul(
                    pj[0:cw, :], wx_st[:, (d * SL + i) * 96 + c0:(d * SL + i) * 96 + c0 + cw],
                    xc_t[:, i * TC:(i + 1) * TC],
                    start=(i == 0), stop=(i == SL - 1)))
        txd = K.tok("sPE")
        K.waitv("scalar", txd)
        K.act(lambda E: E.copy(dtrmy[:], PXD[0:64, :]))
        K.act(lambda E: E.copy(bmy[:], PXB[0:16, :]))
        tevc = K.act(lambda E: E.copy(cmy[:], PCV[0:16, :]))
        K.waitv("gpsimd", tevc)
        xbufs = (dtrmy, bmy, cmy)
        rbufs = (dtrpr, bpr, cpr)
        v3 = None
        for kk in range(3):
            nr = (64, 16, 16)[kk]
            v3 = K.idma(lambda E, s, kk=kk, slot=slot, nr=nr: E.indirect_dma_start(
                out=xdsl_h.ap()[:, :],
                out_offset=bass.IndirectOffsetOnAxis(
                    xdidx_t[0:nr, slot * 6 + kk: slot * 6 + kk + 1], 0),
                in_=xbufs[kk][:], in_offset=None).then_inc(s, 16))
        tbar = K.barrier([v3])
        v6 = None
        for kk in range(3):
            nr = (64, 16, 16)[kk]
            v6 = K.idma(lambda E, s, kk=kk, slot=slot, nr=nr: E.indirect_dma_start(
                out=rbufs[kk][:], out_offset=None,
                in_=xdsl_h.ap()[:, :],
                in_offset=bass.IndirectOffsetOnAxis(
                    xdidx_t[0:nr, slot * 6 + 3 + kk: slot * 6 + 3 + kk + 1], 0)).then_inc(s, 16))
        K.waitv("vector", v6)
        K.dve(lambda E: E.tensor_tensor(dtrs[:], dtrmy[:], dtrpr[:], ALU.add))
        tbs = K.dve(lambda E: E.tensor_tensor(bsb[:], bmy[:], bpr[:], ALU.add))
        tcs = K.dve(lambda E: E.tensor_tensor(csb[:], cmy[:], cpr[:], ALU.add))

        K.waitv("tensor", tcs)
        prev_sp = None
        prev_nd = None
        for i in range(SL):
            K.waitv("tensor", prev_sp)
            K.pe(lambda E, i=i, d=d: E.matmul(
                PXD[:], wdt_st[:, d * EH + i * 128: d * EH + i * 128 + 128],
                dtrs[:], start=True, stop=True))
            tdt = K.tok("sPE")
            K.waitv("scalar", tdt)
            K.waitv("scalar", prev_nd)
            K.act(lambda E, i=i, d=d: E.activation(sp1[:], PXD[:], AF.Abs,
                                                   bias=bdt_t[:, d * SL + i: d * SL + i + 1]))
            K.act(lambda E, i=i, d=d: E.activation(sp2[:], PXD[:], AF.Relu,
                                                   bias=bdt_t[:, d * SL + i: d * SL + i + 1]))
            K.selfe("scalar")
            K.act(lambda E: E.activation(sp1[:], sp1[:], AF.Exp, scale=-1.0))
            K.selfe("scalar")
            tsp = K.act(lambda E: E.activation(sp1[:], sp1[:], AF.Ln, bias=1.0))
            prev_sp = tsp
            K.waitv("vector", tsp)
            tnd = K.dve(lambda E, i=i: E.scalar_tensor_tensor(
                ndt_t[:, i * TC:(i + 1) * TC], sp1[:], -1.0, sp2[:], ALU.mult, ALU.subtract))
            K.selfe("vector")
            tw = K.dve(lambda E, i=i: E.tensor_tensor(
                w_t[:, i * TC:(i + 1) * TC], ndt_t[:, i * TC:(i + 1) * TC],
                xc_t[:, i * TC:(i + 1) * TC], ALU.mult))
            prev_nd = tw
        tndt = K.tok("sDVE")

        K.waitv("scalar", tndt)
        K.waitv("gpsimd", tndt)
        if owned:
            tyz = K.dve(lambda E: E.memset(yacc[:], 0.0))
        rep_prev = [None, None]
        prep_guard = None
        for n in range(N):
            p = n % 2
            K.waitv("tensor", rep_prev[p])
            K.waitv("tensor", prep_guard)
            trb = K.pe(lambda E, n=n: E.matmul(
                PREP[:], selm_t[:, n * 128:(n + 1) * 128], bsb[:], start=True, stop=True))
            K.waitv("scalar", trb)
            tbe = K.act(lambda E, p=p: E.copy(brep[p][:], PREP[:]))
            if owned:
                K.waitv("tensor", tbe)
                trc = K.pe(lambda E, n=n: E.matmul(
                    PREP[:], selm_t[:, n * 128:(n + 1) * 128], csb[:], start=True, stop=True))
                K.waitv("scalar", trc)
                tce = K.act(lambda E, p=p: E.copy(crep[p][:], PREP[:]))
                prep_guard = tce
            else:
                prep_guard = tbe
            b_last = None
            y_last = None
            for i in range(SL):
                ap_ = i % 2
                idx = d * SL * N + i * N + n
                K.waitv("scalar", a_free[ap_])
                ta = K.act(lambda E, i=i, n=n, d=d, ap_=ap_: E.activation(
                    a_t[ap_][:], ndt_t[:, i * TC:(i + 1) * TC], AF.Exp,
                    scale=apos_t[:, d * SL * N + i * N + n: d * SL * N + i * N + n + 1]))
                a_free[ap_] = None
                K.waitv("gpsimd", b_free[ap_])
                K.waitv("gpsimd", tbe)
                tb = K.pool(lambda E, i=i, ap_=ap_, p=p: E.tensor_tensor(
                    b_t[ap_][:], w_t[:, i * TC:(i + 1) * TC], brep[p][:], ALU.mult))
                b_free[ap_] = None
                b_last = tb
                K.waitv("vector", ta)
                K.waitv("vector", tb)
                K.waitv("vector", carry_tok.get(idx))
                if dr == "A":
                    tts = K.dve(lambda E, ap_=ap_, idx=idx: E.tensor_tensor_scan(
                        h_t[:], a_t[ap_][:], b_t[ap_][:], carr[:, idx:idx + 1],
                        ALU.mult, ALU.add))
                    lastcol = TC - 1
                else:
                    tts = K.dve(lambda E, ap_=ap_, idx=idx: E.tensor_tensor_scan(
                        h_t[:, ::-1], a_t[ap_][:, ::-1], b_t[ap_][:, ::-1],
                        carr[:, idx:idx + 1], ALU.mult, ALU.add))
                    lastcol = 0
                a_free[ap_] = ("sDVE", tts[1])
                b_free[ap_] = ("sDVE", tts[1])
                K.selfe("vector")
                tcar = K.dve(lambda E, idx=idx, lastcol=lastcol: E.tensor_copy(
                    carr[:, idx:idx + 1], h_t[:, lastcol:lastcol + 1]))
                carry_tok[idx] = tcar
                if owned:
                    K.waitv("vector", tce)
                    K.dve(lambda E, p=p: E.tensor_tensor(tmp_t[:], h_t[:], crep[p][:], ALU.mult))
                    K.selfe("vector")
                    if n == 0:
                        ty = K.dve(lambda E, i=i: E.tensor_copy(
                            yacc[:, i * TC:(i + 1) * TC], tmp_t[:]))
                    else:
                        ty = K.dve(lambda E, i=i: E.tensor_tensor(
                            yacc[:, i * TC:(i + 1) * TC], yacc[:, i * TC:(i + 1) * TC],
                            tmp_t[:], ALU.add))
                    y_last = ty
            rep_prev[p] = ("sPOOL", b_last[1]) if not owned else ("sDVE", y_last[1])

        if owned:
            for i in range(SL):
                yb = ybf[i % 2]
                K.waitv("vector", ybf_free[i % 2])
                K.dve(lambda E, i=i, d=d: E.scalar_tensor_tensor(
                    sp1[:], xc_t[:, i * TC:(i + 1) * TC], dvec_t[:, d * SL + i: d * SL + i + 1],
                    yacc[:, i * TC:(i + 1) * TC], ALU.mult, ALU.subtract))
                K.selfe("vector")
                tyf = K.dve(lambda E, i=i, yb=yb: E.tensor_tensor(
                    yb[:], sp1[:], zs_t[:, i * TC:(i + 1) * TC], ALU.mult))
                K.waitv("gpsimd", tyf)
                vy = K.idma(lambda E, s, yb=yb, i=i, d=d, t0=t0:
                            E.indirect_dma_start(
                                out=ysl_h.ap()[:, :],
                                out_offset=bass.IndirectOffsetOnAxis(
                                    yslidx_t[:, d * SL + i: d * SL + i + 1], 0),
                                in_=yb[:], in_offset=None,
                                element_offset=t0).then_inc(s, 16))
                ysl_toks.append(vy)
                ybf_free[i % 2] = vy

    tyb = K.barrier(ysl_toks[-1:])
    K.join()
    for t in reversed((winT_t, convd_t, xn2c, xp_t, zs_t, xc_t, ndt_t, w_t, sp1, sp2,
                       dtrmy, dtrpr, dtrs, bmy, bpr, cmy, cpr, bsb, csb,
                       brep[0], brep[1], crep[0], crep[1], a_t[0], a_t[1],
                       b_t[0], b_t[1], h_t, tmp_t, yacc, ybf[0], ybf[1])):
        K.free(t)
    for t in reversed((PZ[0], PZ[1], PCV, PXD, PXB, PREP)):
        K.free(t)

    p3_y = K.sb("p3_y", [128, 2 * EC * 128], BF16)
    p3_w = K.sb("p3_w", [128, 2 * EC * D], BF16)
    p3_x = K.sb("p3_x", [128, D])
    p3_n = K.sb("p3_n", [128, D])
    p3_f = K.sb("p3_f", [128, D])
    p3_o = K.sb("p3_o", [128, D])
    PP3 = [K.psum(f"PP3{i}", [128, D]) for i in range(2)]

    if DBGF:
        for qrow in range(0, Q, 128):
            K.dma("sync", bass.AP(dbgff_h, qrow * D, [[D, 128], [1, D]]),
                  bass.AP(ffnout_h, qrow * D, [[D, 128], [1, D]]))
    twout = K.dma("gpsimd", p3_w[:].rearrange("p (dc c) -> p dc c", dc=2 * EC),
                  bass.AP(woutT_h, 0, [[D, 128], [128 * D, 2 * EC], [1, D]]), "dG")
    K.waitv("tensor", twout)

    out_toks = []
    pp_free = [None, None]
    prev_fin = None
    prev_out = None
    prev_tpw = None
    for tg in range(NGO):
        ysl_flat = bass.AP(ysl_h, 0, [[128, (4 * EH + 128) * H // 128], [1, 128]])
        K.waitv("gpsimd", prev_tpw)
        tydma = None
        for ehs in range(2):
            for dd in range(2):
                for i in range(SL):
                    col = ((dd * 2 + ehs) * SL + i) * NGO + tg
                    dst = (dd * EC + ehs * SL + i) * 128
                    tydma = K.idma(lambda E, s, col=col, dst=dst:
                                   E.indirect_dma_start(
                                       out=p3_y[:, dst:dst + 128], out_offset=None,
                                       in_=ysl_flat,
                                       in_offset=bass.IndirectOffsetOnAxis(
                                           yrdidx_t[:, col:col + 1], 0)).then_inc(s, 16))
        pp = PP3[tg % 2]
        K.waitv("tensor", pp_free[tg % 2])
        K.waitv("tensor", tydma)
        nec = 2 * EC
        for ecg in range(nec):
            for dm in range(max(1, D // 512)):
                MD = min(D, 512)
                K.pe(lambda E, ecg=ecg, dm=dm, pp=pp, MD=MD: E.matmul(
                    pp[:, dm * MD: dm * MD + MD],
                    p3_y[:, ecg * 128:(ecg + 1) * 128],
                    p3_w[:, ecg * D + dm * MD: ecg * D + dm * MD + MD],
                    start=(ecg == 0), stop=(ecg == nec - 1)))
        tpw = K.tok("sPE")
        prev_tpw = tpw
        K.waitv("sync", prev_fin)
        t_x = K.dma("sync", p3_x[:], x_o[tg * 128:(tg + 1) * 128, :])
        t_f = K.dma("sync", p3_f[:], bass.AP(ffnout_h, tg * 128 * D, [[D, 128], [1, D]]))
        K.waitv("scalar", t_x)
        txn = K.act(lambda E, tg=tg: E.activation(p3_n[:], p3_x[:], AF.Identity,
                                                  scale=stato_r[:, tg:tg + 1],
                                                  bias=stato_b[:, tg:tg + 1]))
        K.waitv("vector", txn)
        K.waitv("vector", t_f)
        K.waitv("vector", tpw)
        K.waitv("vector", prev_out)
        K.dve(lambda E: E.scalar_tensor_tensor(p3_o[:], p3_n[:], 2.0, p3_x[:],
                                               ALU.mult, ALU.add))
        K.selfe("vector")
        K.dve(lambda E: E.tensor_tensor(p3_o[:], p3_o[:], p3_f[:], ALU.add))
        K.selfe("vector")
        tfin = K.dve(lambda E, pp=pp: E.scalar_tensor_tensor(
            p3_o[:], pp[:], 0.5, p3_o[:], ALU.mult, ALU.add))
        pp_free[tg % 2] = ("sDVE", tfin[1])
        prev_fin = tfin
        K.waitv("sync", tfin)
        tout = K.dma("sync", out_o[tg * 128:(tg + 1) * 128, :], p3_o[:])
        prev_out = tout
        out_toks.append(tout)

    K.waitv("sync", out_toks[-1])
    K.join()

    with K.stack:
        with nc.Block() as block:
            @block.sync
            def _(E):
                for f in K.ops["sync"]:
                    f(E)

            @block.tensor
            def _(E):
                for f in K.ops["tensor"]:
                    f(E)

            @block.vector
            def _(E):
                for f in K.ops["vector"]:
                    f(E)

            @block.scalar
            def _(E):
                for f in K.ops["scalar"]:
                    f(E)

            @block.gpsimd
            def _(E):
                for f in K.ops["gpsimd"]:
                    f(E)
    return nc



INPUT_NAMES = (
    "L0_attn_x", "L0_attn_ln_w", "L0_attn_ln_b",
    "L0_attn_f_W_in", "L0_attn_f_W_conv", "L0_attn_f_b_conv", "L0_attn_f_W_x",
    "L0_attn_f_W_dt", "L0_attn_f_b_dt", "L0_attn_f_A_log", "L0_attn_f_D", "L0_attn_f_W_out",
    "L0_attn_b_W_in", "L0_attn_b_W_conv", "L0_attn_b_b_conv", "L0_attn_b_W_x",
    "L0_attn_b_W_dt", "L0_attn_b_b_dt", "L0_attn_b_A_log", "L0_attn_b_D", "L0_attn_b_W_out",
    "L0_ffn_shared_up", "L0_ffn_shared_down", "L0_ffn_routed_up", "L0_ffn_routed_down",
    "L0_ffn_router_w")


def _prep_core_inputs(cfg, c, I, nonce, solo=False):
    T, D, EH, N, R, F, NE, TC = (cfg[k] for k in ("T", "D", "EH", "N", "R", "F", "NE", "TC"))
    SL = EH // 128
    H, Q = T // 2, T // 4
    E = 2 * EH
    NCH = H // TC + T // TC
    NFLAG = NCH + 8
    b = c // 4
    hi = (c % 4) >= 2
    eh = c % 2
    qq = eh
    if solo:
        b, hi, qq = 0, False, 0

    x = I["L0_attn_x"][b]
    xp_ = x[::-1] if hi else x
    xp_ = np.ascontiguousarray(xp_, np.float32)
    x_o = np.ascontiguousarray(xp_[qq * Q:(qq + 1) * Q])

    pfx = ("f_", "b_") if not hi else ("b_", "f_")
    g = lambda d, nm: I["L0_attn_" + pfx[d] + nm]

    Etrue = EH if solo else E
    ehw = 0 if solo else eh
    winT = np.zeros((D, 4 * EH), np.float32)
    for d in range(2):
        Win = g(d, "W_in")
        xp_rows = Win[ehw * EH:(ehw + 1) * EH]
        z_rows = Win[Etrue + ehw * EH: Etrue + (ehw + 1) * EH]
        winT[:, d * 2 * EH: d * 2 * EH + EH] = xp_rows.T
        winT[:, d * 2 * EH + EH: d * 2 * EH + 2 * EH] = z_rows.T

    convdm = np.zeros((2 * 4 * SL * 128, 128), np.float32)
    for d in range(2):
        Wc = g(d, "W_conv")[ehw * EH:(ehw + 1) * EH]
        for tap in range(4):
            for i in range(SL):
                r0 = ((d * 4 + tap) * SL + i) * 128
                convdm[r0:r0 + 128, :] = np.diag(Wc[i * 128:(i + 1) * 128, tap])
    convd = np.ascontiguousarray(
        convdm.reshape(2 * 4 * SL, 128, 128).transpose(1, 0, 2).reshape(128, -1))

    def col2(nm):
        out = np.zeros((128, 2 * SL), np.float32)
        for d in range(2):
            v = g(d, nm)[ehw * EH:(ehw + 1) * EH]
            out[:, d * SL:(d + 1) * SL] = v.reshape(SL, 128).T
        return out

    bconv = col2("b_conv")
    bdt = col2("b_dt")
    dvec = col2("D")
    alog = np.zeros((128, 2 * SL * N), np.float32)
    for d in range(2):
        A = g(d, "A_log")[ehw * EH:(ehw + 1) * EH]
        alog[:, d * SL * N:(d + 1) * SL * N] = \
            A.reshape(SL, 128, N).transpose(1, 0, 2).reshape(128, SL * N)

    wxT = np.zeros((2 * EH, 96), np.float32)
    for d in range(2):
        Wx = g(d, "W_x")
        half = slice(ehw * EH, (ehw + 1) * EH)
        wxT[d * EH:(d + 1) * EH, 0:R] = Wx[0:R, half].T
        wxT[d * EH:(d + 1) * EH, 64:64 + N] = Wx[R:R + N, half].T
        wxT[d * EH:(d + 1) * EH, 80:80 + N] = Wx[R + N:R + 2 * N, half].T
    if solo:
        wxT *= 0.5

    wdtT = np.zeros((64, 2 * EH), np.float32)
    for d in range(2):
        Wdt = g(d, "W_dt")[ehw * EH:(ehw + 1) * EH]
        wdtT[0:R, d * EH:(d + 1) * EH] = Wdt.T

    woutT = np.zeros((2 * E, D), np.float32)
    for d in range(2):
        if solo:
            woutT[d * E: d * E + EH] = g(d, "W_out").T[:EH]
        else:
            woutT[d * E:(d + 1) * E] = g(d, "W_out").T

    lnw_b = np.ascontiguousarray(np.broadcast_to(I["L0_attn_ln_w"], (128, D)), np.float32)
    lnb_b = np.ascontiguousarray(np.broadcast_to(I["L0_attn_ln_b"], (128, D)), np.float32)

    fup = np.concatenate([I["L0_ffn_shared_up"], I["L0_ffn_routed_up"]], 0)[:NE]
    fdn = np.concatenate([I["L0_ffn_shared_down"], I["L0_ffn_routed_down"]], 0)[:NE]
    fup = np.ascontiguousarray(fup.reshape(NE * D, F), np.float32)
    fdn = np.ascontiguousarray(fdn.reshape(NE * F, D), np.float32)
    rw = np.ascontiguousarray(I["L0_ffn_router_w"], np.float32)

    selm = np.zeros((16, N * 128), np.float32)
    for n in range(N):
        selm[n, n * 128:(n + 1) * 128] = 1.0
    ident = np.eye(128, dtype=np.float32)
    meta = np.zeros((1, 16), np.int32)
    meta[0, 0] = eh
    meta[0, 1] = 1 - eh
    meta[0, 2] = nonce
    meta[0, 3] = qq
    flagval = (nonce + 1 + np.arange(NFLAG)).astype(np.int32).reshape(1, NFLAG)

    p = np.arange(128, dtype=np.uint32)
    xdump = 2 * NCH * 96 + p
    xdidx = np.zeros((128, NCH * 6), np.uint32)
    offs = (0, 64, 80)
    nrs = (64, 16, 16)
    for s_ in range(NCH):
        for k in range(3):
            reh = eh if solo else (1 - eh)
            w = np.where(p < nrs[k], eh * NCH * 96 + s_ * 96 + offs[k] + p, xdump)
            r_ = np.where(p < nrs[k], reh * NCH * 96 + s_ * 96 + offs[k] + p, xdump)
            xdidx[:, s_ * 6 + k] = w
            xdidx[:, s_ * 6 + 3 + k] = r_
    EHl = EH
    yslidx = np.zeros((128, 2 * SL), np.uint32)
    for d in range(2):
        for i in range(SL):
            yslidx[:, d * SL + i] = eh * 2 * EHl + d * EHl + i * 128 + p
    NGO = Q // 128
    yrdidx = np.zeros((128, 4 * SL * NGO), np.uint32)
    for dd_ in range(2):
        for ehs in range(2):
            for i in range(SL):
                for tg in range(NGO):
                    col = ((dd_ * 2 + ehs) * SL + i) * NGO + tg
                    es = eh if solo else ehs
                    yrdidx[:, col] = ((es * 2 * EHl + dd_ * EHl + i * 128 + p) * (H // 128)
                                     + qq * (Q // 128) + tg)
    fdump = NFLAG * 2 + p
    flgidx = np.zeros((128, NFLAG), np.uint32)
    for s_ in range(NFLAG):
        flgidx[:, s_] = fdump
        flgidx[0, s_] = s_ * 2 + eh
        flgidx[1, s_] = (s_ * 2 + 1 - eh) if solo else (NFLAG * 2 + 64 + (s_ % 32))
    flgsrc = np.zeros((2, NFLAG * 16), np.int32)
    for s_ in range(NFLAG):
        flgsrc[0, s_ * 16:(s_ + 1) * 16] = nonce + 1 + s_
        if solo:
            flgsrc[1, s_ * 16:(s_ + 1) * 16] = nonce + 1 + s_

    return dict(x_b=xp_, x_o=x_o, winT=winT, convd=convd, bconv=bconv, bdt=bdt,
                alog=alog, dvec=dvec, wxT=wxT, wdtT=wdtT, lnw_b=lnw_b, lnb_b=lnb_b,
                woutT=woutT, fup=fup, fdn=fdn, rw=rw, selm=selm, ident=ident,
                meta=meta, flagval=flagval, xdidx=xdidx, yslidx=yslidx,
                yrdidx=yrdidx, flgidx=flgidx, flgsrc=flgsrc)


def assemble_output(cfg, results):
    T, D = cfg["T"], cfg["D"]
    H, Q = T // 2, T // 4
    out = np.zeros((2, T, D), np.float32)
    for c in range(8):
        b = c // 4
        hi = (c % 4) >= 2
        qq = c % 2
        y = results[c]["out_o"]
        if not hi:
            out[b, qq * Q:(qq + 1) * Q] = y
        else:
            out[b, T - (qq + 1) * Q: T - qq * Q] = y[::-1]
    return out


_BUILD_CACHE = {}


def kernel(**inputs):
    assert all(n in inputs for n in INPUT_NAMES), sorted(inputs)
    cfg = mk_cfg(full=True)
    key = "full"
    if key not in _BUILD_CACHE:
        _BUILD_CACHE[key] = build(cfg)
    nc = _BUILD_CACHE[key]
    nonce = int(np.random.randint(1, 2**30))
    in_maps = [_prep_core_inputs(cfg, c, inputs, nonce) for c in range(8)]
    res = run_bass_kernel_spmd(nc, in_maps, core_ids=list(range(8)))
    out = assemble_output(cfg, [res.results[c] for c in range(8)])
    return out.astype(np.float32)


# version 06 (final)
# speedup vs baseline: 2.0074x; 1.1081x over previous
import contextlib
import numpy as np
import ml_dtypes
import concourse.bass as bass
import concourse.mybir as mybir
from concourse.bass_utils import run_bass_kernel_spmd

F32 = mybir.dt.float32
BF16 = mybir.dt.bfloat16
I32 = mybir.dt.int32
U32 = mybir.dt.uint32
AF = mybir.ActivationFunctionType
ALU = mybir.AluOpType
AX = mybir.AxisListType
BF16NP = ml_dtypes.bfloat16

ENGINES = ("sync", "tensor", "vector", "scalar", "gpsimd")


def _nfh(FC):
    return 4 if FC % 4 == 0 else (2 if FC % 2 == 0 else 1)


def mk_cfg(full=True):
    if full:
        return dict(T=16384, D=1024, EH=1024, N=16, R=64, F=4096, NE=9, TC=512,
                    ST=1024, EPS=1e-5, POLL_OUT=4000, POLL_IN=64)
    return dict(T=512, D=256, EH=128, N=4, R=32, F=256, NE=3, TC=128,
                ST=128, EPS=1e-5, POLL_OUT=1000, POLL_IN=8)


def mk_cfg_mid():
    return dict(T=2048, D=256, EH=128, N=4, R=32, F=512, NE=3, TC=128,
                ST=256, EPS=1e-5, POLL_OUT=1000, POLL_IN=8)


class Builder:
    def __init__(self, cfg):
        self.cfg = cfg
        self.nc = bass.Bass("TRN2", target_bir_lowering=False)
        self.ops = {e: [] for e in ENGINES}
        self.cnt = {}
        self.sems = {}
        self.stack = contextlib.ExitStack()
        self._mgrs = {}
        self._regs = {}
        self.selfsync = bool(cfg.get("SELFSYNC", False))
        self.pipedma = (not self.selfsync) and bool(cfg.get("PIPEDMA", True))
        self.blanket = set(cfg.get("BLANKET", ()))
        if self.selfsync:
            self.blanket = set(ENGINES)

    def sb(self, nm, shp, dt=F32):
        mgr = self.nc.sbuf_tensor(nm, shp, dt)
        t = mgr.__enter__()
        self._mgrs[nm] = mgr
        return t

    def psum(self, nm, shp, dt=F32):
        mgr = self.nc.psum_tensor(nm, shp, dt)
        t = mgr.__enter__()
        self._mgrs[nm] = mgr
        return t

    def free(self, t):
        self._mgrs.pop(t.name).__exit__(None, None, None)

    def sem(self, name):
        h = self.stack.enter_context(self.nc.semaphore(name))
        self.sems[name] = h
        self.cnt[name] = 0
        return h

    def emit(self, eng, fn):
        self.ops[eng].append(fn)

    def reg(self, E, name):
        rr = self._regs.setdefault(name, None)
        if rr is None:
            rr = E.alloc_register(name=name)
            self._regs[name] = rr
        return rr

    def inc(self, name, k=1):
        self.cnt[name] += k
        return self.cnt[name]

    def tok(self, name):
        return (name, self.cnt[name])

    def waitv(self, eng, tokv):
        if tokv is None:
            return
        nm, v = tokv
        if nm in ("dO", "dG", "dS"):
            v = self.cnt[nm]
        self.emit(eng, lambda E, s=self.sems[nm], vv=v: E.wait_ge(s, vv))

    def dma(self, eng, out, in_, sem="dO"):
        v = self.inc(sem, 16)
        self.emit(eng, lambda E, o=out, i=in_, s=self.sems[sem]:
                  E.dma_start(out=o, in_=i).then_inc(s, 16))
        pipe = self.pipedma and self.cfg.get("PIPE_DO" if sem == "dO" else "PIPE_DG", True)
        if not pipe:
            self.emit(eng, lambda E, s=self.sems[sem], vv=v: E.wait_ge(s, vv))
        return (sem, v)

    def idma(self, fn, sem="dG"):
        v = self.inc(sem, 16)
        self.emit("gpsimd", lambda E, f=fn, s=self.sems[sem]: f(E, s))
        pipe = self.pipedma and self.cfg.get("PIPE_IDMA", True)
        if not pipe:
            self.emit("gpsimd", lambda E, s=self.sems[sem], vv=v: E.wait_ge(s, vv))
        return (sem, v)

    def selfe(self, eng):
        m = {"tensor": "sPE", "scalar": "sACT", "vector": "sDVE", "gpsimd": "sPOOL"}[eng]
        self.emit(eng, lambda E, s=self.sems[m], vv=self.cnt[m]: E.wait_ge(s, vv))

    def _c(self, eng, semname, fn):
        if eng in self.blanket:
            prev = self.cnt[semname]
            self.emit(eng, lambda E, s=self.sems[semname], vv=prev: E.wait_ge(s, vv))
        v = self.inc(semname)
        self.emit(eng, lambda E, f=fn, s=self.sems[semname]: f(E).then_inc(s, 1))
        return (semname, v)

    def pe(self, fn):
        return self._c("tensor", "sPE", fn)

    def act(self, fn):
        return self._c("scalar", "sACT", fn)

    def dve(self, fn):
        return self._c("vector", "sDVE", fn)

    def pool(self, fn):
        return self._c("gpsimd", "sPOOL", fn)

    def join(self):
        self.emit("sync", lambda E, s=self.sems["dO"], vv=self.cnt["dO"]: E.wait_ge(s, vv))
        self.emit("gpsimd", lambda E, s=self.sems["dG"], vv=self.cnt["dG"]: E.wait_ge(s, vv))
        v = self.inc("sJOIN", 5)
        for e in ENGINES:
            self.emit(e, lambda E, s=self.sems["sJOIN"]: E.nop().then_inc(s, 1))
        for e in ENGINES:
            self.emit(e, lambda E, s=self.sems["sJOIN"], vv=v: E.wait_ge(s, vv))

    def barrier(self, data_toks):
        slot = self._bar_slot
        self._bar_slot += 1
        assert slot < self._nflag
        for t in data_toks:
            self.waitv("gpsimd", t)
        flags_h, flagval_t, flagrd, flgidx_t, flgsrc_t = self._bar_tiles
        dS = self.sems["dS"]
        PO, PI = self.cfg["POLL_OUT"], self.cfg["POLL_IN"]

        def _run(E):
            if self._ds_reg is None:
                self._ds_reg = E.alloc_register(name="ds_count")
                E.reg_mov(self._ds_reg, 0)
                self._bar_regs = tuple(E.alloc_register(name=f"barreg{j}") for j in range(5))
            tgt, done, r0, r1, it = self._bar_regs
            nc_ = E.bass
            engs = [E.engine]
            E.indirect_dma_start(
                out=flags_h.ap()[:, :],
                out_offset=bass.IndirectOffsetOnAxis(flgidx_t[0:2, slot:slot + 1], 0),
                in_=flgsrc_t[0:2, slot * 16:(slot + 1) * 16], in_offset=None).then_inc(dS, 16)
            E.reg_alu(self._ds_reg, self._ds_reg, 16, ALU.add)
            E.reg_load(tgt, flagval_t[0:1, slot:slot + 1])
            E.reg_mov(done, 0)
            E.reg_mov(it, 0)
            loop_s = f"barloop{slot}"
            loop_e = f"barend{slot}"
            nc_.br(loop_s, engines=engs)
            with nc_.body(loop_s, valid_engines=engs):
                E.reg_alu(self._ds_reg, self._ds_reg, 16, ALU.add)
                E.dma_start(out=flagrd[0:1, 0:2],
                            in_=bass.AP(flags_h, slot * 2 * 16, [[16, 2], [1, 1]]),
                            allow_slow_non_contiguous=True).then_inc(dS, 16)
                E.wait_ge(dS, self._ds_reg)
                E.reg_load(r0, flagrd[0:1, 0:1])
                E.reg_load(r1, flagrd[0:1, 1:2])
                with E.If_cmp(r0, tgt, "IS_GE"):
                    with E.If_cmp(r1, tgt, "IS_GE"):
                        E.reg_mov(done, 1)
                E.reg_alu(it, it, 1, ALU.add)
                with E.If_ne(done, 0):
                    E.reg_mov(it, PO * PI)
                nc_.br_lt(it, PO * PI, on_true=loop_s, on_false=loop_e, engines=engs)
            nc_.switch_bb(loop_e)
            return E.nop()

        v = self.inc("sBAR")
        self.emit("gpsimd", lambda E, f=_run, s=self.sems["sBAR"]: f(E).then_inc(s, 1))
        return ("sBAR", v)


def build(cfg):
    K = Builder(cfg)
    nc = K.nc
    T, D, EH, N, R, F, NE, TC, ST = (cfg[k] for k in
                                     ("T", "D", "EH", "N", "R", "F", "NE", "TC", "ST"))
    EPS = cfg["EPS"]
    SL = EH // 128
    DC = D // 128
    H = T // 2
    Q = T // 4
    E = 2 * EH
    EC = E // 128
    FC = F // 128
    NGA = T // 128
    NGO = Q // 128
    NST = Q // ST
    STG = ST // 128
    GB = min(16, NGA)
    NCHA = H // TC
    NCHB = T // TC
    NCH = NCHA + NCHB
    NFLAG = NCH + 8
    MT = min(ST, 512)
    NMT = ST // MT
    assert TC <= 512

    def din(nm, shp, dt=F32):
        h = nc.dram_tensor(nm, shp, dt, kind="ExternalInput")
        return h, h.ap()

    x_b_h, x_b = din("x_b", [T, D])
    x_o_h, x_o = din("x_o", [Q, D])
    winT_h, winT = din("winT", [D, 4 * EH])
    convd_h, convd = din("convd", [128, 2 * 4 * SL * 128])
    bconv_h, bconv_ap = din("bconv", [128, 2 * SL])
    bdt_h, bdt_ap = din("bdt", [128, 2 * SL])
    alog_h, alog_ap = din("alog", [128, 2 * SL * N])
    dvec_h, dvec_ap = din("dvec", [128, 2 * SL])
    wxT_h, wxT_ap = din("wxT", [2 * EH, 96])
    wdtT_h, wdtT_ap = din("wdtT", [64, 2 * EH])
    lnw_h, lnw_ap = din("lnw_b", [128, D])
    lnb_h, lnb_ap = din("lnb_b", [128, D])
    woutT_h, woutT_ap = din("woutT", [2 * E, D])
    NFH = _nfh(FC)
    FH = FC // NFH
    fup_h, fup_ap = din("fup", [NE * FC * 128, DC * 128], BF16)
    fdn_h, fdn_ap = din("fdn", [NE * NFH * 128, FH * D], BF16)
    rw_h, rw_ap = din("rw", [D, 8])
    selm_h, selm_ap = din("selm", [16, N * 128])
    ident_h, ident_ap = din("ident", [128, 128])
    meta_h, meta_ap = din("meta", [1, 16], I32)
    flagval_h, flagval_ap = din("flagval", [1, NFLAG], I32)

    out_o_h = nc.dram_tensor("out_o", [Q, D], F32, kind="ExternalOutput")
    out_o = out_o_h.ap()
    DBGF = cfg.get("DBGF", False)
    if DBGF:
        dbgff_h = nc.dram_tensor("dbgff", [Q, D], F32, kind="ExternalOutput")
        dbgg_h = nc.dram_tensor("dbgg", [Q, 8], F32, kind="ExternalOutput")
        dbgh_h = nc.dram_tensor("dbgh", [F, ST], BF16, kind="ExternalOutput")

    xn2fm_h = nc.dram_tensor("xn2fm", [D, T], BF16)
    ffnout_h = nc.dram_tensor("ffnout", [Q, D], F32)
    xdsl_h = nc.dram_tensor("xdsl", [2 * NCH * 96 + 128, TC], F32, addr_space="Shared")
    ysl_h = nc.dram_tensor("ysl", [4 * EH + 128, H], BF16, addr_space="Shared")
    flags_h = nc.dram_tensor("flags", [NFLAG * 2 + 128, 16], I32, addr_space="Shared")
    xdidx_h, xdidx_ap = din("xdidx", [128, NCH * 6], U32)
    yslidx_h, yslidx_ap = din("yslidx", [128, 2 * SL], U32)
    yrdidx_h, yrdidx_ap = din("yrdidx", [128, 4 * SL * NGO], U32)
    flgidx_h, flgidx_ap = din("flgidx", [128, NFLAG], U32)
    flgsrc_h, flgsrc_ap = din("flgsrc", [2, NFLAG * 16], I32)

    for s in ("dS", "dO", "dG", "sPE", "sACT", "sDVE", "sPOOL", "sBAR", "sJOIN"):
        K.sem(s)

    meta_t = K.sb("meta_t", [1, 16], I32)
    flagval_t = K.sb("flagval_t", [1, NFLAG], I32)
    flagrd = K.sb("flagrd", [1, 2], I32)
    xdidx_t = K.sb("xdidx_t", [128, NCH * 6], U32)
    yslidx_t = K.sb("yslidx_t", [128, 2 * SL], U32)
    yrdidx_t = K.sb("yrdidx_t", [128, 4 * SL * NGO], U32)
    flgidx_t = K.sb("flgidx_t", [128, NFLAG], U32)
    flgsrc_t = K.sb("flgsrc_t", [2, NFLAG * 16], I32)
    K._bar_tiles = (flags_h, flagval_t, flagrd, flgidx_t, flgsrc_t)
    K._bar_slot = 0
    K._nflag = NFLAG
    K._ds_reg = None

    identb_t = K.sb("identb_t", [128, 128], BF16)
    identf_t = K.sb("identf_t", [128, 128])
    lnw_t = K.sb("lnw_t", [128, D])
    lnb_t = K.sb("lnb_t", [128, D])
    selm_t = K.sb("selm_t", [16, N * 128], BF16)
    bconv_t = K.sb("bconv_t", [128, 2 * SL])
    bdt_t = K.sb("bdt_t", [128, 2 * SL])
    apos_t = K.sb("apos_t", [128, 2 * SL * N])
    dvec_t = K.sb("dvec_t", [128, 2 * SL])
    wx_st = K.sb("wx_st", [128, 2 * SL * 96], BF16)
    wdt_st = K.sb("wdt_st", [64, 2 * EH])
    stat_c = K.sb("stat_c", [128, NGA])
    stat_ob = K.sb("stat_ob", [128, NGA])
    stato_r = K.sb("stato_r", [128, NGO])
    stato_b = K.sb("stato_b", [128, NGO])
    ones_t = K.sb("ones_t", [128, 1])
    eps_t = K.sb("eps_t", [128, 1])
    carr = K.sb("carr", [128, 2 * SL * N])

    t_meta = K.dma("sync", meta_t[:], meta_ap)
    K.dma("sync", flagval_t[:], flagval_ap)
    t_id = K.dma("gpsimd", identb_t[:], ident_ap, "dG")
    K.dma("sync", identf_t[:], ident_ap)
    K.dma("sync", lnw_t[:], lnw_ap)
    t_lnb = K.dma("sync", lnb_t[:], lnb_ap)
    K.dma("gpsimd", selm_t[:], selm_ap, "dG")
    K.dma("sync", bconv_t[:], bconv_ap)
    K.dma("sync", bdt_t[:], bdt_ap)
    t_al = K.dma("sync", apos_t[:], alog_ap)
    K.dma("sync", dvec_t[:], dvec_ap)
    t_wx = K.dma("gpsimd", wx_st[:].rearrange("p (s c) -> p s c", s=2 * SL),
                 bass.AP(wxT_h, 0, [[96, 128], [128 * 96, 2 * SL], [1, 96]]), "dG")
    t_wdt = K.dma("sync", wdt_st[:], wdtT_ap)
    K.dma("sync", xdidx_t[:], xdidx_ap)
    K.dma("sync", yslidx_t[:], yslidx_ap)
    K.dma("sync", yrdidx_t[:], yrdidx_ap)
    tfi = K.dma("sync", flgidx_t[:], flgidx_ap)
    tfs = K.dma("sync", flgsrc_t[:], flgsrc_ap)
    K.waitv("gpsimd", tfs)
    K.waitv("gpsimd", tfi)

    zt = K.sb("zt", [128, 2048])
    tz = K.dve(lambda E: E.memset(zt[:], 0.0))
    K.waitv("gpsimd", tz)
    ZCH = 128 * 2048
    xd_elems = (2 * NCH * 96 + 128) * TC
    for o in range(0, xd_elems, ZCH):
        n = min(ZCH, xd_elems - o)
        nr, ncol = (128, n // 128) if n % 128 == 0 and n >= 128 else (1, n)
        K.dma("gpsimd", bass.AP(xdsl_h, o, [[ncol, nr], [1, ncol]]),
              zt[0:nr, 0:ncol], "dG")
    ys_elems = (4 * EH + 128) * H
    ztb = zt[:].bitcast(BF16)
    for o in range(0, ys_elems, 2 * ZCH):
        n = min(2 * ZCH, ys_elems - o)
        nr, ncol = (128, n // 128) if n % 128 == 0 and n >= 128 else (1, n)
        K.dma("gpsimd", bass.AP(ysl_h, o, [[ncol, nr], [1, ncol]]),
              ztb[0:nr, 0:ncol], "dG")
    K.barrier([K.tok("dG")])
    K.join()
    K.free(zt)
    K.waitv("scalar", t_al)
    K.act(lambda E: E.activation(apos_t[:], apos_t[:], AF.Exp))
    K.waitv("vector", t_meta)
    K.dve(lambda E: E.memset(carr[:], 0.0))
    K.dve(lambda E: E.memset(ones_t[:], 1.0))
    tmemc = K.dve(lambda E: E.memset(eps_t[:], float(EPS)))
    K.waitv("scalar", tmemc)

    g_x = K.sb("g_x", [128, D])
    g_t1 = K.sb("g_t1", [128, D])
    g_t2 = K.sb("g_t2", [128, D], BF16)
    g_a1 = K.sb("g_a1", [128, NGA])
    g_a2 = K.sb("g_a2", [128, NGA])
    trbuf = K.sb("trbuf", [128, DC * GB * 128], BF16)
    PT32 = K.psum("PT32", [128, 128])
    PTRB = K.psum("PTRB", [128, 128], BF16)

    def ln_stats(x_ap, ngroups, acc1, acc2, prev=None):
        last = prev
        for g in range(ngroups):
            K.waitv("sync", last)
            t_x = K.dma("sync", g_x[:], x_ap[g * 128:(g + 1) * 128, :])
            K.waitv("scalar", t_x)
            K.act(lambda E, g=g: E.activation(g_t1[:], g_x[:], AF.Identity,
                                              accum_out=acc1[:, g:g + 1]))
            last = K.act(lambda E, g=g: E.activation(g_t1[:], g_x[:], AF.Square,
                                                     accum_out=acc2[:, g:g + 1]))
        return last

    def finalize_stats(acc1, acc2, out_scale, out_bias, ngroups, combined):
        a1 = acc1[:, 0:ngroups]
        a2 = acc2[:, 0:ngroups]
        sc = out_scale[:, 0:ngroups]
        ob = out_bias[:, 0:ngroups]
        K.dve(lambda E: E.tensor_scalar_mul(a1, a1, 1.0 / D))
        K.dve(lambda E: E.tensor_scalar_mul(a2, a2, 1.0 / D))
        K.selfe("vector")
        K.dve(lambda E: E.tensor_tensor(sc, a1, a1, ALU.mult))
        K.selfe("vector")
        tv = K.dve(lambda E: E.tensor_tensor(a2, a2, sc, ALU.subtract))
        K.waitv("scalar", tv)
        K.act(lambda E: E.activation(sc, a2, AF.Ln, bias=eps_t[:]))
        K.selfe("scalar")
        tr = K.act(lambda E: E.activation(sc, sc, AF.Exp, scale=-0.5))
        if combined:
            K.waitv("vector", tr)
            K.dve(lambda E: E.tensor_tensor(ob, sc, sc, ALU.mult))
            K.selfe("vector")
            tv2 = K.dve(lambda E: E.tensor_tensor(ob, a2, ob, ALU.mult))
            K.waitv("scalar", tv2)
            K.act(lambda E: E.activation(ob, ob, AF.Ln, bias=eps_t[:]))
            K.selfe("scalar")
            tr2 = K.act(lambda E: E.activation(ob, ob, AF.Exp, scale=-0.5))
            K.waitv("vector", tr2)
            K.dve(lambda E: E.tensor_tensor(sc, sc, ob, ALU.mult))
        else:
            K.waitv("vector", tr)
        K.selfe("vector")
        K.dve(lambda E: E.tensor_tensor(ob, a1, sc, ALU.mult))
        K.selfe("vector")
        return K.dve(lambda E: E.tensor_scalar_mul(ob, ob, -1.0))

    tq = ln_stats(x_b, NGA, g_a1, g_a2)
    K.waitv("vector", tq)
    tst = finalize_stats(g_a1, g_a2, stat_c, stat_ob, NGA, True)
    K.waitv("scalar", tst)
    tq2 = ln_stats(x_o, NGO, g_a1, g_a2, prev=tq)
    K.waitv("vector", tq2)
    tst2 = finalize_stats(g_a1, g_a2, stato_r, stato_b, NGO, False)

    K.waitv("scalar", tst2)
    K.waitv("tensor", t_id)
    nblk = (NGA + GB - 1) // GB
    tr_toks = [None, None]
    wr_tok = None
    prev_x2 = tst2
    prev_ttr = None
    for blk in range(nblk):
        gs = list(range(blk * GB, min((blk + 1) * GB, NGA)))
        K.waitv("scalar", wr_tok)
        for gi, g in enumerate(gs):
            K.waitv("sync", prev_x2)
            t_x = K.dma("sync", g_x[:], x_b[g * 128:(g + 1) * 128, :])
            K.waitv("scalar", t_x)
            tc_ = K.act(lambda E, g=g: E.activation(g_t1[:], g_x[:], AF.Identity,
                                                    scale=stat_c[:, g:g + 1],
                                                    bias=stat_ob[:, g:g + 1]))
            K.waitv("vector", tc_)
            K.waitv("vector", prev_ttr)
            K.dve(lambda E: E.tensor_tensor(g_t1[:], g_t1[:], lnw_t[:], ALU.mult))
            K.selfe("vector")
            tx2 = K.dve(lambda E: E.tensor_tensor(g_t2[:], g_t1[:], lnb_t[:], ALU.add))
            prev_x2 = tx2
            K.waitv("tensor", tx2)
            for dc in range(DC):
                K.waitv("tensor", tr_toks[1])
                ttr = K.pe(lambda E, dc=dc: E.transpose(
                    PTRB[:], g_t2[:, dc * 128:(dc + 1) * 128], identb_t[:]))
                prev_ttr = ttr
                K.waitv("scalar", ttr)
                tev = K.act(lambda E, dc=dc, gi=gi: E.copy(
                    trbuf[:, (dc * GB + gi) * 128:(dc * GB + gi) * 128 + 128], PTRB[:]))
                tr_toks = [tr_toks[1], ("sACT", tev[1])]
        t0 = blk * GB * 128
        ncols = len(gs) * 128
        K.waitv("sync", tr_toks[0])
        K.waitv("sync", tr_toks[1])
        wr_tok = K.dma("sync", bass.AP(xn2fm_h, t0, [[T, 128], [128 * T, DC], [1, ncols]]),
                       trbuf[:, 0:DC * ncols].rearrange("p (dc c) -> p dc c", dc=DC))
    K.join()
    for t in reversed((g_x, g_t1, g_t2, g_a1, g_a2, trbuf)):
        K.free(t)
    K.free(PTRB)

    f_xt = K.sb("f_xt", [128, D])
    f_xn = K.sb("f_xn", [128, D])
    f_xnb = K.sb("f_xnb", [128, DC * ST], BF16)
    f_xnf = K.sb("f_xnf", [128, DC * 128])
    f_rw = K.sb("f_rw", [128, DC * 8])
    f_lg = K.sb("f_lg", [128, 8])
    f_g1 = K.sb("f_g1", [128, 8])
    f_g2 = K.sb("f_g2", [128, 8])
    f_m = K.sb("f_m", [128, 1])
    f_gate = K.sb("f_gate", [128, STG * 8])
    f_wu = [K.sb(f"f_wu{i}", [128, DC * 128], BF16) for i in range(2)]
    f_wdh = [K.sb(f"f_wdh{i}", [128, FH * D], BF16) for i in range(2)]
    f_h = K.sb("f_h", [128, FC * ST], BF16)
    f_acc = K.sb("f_acc", [128, STG * D])
    PU = [K.psum(f"PU{i}", [128, MT]) for i in range(2)]
    PDW = [K.psum(f"PDW{i}", [128, D]) for i in range(2)]
    PRT = K.psum("PRT", [128, 8])

    t_rw = K.dma("sync", f_rw[:].rearrange("p (dc c) -> p dc c", dc=DC),
                 bass.AP(rw_h, 0, [[8, 128], [128 * 8, DC], [1, 8]]))
    K.waitv("tensor", t_rw)

    tff = None
    fwdh_free = [None, None]
    for st in range(NST):
        K.waitv("vector", tff)
        tacc0 = K.dve(lambda E: E.memset(f_acc[:], 0.0))
        tr_toks = [None, None]
        tgate = None
        prev_xn = None
        for tg in range(STG):
            g = st * STG + tg
            K.waitv("sync", prev_xn)
            t_x = K.dma("sync", f_xt[:], x_o[g * 128:(g + 1) * 128, :])
            K.waitv("scalar", t_x)
            K.waitv("scalar", tr_toks[0])
            K.waitv("scalar", tr_toks[1])
            txn = K.act(lambda E, g=g: E.activation(f_xn[:], f_xt[:], AF.Identity,
                                                    scale=stato_r[:, g:g + 1],
                                                    bias=stato_b[:, g:g + 1]))
            prev_xn = txn
            K.waitv("tensor", txn)
            for dc in range(DC):
                K.waitv("tensor", tr_toks[1])
                ttr = K.pe(lambda E, dc=dc: E.transpose(
                    PT32[:], f_xn[:, dc * 128:(dc + 1) * 128], identf_t[:]))
                K.waitv("scalar", ttr)
                K.act(lambda E, dc=dc: E.copy(
                    f_xnf[:, dc * 128:(dc + 1) * 128], PT32[:]))
                tev = K.act(lambda E, dc=dc, tg=tg: E.copy(
                    f_xnb[:, dc * ST + tg * 128: dc * ST + tg * 128 + 128], PT32[:]))
                tr_toks = [tr_toks[1], ("sACT", tev[1])]
            K.waitv("tensor", tr_toks[1])
            for dc in range(DC):
                K.pe(lambda E, dc=dc: E.matmul(
                    PRT[:], f_xnf[:, dc * 128:(dc + 1) * 128],
                    f_rw[:, dc * 8:(dc + 1) * 8],
                    start=(dc == 0), stop=(dc == DC - 1)))
            trt = K.tok("sPE")
            K.waitv("vector", trt)
            K.dve(lambda E: E.tensor_copy(f_lg[:], PRT[:]))
            K.selfe("vector")
            K.dve(lambda E: E.tensor_reduce(f_m[:], f_lg[:], AX.X, ALU.max))
            K.selfe("vector")
            tnm = K.dve(lambda E: E.tensor_scalar_mul(f_m[:], f_m[:], -1.0))
            K.waitv("scalar", tnm)
            te = K.act(lambda E: E.activation(f_g1[:], f_lg[:], AF.Exp, bias=f_m[:]))
            K.waitv("vector", te)
            K.dve(lambda E: E.tensor_reduce(f_m[:], f_g1[:], AX.X, ALU.add))
            K.selfe("vector")
            K.dve(lambda E: E.reciprocal(f_m[:], f_m[:]))
            K.selfe("vector")
            K.dve(lambda E: E.tensor_scalar(f_g1[:], f_g1[:], f_m[:], None, ALU.mult))
            K.selfe("vector")
            K.dve(lambda E: E.tensor_reduce(f_m[:], f_g1[:], AX.X, ALU.max))
            K.selfe("vector")
            K.dve(lambda E: E.tensor_scalar(f_g2[:], f_g1[:], f_m[:], None, ALU.is_equal))
            K.selfe("vector")
            K.dve(lambda E: E.scalar_tensor_tensor(f_g2[:], f_g2[:], -1e30, f_g1[:],
                                                   ALU.mult, ALU.add))
            K.selfe("vector")
            K.dve(lambda E: E.tensor_reduce(f_m[:], f_g2[:], AX.X, ALU.max))
            K.selfe("vector")
            K.dve(lambda E: E.tensor_scalar(f_g2[:], f_g1[:], f_m[:], None, ALU.is_ge))
            K.selfe("vector")
            tgate = K.dve(lambda E, tg=tg: E.tensor_tensor(
                f_gate[:, tg * 8:(tg + 1) * 8], f_g1[:], f_g2[:], ALU.mult))

        if DBGF:
            K.waitv("sync", tgate)
            for tg in range(STG):
                K.dma("sync", bass.AP(dbgg_h, (st * ST + tg * 128) * 8, [[8, 128], [1, 8]]),
                      f_gate[:, tg * 8:(tg + 1) * 8])
        wu_free = [None, None]
        pu_free = [None, None]
        tdn_last = None
        for e in range(NE):
            K.waitv("scalar", tdn_last)
            K.waitv("scalar", tgate)
            for fc in range(FC):
                buf = fc % 2
                K.waitv("gpsimd", wu_free[buf])
                twu = K.dma("gpsimd", f_wu[buf][:],
                            fup_ap[(e * FC + fc) * 128:(e * FC + fc + 1) * 128, :], "dG")
                K.waitv("tensor", twu)
                for mt in range(NMT):
                    pu = PU[mt % 2]
                    K.waitv("tensor", pu_free[mt % 2])
                    for dc in range(DC):
                        K.pe(lambda E, dc=dc, mt=mt, buf=buf, pu=pu: E.matmul(
                            pu[:], f_wu[buf][:, dc * 128:(dc + 1) * 128],
                            f_xnb[:, dc * ST + mt * MT: dc * ST + mt * MT + MT],
                            start=(dc == 0), stop=(dc == DC - 1)))
                    tmm = K.tok("sPE")
                    K.waitv("scalar", tmm)
                    tgl = K.act(lambda E, fc=fc, mt=mt, pu=pu: E.activation(
                        f_h[:, fc * ST + mt * MT: fc * ST + mt * MT + MT],
                        pu[:], AF.Gelu))
                    pu_free[mt % 2] = ("sACT", tgl[1])
                wu_free[buf] = ("sPE", tmm[1])
            if DBGF and e == 0 and st == 0:
                K.waitv("sync", K.tok("sACT"))
                for fc in range(FC):
                    K.dma("sync", bass.AP(dbgh_h, fc * 128 * ST, [[ST, 128], [1, ST]]),
                          f_h[:, fc * ST:(fc + 1) * ST])
            tgl_last = K.tok("sACT")
            for fh in range(NFH):
                buf = fh % 2
                K.waitv("sync", fwdh_free[buf])
                twd = K.dma("sync", f_wdh[buf][:],
                            fdn_ap[(e * NFH + fh) * 128:(e * NFH + fh + 1) * 128, :], "dO")
                K.waitv("tensor", twd)
                K.waitv("tensor", tgl_last)
                if e > 0 or fh > 0:
                    K.waitv("tensor", tga_last)
                for tg in range(STG):
                    pd = PDW[tg % 2]
                    if tg >= 2:
                        K.waitv("tensor", tga_toks[tg - 2])
                    for fc in range(FH):
                        fcg = fh * FH + fc
                        for dm in range(D // 512 if D >= 512 else 1):
                            MD = min(D, 512)
                            K.pe(lambda E, fcg=fcg, fc=fc, tg=tg, dm=dm, pd=pd, MD=MD,
                                 buf=buf: E.matmul(
                                pd[:, dm * MD: dm * MD + MD],
                                f_h[:, fcg * ST + tg * 128: fcg * ST + tg * 128 + 128],
                                f_wdh[buf][:, fc * D + dm * MD: fc * D + dm * MD + MD],
                                start=(fc == 0), stop=(fc == FH - 1)))
                    tdn = K.tok("sPE")
                    K.waitv("vector", tdn)
                    if e == 0:
                        tga = K.dve(lambda E, tg=tg, pd=pd: E.scalar_tensor_tensor(
                            f_acc[:, tg * D: tg * D + D], pd[:], ones_t[:],
                            f_acc[:, tg * D: tg * D + D], ALU.mult, ALU.add))
                    else:
                        tga = K.dve(lambda E, tg=tg, e=e, pd=pd: E.scalar_tensor_tensor(
                            f_acc[:, tg * D: tg * D + D], pd[:],
                            f_gate[:, tg * 8 + e - 1: tg * 8 + e],
                            f_acc[:, tg * D: tg * D + D], ALU.mult, ALU.add))
                    if tg == 0:
                        tga_toks = {}
                    tga_toks[tg] = tga
                tga_last = tga
                tdn_last = tdn
                fwdh_free[buf] = tdn
        K.waitv("sync", tga_last)
        tff = K.dma("sync", bass.AP(ffnout_h, st * ST * D, [[D, 128], [128 * D, STG], [1, D]]),
                    f_acc[:].rearrange("p (tg d) -> p tg d", tg=STG))
    K.join()
    for t in reversed((f_xt, f_xn, f_xnb, f_xnf, f_rw, f_lg, f_g1, f_g2, f_m, f_gate,
                       f_wu[0], f_wu[1], f_wdh[0], f_wdh[1], f_h, f_acc)):
        K.free(t)
    for t in reversed((PU[0], PU[1], PDW[0], PDW[1], PRT)):
        K.free(t)

    winT_t = K.sb("winT_t", [128, DC * 2 * EH], BF16)
    convd_t = K.sb("convd_t", [128, 4 * SL * 128], BF16)
    xn2c = K.sb("xn2c", [128, DC * TC], BF16)
    xp_t = K.sb("xp_t", [128, SL * (TC + 4)], BF16)
    zs_t = K.sb("zs_t", [128, SL * TC], BF16)
    xc_t = K.sb("xc_t", [128, SL * TC], BF16)
    ndt_t = K.sb("ndt_t", [128, SL * TC], BF16)
    w_t = K.sb("w_t", [128, SL * TC], BF16)
    sp1 = K.sb("sp1", [128, TC])
    sp2 = K.sb("sp2", [128, TC])
    dtrmy = K.sb("dtrmy", [64, TC])
    dtrpr = K.sb("dtrpr", [64, TC])
    dtrs = K.sb("dtrs", [64, TC])
    bmy = K.sb("bmy", [16, TC])
    bpr = K.sb("bpr", [16, TC])
    cmy = K.sb("cmy", [16, TC])
    cpr = K.sb("cpr", [16, TC])
    bsb = K.sb("bsb", [16, TC], BF16)
    csb = K.sb("csb", [16, TC], BF16)
    brep = [K.sb(f"brep{i}", [128, TC], BF16) for i in range(2)]
    crep = [K.sb(f"crep{i}", [128, TC], BF16) for i in range(2)]
    a_t = [K.sb(f"a_t{i}", [128, TC]) for i in range(2)]
    b_t = [K.sb(f"b_t{i}", [128, TC], BF16) for i in range(2)]
    h_t = [K.sb(f"h_t{i}", [128, TC], BF16) for i in range(2)]
    tmp_t = K.sb("tmp_t", [128, TC], BF16)
    yacc = K.sb("yacc", [128, SL * TC])
    ybf = [K.sb(f"ybf{i}", [128, TC], BF16) for i in range(2)]
    PZ = [K.psum(f"PZ{i}", [128, TC]) for i in range(2)]
    PCV = K.psum("PCV", [128, TC])
    PXD = K.psum("PXD", [128, TC])
    PXB = K.psum("PXB", [128, TC])
    PREP = K.psum("PREP", [128, TC])

    a_free = [None, None]
    b_free = [None, None]
    ybf_free = [None, None]
    carry_tok = {}
    ysl_toks = []

    chunks = [("B", k) for k in range(NCHB - 1, -1, -1)] + [("A", k) for k in range(NCHA)]
    cur_dir = None

    for ci, (dr, k) in enumerate(chunks):
        d = 1 if dr == "B" else 0
        t0 = k * TC
        owned = (dr == "A") or (t0 < H)
        first = (dr == "B" and k == NCHB - 1) or (dr == "A" and k == 0)
        slot = ci
        K.join()

        if cur_dir != d:
            twl = K.dma("gpsimd", winT_t[:].rearrange("p (dc c) -> p dc c", dc=DC),
                        bass.AP(winT_h, d * 2 * EH,
                                [[4 * EH, 128], [128 * 4 * EH, DC], [1, 2 * EH]]), "dG")
            tcl = K.dma("gpsimd", convd_t[:],
                        convd[:, d * 4 * SL * 128:(d + 1) * 4 * SL * 128], "dG")
            cur_dir = d
            K.waitv("tensor", twl)
            K.waitv("tensor", tcl)

        txn = K.dma("sync", xn2c[:].rearrange("p (dc c) -> p dc c", dc=DC),
                    bass.AP(xn2fm_h, t0, [[T, 128], [128 * T, DC], [1, TC]]))
        if first:
            tst_ = K.dve(lambda E: E.memset(xp_t[:], 0.0))
        else:
            for i in range(SL):
                if dr == "A":
                    tst_ = K.dve(lambda E, i=i: E.tensor_copy(
                        xp_t[:, i * (TC + 4): i * (TC + 4) + 3],
                        xp_t[:, i * (TC + 4) + TC: i * (TC + 4) + TC + 3]))
                else:
                    tst_ = K.dve(lambda E, i=i: E.tensor_copy(
                        xp_t[:, i * (TC + 4) + TC: i * (TC + 4) + TC + 3],
                        xp_t[:, i * (TC + 4): i * (TC + 4) + 3]))
        K.waitv("scalar", tst_)
        K.waitv("tensor", txn)

        off0 = 3 if dr == "A" else 0
        pz_ev = [None, None]
        last_ev = None
        nxz = 2 * SL if owned else SL
        for j in range(nxz):
            i = j % SL
            isz = j >= SL
            pz = PZ[j % 2]
            K.waitv("tensor", pz_ev[j % 2])
            for dc in range(DC):
                col = dc * 2 * EH + (EH if isz else 0) + i * 128
                K.pe(lambda E, col=col, dc=dc, pz=pz: E.matmul(
                    pz[:], winT_t[:, col: col + 128],
                    xn2c[:, dc * TC:(dc + 1) * TC],
                    start=(dc == 0), stop=(dc == DC - 1)))
            tm = K.tok("sPE")
            K.waitv("scalar", tm)
            if not isz:
                last_ev = K.act(lambda E, i=i, pz=pz, off0=off0: E.copy(
                    xp_t[:, i * (TC + 4) + off0: i * (TC + 4) + off0 + TC], pz[:]))
            else:
                last_ev = K.act(lambda E, i=i, pz=pz: E.activation(
                    zs_t[:, i * TC:(i + 1) * TC], pz[:], AF.Silu))
            pz_ev[j % 2] = last_ev

        K.waitv("tensor", last_ev)
        txc = None
        for i in range(SL):
            if txc is not None:
                K.waitv("tensor", txc)
            for tap in range(4):
                off = tap if dr == "A" else (3 - tap)
                K.pe(lambda E, i=i, tap=tap, off=off: E.matmul(
                    PCV[:], convd_t[:, (tap * SL + i) * 128:(tap * SL + i) * 128 + 128],
                    xp_t[:, i * (TC + 4) + off: i * (TC + 4) + off + TC],
                    start=(tap == 0), stop=(tap == 3)))
            tcv = K.tok("sPE")
            K.waitv("scalar", tcv)
            txc = K.act(lambda E, i=i, d=d: E.activation(
                xc_t[:, i * TC:(i + 1) * TC], PCV[:], AF.Silu,
                bias=bconv_t[:, d * SL + i: d * SL + i + 1]))

        K.waitv("tensor", txc)
        for grp, (pj, c0, cw) in enumerate(((PXD, 0, 64), (PXB, 64, 16), (PCV, 80, 16))):
            for i in range(SL):
                K.pe(lambda E, i=i, pj=pj, c0=c0, cw=cw, d=d: E.matmul(
                    pj[0:cw, :], wx_st[:, (d * SL + i) * 96 + c0:(d * SL + i) * 96 + c0 + cw],
                    xc_t[:, i * TC:(i + 1) * TC],
                    start=(i == 0), stop=(i == SL - 1)))
        txd = K.tok("sPE")
        K.waitv("scalar", txd)
        K.act(lambda E: E.copy(dtrmy[:], PXD[0:64, :]))
        K.act(lambda E: E.copy(bmy[:], PXB[0:16, :]))
        tevc = K.act(lambda E: E.copy(cmy[:], PCV[0:16, :]))
        K.waitv("gpsimd", tevc)
        xbufs = (dtrmy, bmy, cmy)
        rbufs = (dtrpr, bpr, cpr)
        v3 = None
        for kk in range(3):
            nr = (64, 16, 16)[kk]
            v3 = K.idma(lambda E, s, kk=kk, slot=slot, nr=nr: E.indirect_dma_start(
                out=xdsl_h.ap()[:, :],
                out_offset=bass.IndirectOffsetOnAxis(
                    xdidx_t[0:nr, slot * 6 + kk: slot * 6 + kk + 1], 0),
                in_=xbufs[kk][:], in_offset=None).then_inc(s, 16))
        tbar = K.barrier([v3])
        v6 = None
        for kk in range(3):
            nr = (64, 16, 16)[kk]
            v6 = K.idma(lambda E, s, kk=kk, slot=slot, nr=nr: E.indirect_dma_start(
                out=rbufs[kk][:], out_offset=None,
                in_=xdsl_h.ap()[:, :],
                in_offset=bass.IndirectOffsetOnAxis(
                    xdidx_t[0:nr, slot * 6 + 3 + kk: slot * 6 + 3 + kk + 1], 0)).then_inc(s, 16))
        K.waitv("vector", v6)
        K.dve(lambda E: E.tensor_tensor(dtrs[:], dtrmy[:], dtrpr[:], ALU.add))
        tbs = K.dve(lambda E: E.tensor_tensor(bsb[:], bmy[:], bpr[:], ALU.add))
        tcs = K.dve(lambda E: E.tensor_tensor(csb[:], cmy[:], cpr[:], ALU.add))

        K.waitv("tensor", tcs)
        prev_sp = None
        prev_nd = None
        for i in range(SL):
            K.waitv("tensor", prev_sp)
            K.pe(lambda E, i=i, d=d: E.matmul(
                PXD[:], wdt_st[:, d * EH + i * 128: d * EH + i * 128 + 128],
                dtrs[:], start=True, stop=True))
            tdt = K.tok("sPE")
            K.waitv("scalar", tdt)
            K.waitv("scalar", prev_nd)
            K.act(lambda E, i=i, d=d: E.activation(sp1[:], PXD[:], AF.Abs,
                                                   bias=bdt_t[:, d * SL + i: d * SL + i + 1]))
            K.act(lambda E, i=i, d=d: E.activation(sp2[:], PXD[:], AF.Relu,
                                                   bias=bdt_t[:, d * SL + i: d * SL + i + 1]))
            K.selfe("scalar")
            K.act(lambda E: E.activation(sp1[:], sp1[:], AF.Exp, scale=-1.0))
            K.selfe("scalar")
            tsp = K.act(lambda E: E.activation(sp1[:], sp1[:], AF.Ln, bias=1.0))
            prev_sp = tsp
            K.waitv("vector", tsp)
            tnd = K.dve(lambda E, i=i: E.scalar_tensor_tensor(
                ndt_t[:, i * TC:(i + 1) * TC], sp1[:], -1.0, sp2[:], ALU.mult, ALU.subtract))
            K.selfe("vector")
            tw = K.dve(lambda E, i=i: E.tensor_tensor(
                w_t[:, i * TC:(i + 1) * TC], ndt_t[:, i * TC:(i + 1) * TC],
                xc_t[:, i * TC:(i + 1) * TC], ALU.mult))
            prev_nd = tw
        tndt = K.tok("sDVE")

        K.waitv("scalar", tndt)
        K.waitv("gpsimd", tndt)
        rep_prev = [None, None]
        prep_guard = None
        h_free = [(), ()]
        it = 0
        y_last = None
        pend_car = None
        beng = "vector" if owned else "gpsimd"

        def flush_car():
            pidx, plc, phb, ptts = pend_car
            K.waitv("scalar", ptts)
            tcar = K.act(lambda E, idx=pidx, lastcol=plc, hb=phb: E.copy(
                carr[:, idx:idx + 1], h_t[hb][:, lastcol:lastcol + 1]))
            carry_tok[pidx] = tcar
            h_free[phb] = tuple(h_free[phb]) + (tcar,)
        for n in range(N):
            p = n % 2
            for tkr in rep_prev[p] or ():
                K.waitv("tensor", tkr)
            K.waitv("tensor", prep_guard)
            trb = K.pe(lambda E, n=n: E.matmul(
                PREP[:], selm_t[:, n * 128:(n + 1) * 128], bsb[:], start=True, stop=True))
            K.waitv("scalar", trb)
            tbe = K.act(lambda E, p=p: E.copy(brep[p][:], PREP[:]))
            if owned:
                K.waitv("tensor", tbe)
                trc = K.pe(lambda E, n=n: E.matmul(
                    PREP[:], selm_t[:, n * 128:(n + 1) * 128], csb[:], start=True, stop=True))
                K.waitv("scalar", trc)
                tce = K.act(lambda E, p=p: E.copy(crep[p][:], PREP[:]))
                prep_guard = tce
            else:
                prep_guard = tbe
            b_last = None
            mul_last = None
            for i in range(SL):
                ap_ = i % 2
                hb = it % 2
                idx = d * SL * N + i * N + n
                K.waitv("scalar", a_free[ap_])
                ta = K.act(lambda E, i=i, n=n, d=d, ap_=ap_: E.activation(
                    a_t[ap_][:], ndt_t[:, i * TC:(i + 1) * TC], AF.Exp,
                    scale=apos_t[:, d * SL * N + i * N + n: d * SL * N + i * N + n + 1]))
                a_free[ap_] = None
                if pend_car is not None:
                    flush_car()
                    pend_car = None
                K.waitv(beng, b_free[ap_])
                K.waitv(beng, tbe)
                bfn = K.dve if owned else K.pool
                tb = bfn(lambda E, i=i, ap_=ap_, p=p: E.tensor_tensor(
                    b_t[ap_][:], w_t[:, i * TC:(i + 1) * TC], brep[p][:], ALU.mult))
                b_free[ap_] = None
                b_last = tb
                K.waitv("vector", ta)
                K.waitv("vector", tb)
                K.waitv("vector", carry_tok.get(idx))
                for tkr in h_free[hb]:
                    K.waitv("vector", tkr)
                if dr == "A":
                    tts = K.dve(lambda E, ap_=ap_, idx=idx, hb=hb: E.tensor_tensor_scan(
                        h_t[hb][:], a_t[ap_][:], b_t[ap_][:], carr[:, idx:idx + 1],
                        ALU.mult, ALU.add))
                    lastcol = TC - 1
                else:
                    tts = K.dve(lambda E, ap_=ap_, idx=idx, hb=hb: E.tensor_tensor_scan(
                        h_t[hb][:, ::-1], a_t[ap_][:, ::-1], b_t[ap_][:, ::-1],
                        carr[:, idx:idx + 1], ALU.mult, ALU.add))
                    lastcol = 0
                a_free[ap_] = ("sDVE", tts[1])
                b_free[ap_] = ("sDVE", tts[1])
                pend_car = (idx, lastcol, hb, tts)
                readers = []
                if owned:
                    K.waitv("gpsimd", tts)
                    K.waitv("gpsimd", tce)
                    K.selfe("gpsimd")
                    tmul = K.pool(lambda E, p=p, hb=hb: E.tensor_tensor(
                        tmp_t[:], h_t[hb][:], crep[p][:], ALU.mult))
                    K.selfe("gpsimd")
                    if n == 0:
                        ty = K.pool(lambda E, i=i: E.tensor_copy(
                            yacc[:, i * TC:(i + 1) * TC], tmp_t[:]))
                    else:
                        ty = K.pool(lambda E, i=i: E.tensor_tensor(
                            yacc[:, i * TC:(i + 1) * TC], yacc[:, i * TC:(i + 1) * TC],
                            tmp_t[:], ALU.add))
                    readers.append(tmul)
                    mul_last = tmul
                    y_last = ty
                h_free[hb] = readers
                it += 1
            if owned:
                rep_prev[p] = (("sDVE", b_last[1]), mul_last)
            else:
                rep_prev[p] = (("sPOOL", b_last[1]),)
        if pend_car is not None:
            flush_car()
            pend_car = None

        if owned:
            K.waitv("vector", y_last)
            for i in range(SL):
                yb = ybf[i % 2]
                K.waitv("vector", ybf_free[i % 2])
                K.dve(lambda E, i=i, d=d: E.scalar_tensor_tensor(
                    sp1[:], xc_t[:, i * TC:(i + 1) * TC], dvec_t[:, d * SL + i: d * SL + i + 1],
                    yacc[:, i * TC:(i + 1) * TC], ALU.mult, ALU.subtract))
                K.selfe("vector")
                tyf = K.dve(lambda E, i=i, yb=yb: E.tensor_tensor(
                    yb[:], sp1[:], zs_t[:, i * TC:(i + 1) * TC], ALU.mult))
                K.waitv("gpsimd", tyf)
                vy = K.idma(lambda E, s, yb=yb, i=i, d=d, t0=t0:
                            E.indirect_dma_start(
                                out=ysl_h.ap()[:, :],
                                out_offset=bass.IndirectOffsetOnAxis(
                                    yslidx_t[:, d * SL + i: d * SL + i + 1], 0),
                                in_=yb[:], in_offset=None,
                                element_offset=t0).then_inc(s, 16))
                ysl_toks.append(vy)
                ybf_free[i % 2] = vy

    tyb = K.barrier(ysl_toks[-1:])
    K.join()
    for t in reversed((winT_t, convd_t, xn2c, xp_t, zs_t, xc_t, ndt_t, w_t, sp1, sp2,
                       dtrmy, dtrpr, dtrs, bmy, bpr, cmy, cpr, bsb, csb,
                       brep[0], brep[1], crep[0], crep[1], a_t[0], a_t[1],
                       b_t[0], b_t[1], h_t[0], h_t[1], tmp_t, yacc, ybf[0], ybf[1])):
        K.free(t)
    for t in reversed((PZ[0], PZ[1], PCV, PXD, PXB, PREP)):
        K.free(t)

    p3_y = K.sb("p3_y", [128, 2 * EC * 128], BF16)
    p3_w = K.sb("p3_w", [128, 2 * EC * D], BF16)
    p3_x = K.sb("p3_x", [128, D])
    p3_n = K.sb("p3_n", [128, D])
    p3_f = K.sb("p3_f", [128, D])
    p3_o = K.sb("p3_o", [128, D])
    PP3 = [K.psum(f"PP3{i}", [128, D]) for i in range(2)]

    if DBGF:
        for qrow in range(0, Q, 128):
            K.dma("sync", bass.AP(dbgff_h, qrow * D, [[D, 128], [1, D]]),
                  bass.AP(ffnout_h, qrow * D, [[D, 128], [1, D]]))
    twout = K.dma("gpsimd", p3_w[:].rearrange("p (dc c) -> p dc c", dc=2 * EC),
                  bass.AP(woutT_h, 0, [[D, 128], [128 * D, 2 * EC], [1, D]]), "dG")
    K.waitv("tensor", twout)

    out_toks = []
    pp_free = [None, None]
    prev_fin = None
    prev_out = None
    prev_tpw = None
    for tg in range(NGO):
        ysl_flat = bass.AP(ysl_h, 0, [[128, (4 * EH + 128) * H // 128], [1, 128]])
        K.waitv("gpsimd", prev_tpw)
        tydma = None
        for ehs in range(2):
            for dd in range(2):
                for i in range(SL):
                    col = ((dd * 2 + ehs) * SL + i) * NGO + tg
                    dst = (dd * EC + ehs * SL + i) * 128
                    tydma = K.idma(lambda E, s, col=col, dst=dst:
                                   E.indirect_dma_start(
                                       out=p3_y[:, dst:dst + 128], out_offset=None,
                                       in_=ysl_flat,
                                       in_offset=bass.IndirectOffsetOnAxis(
                                           yrdidx_t[:, col:col + 1], 0)).then_inc(s, 16))
        pp = PP3[tg % 2]
        K.waitv("tensor", pp_free[tg % 2])
        K.waitv("tensor", tydma)
        nec = 2 * EC
        for ecg in range(nec):
            for dm in range(max(1, D // 512)):
                MD = min(D, 512)
                K.pe(lambda E, ecg=ecg, dm=dm, pp=pp, MD=MD: E.matmul(
                    pp[:, dm * MD: dm * MD + MD],
                    p3_y[:, ecg * 128:(ecg + 1) * 128],
                    p3_w[:, ecg * D + dm * MD: ecg * D + dm * MD + MD],
                    start=(ecg == 0), stop=(ecg == nec - 1)))
        tpw = K.tok("sPE")
        prev_tpw = tpw
        K.waitv("sync", prev_fin)
        t_x = K.dma("sync", p3_x[:], x_o[tg * 128:(tg + 1) * 128, :])
        t_f = K.dma("sync", p3_f[:], bass.AP(ffnout_h, tg * 128 * D, [[D, 128], [1, D]]))
        K.waitv("scalar", t_x)
        txn = K.act(lambda E, tg=tg: E.activation(p3_n[:], p3_x[:], AF.Identity,
                                                  scale=stato_r[:, tg:tg + 1],
                                                  bias=stato_b[:, tg:tg + 1]))
        K.waitv("vector", txn)
        K.waitv("vector", t_f)
        K.waitv("vector", tpw)
        K.waitv("vector", prev_out)
        K.dve(lambda E: E.scalar_tensor_tensor(p3_o[:], p3_n[:], 2.0, p3_x[:],
                                               ALU.mult, ALU.add))
        K.selfe("vector")
        K.dve(lambda E: E.tensor_tensor(p3_o[:], p3_o[:], p3_f[:], ALU.add))
        K.selfe("vector")
        tfin = K.dve(lambda E, pp=pp: E.scalar_tensor_tensor(
            p3_o[:], pp[:], 0.5, p3_o[:], ALU.mult, ALU.add))
        pp_free[tg % 2] = ("sDVE", tfin[1])
        prev_fin = tfin
        K.waitv("sync", tfin)
        tout = K.dma("sync", out_o[tg * 128:(tg + 1) * 128, :], p3_o[:])
        prev_out = tout
        out_toks.append(tout)

    K.waitv("sync", out_toks[-1])
    K.join()

    with K.stack:
        with nc.Block() as block:
            @block.sync
            def _(E):
                for f in K.ops["sync"]:
                    f(E)

            @block.tensor
            def _(E):
                for f in K.ops["tensor"]:
                    f(E)

            @block.vector
            def _(E):
                for f in K.ops["vector"]:
                    f(E)

            @block.scalar
            def _(E):
                for f in K.ops["scalar"]:
                    f(E)

            @block.gpsimd
            def _(E):
                for f in K.ops["gpsimd"]:
                    f(E)
    return nc



INPUT_NAMES = (
    "L0_attn_x", "L0_attn_ln_w", "L0_attn_ln_b",
    "L0_attn_f_W_in", "L0_attn_f_W_conv", "L0_attn_f_b_conv", "L0_attn_f_W_x",
    "L0_attn_f_W_dt", "L0_attn_f_b_dt", "L0_attn_f_A_log", "L0_attn_f_D", "L0_attn_f_W_out",
    "L0_attn_b_W_in", "L0_attn_b_W_conv", "L0_attn_b_b_conv", "L0_attn_b_W_x",
    "L0_attn_b_W_dt", "L0_attn_b_b_dt", "L0_attn_b_A_log", "L0_attn_b_D", "L0_attn_b_W_out",
    "L0_ffn_shared_up", "L0_ffn_shared_down", "L0_ffn_routed_up", "L0_ffn_routed_down",
    "L0_ffn_router_w")


_PACK_CACHE = {}


def _pack_ffn(cfg, I):
    D, F, NE = cfg["D"], cfg["F"], cfg["NE"]
    key = (id(I["L0_ffn_routed_up"]), NE, F, D)
    if key in _PACK_CACHE:
        return _PACK_CACHE[key]
    FC, DC = F // 128, D // 128
    NFH = _nfh(FC)
    FH = FC // NFH
    fup0 = np.concatenate([I["L0_ffn_shared_up"], I["L0_ffn_routed_up"]], 0)[:NE]
    t = np.asarray(fup0, np.float32).reshape(NE, DC, 128, FC, 128)
    fupb = np.ascontiguousarray(t.transpose(0, 3, 2, 1, 4)).reshape(
        NE * FC * 128, DC * 128).astype(BF16NP)
    fdn0 = np.concatenate([I["L0_ffn_shared_down"], I["L0_ffn_routed_down"]], 0)[:NE]
    t2 = np.asarray(fdn0, np.float32).reshape(NE, NFH, FH, 128, D)
    fdnb = np.ascontiguousarray(t2.transpose(0, 1, 3, 2, 4)).reshape(
        NE * NFH * 128, FH * D).astype(BF16NP)
    _PACK_CACHE[key] = (fupb, fdnb)
    return fupb, fdnb


def _prep_core_inputs(cfg, c, I, nonce, solo=False):
    T, D, EH, N, R, F, NE, TC = (cfg[k] for k in ("T", "D", "EH", "N", "R", "F", "NE", "TC"))
    SL = EH // 128
    H, Q = T // 2, T // 4
    E = 2 * EH
    NCH = H // TC + T // TC
    NFLAG = NCH + 8
    b = c // 4
    hi = (c % 4) >= 2
    eh = c % 2
    qq = eh
    if solo:
        b, hi, qq = 0, False, 0

    x = I["L0_attn_x"][b]
    xp_ = x[::-1] if hi else x
    xp_ = np.ascontiguousarray(xp_, np.float32)
    x_o = np.ascontiguousarray(xp_[qq * Q:(qq + 1) * Q])

    pfx = ("f_", "b_") if not hi else ("b_", "f_")
    g = lambda d, nm: I["L0_attn_" + pfx[d] + nm]

    Etrue = EH if solo else E
    ehw = 0 if solo else eh
    winT = np.zeros((D, 4 * EH), np.float32)
    for d in range(2):
        Win = g(d, "W_in")
        xp_rows = Win[ehw * EH:(ehw + 1) * EH]
        z_rows = Win[Etrue + ehw * EH: Etrue + (ehw + 1) * EH]
        winT[:, d * 2 * EH: d * 2 * EH + EH] = xp_rows.T
        winT[:, d * 2 * EH + EH: d * 2 * EH + 2 * EH] = z_rows.T

    convdm = np.zeros((2 * 4 * SL * 128, 128), np.float32)
    for d in range(2):
        Wc = g(d, "W_conv")[ehw * EH:(ehw + 1) * EH]
        for tap in range(4):
            for i in range(SL):
                r0 = ((d * 4 + tap) * SL + i) * 128
                convdm[r0:r0 + 128, :] = np.diag(Wc[i * 128:(i + 1) * 128, tap])
    convd = np.ascontiguousarray(
        convdm.reshape(2 * 4 * SL, 128, 128).transpose(1, 0, 2).reshape(128, -1))

    def col2(nm):
        out = np.zeros((128, 2 * SL), np.float32)
        for d in range(2):
            v = g(d, nm)[ehw * EH:(ehw + 1) * EH]
            out[:, d * SL:(d + 1) * SL] = v.reshape(SL, 128).T
        return out

    bconv = col2("b_conv")
    bdt = col2("b_dt")
    dvec = col2("D")
    alog = np.zeros((128, 2 * SL * N), np.float32)
    for d in range(2):
        A = g(d, "A_log")[ehw * EH:(ehw + 1) * EH]
        alog[:, d * SL * N:(d + 1) * SL * N] = \
            A.reshape(SL, 128, N).transpose(1, 0, 2).reshape(128, SL * N)

    wxT = np.zeros((2 * EH, 96), np.float32)
    for d in range(2):
        Wx = g(d, "W_x")
        half = slice(ehw * EH, (ehw + 1) * EH)
        wxT[d * EH:(d + 1) * EH, 0:R] = Wx[0:R, half].T
        wxT[d * EH:(d + 1) * EH, 64:64 + N] = Wx[R:R + N, half].T
        wxT[d * EH:(d + 1) * EH, 80:80 + N] = Wx[R + N:R + 2 * N, half].T
    if solo:
        wxT *= 0.5

    wdtT = np.zeros((64, 2 * EH), np.float32)
    for d in range(2):
        Wdt = g(d, "W_dt")[ehw * EH:(ehw + 1) * EH]
        wdtT[0:R, d * EH:(d + 1) * EH] = Wdt.T

    woutT = np.zeros((2 * E, D), np.float32)
    for d in range(2):
        if solo:
            woutT[d * E: d * E + EH] = g(d, "W_out").T[:EH]
        else:
            woutT[d * E:(d + 1) * E] = g(d, "W_out").T

    lnw_b = np.ascontiguousarray(np.broadcast_to(I["L0_attn_ln_w"], (128, D)), np.float32)
    lnb_b = np.ascontiguousarray(np.broadcast_to(I["L0_attn_ln_b"], (128, D)), np.float32)

    fup, fdn = _pack_ffn(cfg, I)
    rw = np.ascontiguousarray(I["L0_ffn_router_w"], np.float32)

    selm = np.zeros((16, N * 128), np.float32)
    for n in range(N):
        selm[n, n * 128:(n + 1) * 128] = 1.0
    ident = np.eye(128, dtype=np.float32)
    meta = np.zeros((1, 16), np.int32)
    meta[0, 0] = eh
    meta[0, 1] = 1 - eh
    meta[0, 2] = nonce
    meta[0, 3] = qq
    flagval = (nonce + 1 + np.arange(NFLAG)).astype(np.int32).reshape(1, NFLAG)

    p = np.arange(128, dtype=np.uint32)
    xdump = 2 * NCH * 96 + p
    xdidx = np.zeros((128, NCH * 6), np.uint32)
    offs = (0, 64, 80)
    nrs = (64, 16, 16)
    for s_ in range(NCH):
        for k in range(3):
            reh = eh if solo else (1 - eh)
            w = np.where(p < nrs[k], eh * NCH * 96 + s_ * 96 + offs[k] + p, xdump)
            r_ = np.where(p < nrs[k], reh * NCH * 96 + s_ * 96 + offs[k] + p, xdump)
            xdidx[:, s_ * 6 + k] = w
            xdidx[:, s_ * 6 + 3 + k] = r_
    EHl = EH
    yslidx = np.zeros((128, 2 * SL), np.uint32)
    for d in range(2):
        for i in range(SL):
            yslidx[:, d * SL + i] = eh * 2 * EHl + d * EHl + i * 128 + p
    NGO = Q // 128
    yrdidx = np.zeros((128, 4 * SL * NGO), np.uint32)
    for dd_ in range(2):
        for ehs in range(2):
            for i in range(SL):
                for tg in range(NGO):
                    col = ((dd_ * 2 + ehs) * SL + i) * NGO + tg
                    es = eh if solo else ehs
                    yrdidx[:, col] = ((es * 2 * EHl + dd_ * EHl + i * 128 + p) * (H // 128)
                                     + qq * (Q // 128) + tg)
    fdump = NFLAG * 2 + p
    flgidx = np.zeros((128, NFLAG), np.uint32)
    for s_ in range(NFLAG):
        flgidx[:, s_] = fdump
        flgidx[0, s_] = s_ * 2 + eh
        flgidx[1, s_] = (s_ * 2 + 1 - eh) if solo else (NFLAG * 2 + 64 + (s_ % 32))
    flgsrc = np.zeros((2, NFLAG * 16), np.int32)
    for s_ in range(NFLAG):
        flgsrc[0, s_ * 16:(s_ + 1) * 16] = nonce + 1 + s_
        if solo:
            flgsrc[1, s_ * 16:(s_ + 1) * 16] = nonce + 1 + s_

    return dict(x_b=xp_, x_o=x_o, winT=winT, convd=convd, bconv=bconv, bdt=bdt,
                alog=alog, dvec=dvec, wxT=wxT, wdtT=wdtT, lnw_b=lnw_b, lnb_b=lnb_b,
                woutT=woutT, fup=fup, fdn=fdn, rw=rw, selm=selm, ident=ident,
                meta=meta, flagval=flagval, xdidx=xdidx, yslidx=yslidx,
                yrdidx=yrdidx, flgidx=flgidx, flgsrc=flgsrc)


def assemble_output(cfg, results):
    T, D = cfg["T"], cfg["D"]
    H, Q = T // 2, T // 4
    out = np.zeros((2, T, D), np.float32)
    for c in range(8):
        b = c // 4
        hi = (c % 4) >= 2
        qq = c % 2
        y = results[c]["out_o"]
        if not hi:
            out[b, qq * Q:(qq + 1) * Q] = y
        else:
            out[b, T - (qq + 1) * Q: T - qq * Q] = y[::-1]
    return out


_BUILD_CACHE = {}


def kernel(**inputs):
    assert all(n in inputs for n in INPUT_NAMES), sorted(inputs)
    cfg = mk_cfg(full=True)
    key = "full"
    if key not in _BUILD_CACHE:
        _BUILD_CACHE[key] = build(cfg)
    nc = _BUILD_CACHE[key]
    nonce = int(np.random.randint(1, 2**30))
    in_maps = [_prep_core_inputs(cfg, c, inputs, nonce) for c in range(8)]
    res = run_bass_kernel_spmd(nc, in_maps, core_ids=list(range(8)))
    out = assemble_output(cfg, [res.results[c] for c in range(8)])
    return out.astype(np.float32)
